# version 28
# baseline (speedup 1.0000x reference)
import numpy as np
import concourse.bass as bass
import concourse.mybir as mybir
from concourse.bass import IndirectOffsetOnAxis
from concourse.bass_utils import run_bass_kernel_spmd
from contextlib import ExitStack

F32 = mybir.dt.float32
BF16 = mybir.dt.bfloat16
I32 = mybir.dt.int32
U8 = mybir.dt.uint8
AF = mybir.ActivationFunctionType
ALU = mybir.AluOpType
AX = mybir.AxisListType

S = 2048
DM = 1024
ND = 8
NSC = 4
SC = 512
NT = 16
FF = 4096
NF = 32

CCOL_VALS = (1e-5, 1.0, -1.0, 0.5, 2047.0, 0.0, -1e-4, 1e-4, 2e-4, -2e-4, 0.03125, 0.125)
CCOL = {v: i for i, v in enumerate(CCOL_VALS)}

ENGBODY = {"te": "tensor", "ve": "vector", "se": "scalar", "gp": "gpsimd", "sy": "sync"}
ENGSEM = {"te": "pe", "ve": "ve", "se": "se", "gp": "gp", "sy": "sy"}


class C:
    def __init__(self):
        self.nc = bass.Bass("TRN2", target_bir_lowering=False, debug=False)
        self.ctx = ExitStack()
        self.cnt = {}
        self.sems = {}
        self.pending = {}
        self.watermark = {}
        self.own_count = {}
        self.uid = 0

    def nid(self, p):
        self.uid += 1
        return f"{p}{self.uid}"

    def sem(self, name):
        if name not in self.sems:
            self.sems[name] = self.ctx.enter_context(self.nc.semaphore(name))
            self.cnt[name] = 0
        return self.sems[name]

    def inc(self, instr, name, n=1):
        instr.then_inc(self.sems[name], n)
        self.cnt[name] += n
        return (name, self.cnt[name])

    def sb(self, name, shape, dtype):
        return self.ctx.enter_context(self.nc.sbuf_tensor(self.nid(name), shape, dtype))

    def ps(self, name, shape=(128, 512), dtype=F32):
        return self.ctx.enter_context(self.nc.psum_tensor(self.nid(name), list(shape), dtype))

    def step(self, eng, fn, waits=(), sem=None, dma=False):
        semn = sem or ENGSEM[eng]
        self.sem(semn)
        n = 16 if dma else 1
        self.cnt[semn] += n
        tok = (semn, self.cnt[semn])
        q = self.pending.setdefault(eng, [])
        q.append((fn, tuple(waits), semn, n))
        if len(q) >= 600:
            self.flush(eng)
        return tok

    def flush(self, eng=None):
        if eng is None:
            for e in list(self.pending):
                self.flush(e)
            return
        q = self.pending.get(eng)
        if not q:
            return
        self.pending[eng] = []
        body = getattr(self.blk, ENGBODY[eng])
        wm = self.watermark.setdefault(eng, {})

        def b(e):
            for fn, waits, semn, n in q:
                for (s, v) in waits:
                    if v > 0 and v > wm.get(s, 0):
                        e.wait_ge(self.sems[s], v)
                        wm[s] = v
                last = fn(e)
                if last is None:
                    last = e.nop()
                last.then_inc(self.sems[semn], n)
            return None

        body(b)

    def setup(self):
        nc = self.nc
        self.xr = [self.sb("xr", [128, S], F32) for _ in range(ND)]
        self.xl = [self.sb("xl", [128, S], BF16) for _ in range(ND)]
        self.PB = self.ps("PB", (128, 4096))
        self.pb = [self.PB[:, i * 512:(i + 1) * 512] for i in range(8)]
        self.pbank_tok = [None] * 8
        for s in ("pe", "ve", "se", "gp", "sy"):
            self.sem(s)
        self.ones_bf = self.sb("onesbf", [128, 1], BF16)
        self.ones_colf = self.sb("onescf", [128, 1], F32)
        self.ones_row = self.sb("onesrow", [1, 128], F32)
        self.ident = self.sb("ident", [128, 128], F32)
        self.identi = self.sb("identi", [128, 128], I32)
        self.mean_row = self.sb("meanrow", [1, SC], F32)
        self.rstd_row = self.sb("rstdrow", [1, SC], F32)
        self.srow = [self.sb("srow", [1, SC], F32) for _ in range(2)]
        self.tscr = self.sb("tscr", [128, SC], F32)
        self.cc = self.sb("cc", [128, len(CCOL_VALS)], F32)
        self.blk = self.ctx.enter_context(nc.Block("main"))

        def init_ve(ve):
            ve.memset(self.ones_bf[:], 1.0)
            ve.memset(self.ones_colf[:], 1.0)
            for j, v in enumerate(CCOL_VALS):
                ve.memset(self.cc[:, j:j + 1], v)
            return ve.memset(self.ones_row[:], 1.0)

        tv = self.step("ve", init_ve)

        def init_gp(gp):
            return gp.iota(self.identi[:], [[-1, 128]], base=0, channel_multiplier=1)

        tg = self.step("gp", init_gp)
        self.tok_ident = self.step("ve", lambda ve: ve.tensor_scalar(
            self.ident[:], self.identi[:], 0, None, ALU.is_equal), waits=[tg])
        self.tok_consts = tv

    def bank(self, i):
        t = self.pbank_tok[i]
        return [t] if t is not None else []

    def bank_done(self, i, tok):
        self.pbank_tok[i] = tok

    def barrier(self):
        self.flush()
        vals = [(s, self.cnt[s]) for s in self.sems]
        for e in ("te", "ve", "se", "gp", "sy"):
            own = ENGSEM[e]

            def b(eng, own=own):
                for (s, v) in vals:
                    if s != own and v > 0:
                        eng.wait_ge(self.sems[s], v)
                return None

            self.step(e, b)
        self.flush()


class Ring:

    def __init__(self, c, name, nslot, queues=("sy", "se")):
        self.c = c
        self.name = name
        self.nslot = nslot
        self.queues = queues
        self.i = 0
        for i in range(nslot):
            c.sem(f"{name}_{i}")

    def load(self, fn, waits=()):
        s = f"{self.name}_{self.i % self.nslot}"
        q = self.queues[self.i % len(self.queues)]
        self.i += 1
        return self.c.step(q, fn, waits=waits, sem=s, dma=True)


def emit_ln(c, out_tiles=None, out_dtype=BF16, trange=(0, S)):
    if out_tiles is None:
        out_tiles = c.xl
    xb = c.xl
    t0, t1 = trange
    nsc = (t1 - t0) // SC

    cast_waits = getattr(c, "_x_toks", None)
    cast_tok = [None] * ND
    c._x_toks = None

    with ExitStack() as sctx:
        sq = [sctx.enter_context(c.nc.sbuf_tensor(c.nid("lnsq"), [128, SC], BF16)) for _ in range(2)]
        sq_cons = [None, None]
        prev_trep = None
        for sci in range(nsc):
            sc0 = t0 + sci * SC
            sb_ = sci % 2
            stat_tok = None
            for di in range(ND):
                cw = [cast_waits[di]] if (sci == 0 and cast_waits is not None) else []
                cast_tok[di] = c.step("se", lambda se, di=di, sc0=sc0: se.activation(
                    xb[di][:, sc0:sc0 + SC], c.xr[di][:, sc0:sc0 + SC], AF.Copy), waits=cw)
                w = []
                if sq_cons[di % 2] is not None:
                    w.append(sq_cons[di % 2])
                tsq = c.step("se", lambda se, di=di, sc0=sc0: se.activation(
                    sq[di % 2][:], xb[di][:, sc0:sc0 + SC], AF.Square), waits=w)

                def statfn(te, di=di, sc0=sc0, sb_=sb_):
                    te.matmul(c.pb[sb_][0:1, :], c.ones_bf[:], xb[di][:, sc0:sc0 + SC],
                              start=(di == 0), stop=(di == ND - 1))
                    return te.matmul(c.pb[sb_ + 2][0:1, :], c.ones_bf[:], sq[di % 2][:],
                                     start=(di == 0), stop=(di == ND - 1))

                bw = (c.bank(sb_) + c.bank(sb_ + 2)) if di == 0 else []
                stat_tok = c.step("te", statfn, waits=[tsq] + bw)
                sq_cons[di % 2] = stat_tok

            def rowsfn(ve, sb_=sb_):
                ve.tensor_scalar(c.mean_row[:], c.pb[sb_][0:1, :], 1.0 / DM, None, ALU.mult)
                ve.tensor_scalar(c.srow[0][:], c.pb[sb_ + 2][0:1, :], 1.0 / DM, None, ALU.mult)
                ve.tensor_mul(c.srow[1][:], c.mean_row[:], c.mean_row[:])
                return ve.tensor_sub(c.srow[0][:], c.srow[0][:], c.srow[1][:])

            trows = c.step("ve", rowsfn, waits=[stat_tok])
            c.bank_done(sb_, trows)
            c.bank_done(sb_ + 2, trows)
            tsqrt = c.step("se", lambda se: se.activation(
                c.srow[1][:], c.srow[0][:], AF.Sqrt, bias=c.cc[0:1, 0:1]), waits=[trows])
            trstd = c.step("ve", lambda ve: ve.reciprocal(c.rstd_row[:], c.srow[1][:]),
                           waits=[tsqrt])

            bm, br = 4 + (sci % 2) * 2, 5 + (sci % 2) * 2

            def repfn(te, bm=bm, br=br):
                te.matmul(c.pb[bm][:], c.ones_row[:], c.mean_row[:], start=True, stop=True)
                return te.matmul(c.pb[br][:], c.ones_row[:], c.rstd_row[:], start=True, stop=True)

            trep = c.step("te", repfn, waits=[trstd] + c.bank(bm) + c.bank(br))

            def normfn(ve, sc0=sc0, bm=bm, br=br):
                for di in range(ND):
                    ve.tensor_sub(c.tscr[:], c.xr[di][:, sc0:sc0 + SC], c.pb[bm][:])
                    last = ve.tensor_mul(out_tiles[di][:, sc0:sc0 + SC], c.tscr[:], c.pb[br][:])
                return last

            tnorm = c.step("ve", normfn, waits=[trep])
            c.bank_done(bm, tnorm)
            c.bank_done(br, tnorm)
    c._cast_tok = cast_tok
    c.barrier()


def emit_l0_ffn_quant(c, w):
    nc = c.nc
    ct = c._cast_tok
    wfv = [c.xr[0][:], c.xr[1][:], c.xr[5][:], c.xr[6][:]]
    wfct = [ct[0], ct[1], ct[5], ct[6]]
    x2b = c.xr[2][:].bitcast(BF16)
    wtv = [x2b[:, 0:2048], x2b[:, 2048:4096]]
    dump = c.xr[3][:]
    onesf = c.xr[4][:, 0:128]
    acc = c.xr[4][:, 128:168]
    inv = c.xr[4][:, 168:170]
    ring = Ring(c, "l0q", 4)
    st_ring = Ring(c, "l0qs", 2, queues=("gp",))
    tof = c.step("ve", lambda ve: ve.memset(onesf, 1.0), waits=[ct[4]])
    q_cons = [None, None]
    tq_slot = [None] * 4
    prev_red = None
    for wi, (wsrc, wdst, nrow) in enumerate(((w["w1tf"], w["w1t"], ND),
                                             (w["w2tf"], w["w2t"], NF))):
        ncol = wsrc.shape[1]
        nh = 2 if ncol > 2048 else 1
        cw = min(ncol, 2048)
        nu = nrow * nh
        vts = []
        for u in range(nu):
            r, hc = u // nh, u % nh
            src_ = wsrc[r * 128:(r + 1) * 128, hc * cw:(hc + 1) * cw]
            if u >= 4:
                bw = [vts[u - 4]]
            elif wi == 0:
                bw = [wfct[u]]
            else:
                bw = [t for t in (tq_slot[u % 4],) if t]
            lt = ring.load(lambda e, u=u, src_=src_, cw=cw: e.dma_start(
                out=wfv[u % 4][:, :cw], in_=src_), waits=bw)
            aw = [lt]
            if u == 0 and wi == 1 and prev_red is not None:
                aw.append(prev_red[0])
            vts.append(c.step("se", lambda se, u=u, cw=cw: se.activation(
                dump[:, :cw], wfv[u % 4][:, :cw], AF.Abs,
                accum_out=acc[:, u:u + 1]), waits=aw))
            yield "q" if u == nu - 1 else "a"
        trw = [prev_red[1]] if prev_red is not None else []
        tred = c.step("ve", lambda ve, nu=nu: ve.reduce_sum(
            acc[:, 39:40], acc[:, 0:nu], AX.X), waits=[vts[-1]] + trw)
        tsum = c.step("te", lambda te, wi=wi: te.matmul(
            c.pb[6][:, wi:wi + 1], onesf, acc[:, 39:40], start=True, stop=True),
            waits=[tred, tof] + c.bank(6))
        nel = float(wsrc.shape[0] * ncol)
        tscl0 = c.step("ve", lambda ve, nel=nel, wi=wi: ve.tensor_scalar(
            acc[:, 38:39], c.pb[6][:, wi:wi + 1],
            1.0 / nel, 1e-8, ALU.mult, ALU.max), waits=[tsum])
        c.bank_done(6, tscl0)
        yield "a"
        tinv = c.step("ve", lambda ve, wi=wi: ve.reciprocal(
            inv[:, wi:wi + 1], acc[:, 38:39]), waits=[tscl0])
        prev_red = (tred, tsum)
        yield "a"
        for u in range(nu):
            r, hc = u // nh, u % nh
            src_ = wsrc[r * 128:(r + 1) * 128, hc * cw:(hc + 1) * cw]
            bw = [tq_slot[u % 4]] if tq_slot[u % 4] else []
            if u < 4:
                bw.append(vts[-1])
            lt = ring.load(lambda e, u=u, src_=src_, cw=cw: e.dma_start(
                out=wfv[u % 4][:, :cw], in_=src_), waits=bw)

            def qfn(ve, u=u, wi=wi, cw=cw):
                v = wfv[u % 4][:, :cw]
                t = wtv[u % 2][:, :cw]
                ve.tensor_scalar(v, v, inv[:, wi:wi + 1], None, ALU.mult)
                ve.tensor_scalar(t, v, -0.5, None, ALU.is_le)
                return ve.scalar_tensor_tensor(t, v, 0.5, t, ALU.is_ge, ALU.subtract)

            qw = [q_cons[u % 2]] if q_cons[u % 2] else []
            if u < 2 and wi == 0:
                qw.append(ct[2])
            tq = c.step("ve", qfn, waits=[lt, tinv] + qw)
            tq_slot[u % 4] = tq
            tst = st_ring.load(lambda e, wdst=wdst, r=r, hc=hc, cw=cw, u=u: e.dma_start(
                out=wdst[r * 128:(r + 1) * 128, hc * cw:(hc + 1) * cw],
                in_=wtv[u % 2][:, :cw]), waits=[tq])
            q_cons[u % 2] = tst
            yield "a"
    c._q_prev = (q_cons[0], q_cons[1], tq)


def emit_l0_ffn(c, w, first):
    nc = c.nc
    emit_ln(c)

    with ExitStack() as sctx:
        hT = [sctx.enter_context(nc.sbuf_tensor(c.nid("hT"), [128, 1024], BF16)) for _ in range(NF)]
        wbuf = [sctx.enter_context(nc.sbuf_tensor(c.nid("wst"), [128, 2048], BF16)) for _ in range(2)]
        w2buf = [sctx.enter_context(nc.sbuf_tensor(c.nid("w2st"), [128, FF], BF16)) for _ in range(2)]
        ring = Ring(c, "l0f", 2)

        for half in range(2):
            scs = [2 * half, 2 * half + 1]
            wtoks, cons_of_load = [], [None] * 99
            j = 0
            etoks = []
            for fp in range(NF // 2):
                bw = [etoks[(fp - 2) * 4 + 3]] if fp >= 2 else []
                src = w["w1t"].rearrange("(dp p) f -> p dp f", p=128)[:, :, fp * 256:(fp + 1) * 256]
                lt = ring.load(lambda e, src=src, fp=fp: e.dma_start(
                    out=wbuf[fp % 2][:], in_=src), waits=bw)
                for fo in range(2):
                    fi = fp * 2 + fo
                    ba, bb = j % 6, (j + 1) % 6

                    def mmfn(te, fp=fp, fo=fo, scs=scs, ba=ba, bb=bb):
                        for di in range(ND):
                            wt = wbuf[fp % 2][:, di * 256 + fo * 128: di * 256 + (fo + 1) * 128]
                            te.matmul(c.pb[ba][:], wt,
                                      c.xl[di][:, scs[0] * SC:(scs[0] + 1) * SC],
                                      start=(di == 0), stop=(di == ND - 1), skip_group_check=True)
                            last = te.matmul(c.pb[bb][:], wt,
                                             c.xl[di][:, scs[1] * SC:(scs[1] + 1) * SC],
                                             start=(di == 0), stop=(di == ND - 1), skip_group_check=True)
                        return last

                    mt = c.step("te", mmfn, waits=[lt] + c.bank(ba) + c.bank(bb))
                    for k2, bx in ((0, ba), (1, bb)):
                        et = c.step("se", lambda se, fi=fi, k2=k2, bx=bx: se.activation(
                            hT[fi][:, k2 * SC:(k2 + 1) * SC], c.pb[bx][:],
                            AF.Gelu, bias=w["b1c"][:, fi:fi + 1]), waits=[mt])
                        c.bank_done(bx, et)
                        etoks.append(et)
                    j += 2

            j = 0
            etoks2 = []
            for di in range(ND):
                bw = [etoks2[(di - 2) * 2 + 1]] if di >= 2 else []
                src = w["w2t"].rearrange("(fp p) d -> p fp d", p=128)[:, :, di * 128:(di + 1) * 128]
                lt = ring.load(lambda e, src=src, di=di: e.dma_start(
                    out=w2buf[di % 2][:], in_=src), waits=bw + [etoks[-1]])
                ba, bb = j % 6, (j + 1) % 6

                def mmfn(te, di=di, ba=ba, bb=bb):
                    for fi in range(NF):
                        wt = w2buf[di % 2][:, fi * 128:(fi + 1) * 128]
                        te.matmul(c.pb[ba][:], wt, hT[fi][:, 0:SC],
                                  start=(fi == 0), stop=(fi == NF - 1), skip_group_check=True)
                        last = te.matmul(c.pb[bb][:], wt, hT[fi][:, SC:2 * SC],
                                         start=(fi == 0), stop=(fi == NF - 1), skip_group_check=True)
                    return last

                mt = c.step("te", mmfn, waits=[lt] + c.bank(ba) + c.bank(bb))
                for k2, bx in ((0, ba), (1, bb)):
                    sl = slice(scs[k2] * SC, (scs[k2] + 1) * SC)
                    et = c.step("ve", lambda ve, di=di, sl=sl, bx=bx: ve.scalar_tensor_tensor(
                        c.xr[di][:, sl], c.pb[bx][:], w["b2c"][:, di:di + 1],
                        c.xr[di][:, sl], ALU.add, ALU.add), waits=[mt])
                    c.bank_done(bx, et)
                    etoks2.append(et)
                j += 2
            c.barrier()


def emit_l1_ffn(c, w, xspill):
    nc = c.nc
    emit_ln(c)
    tspill = _emit_spill(c, xspill, store=True)

    with ExitStack() as sctx:
        u = [sctx.enter_context(nc.sbuf_tensor(c.nid("u"), [128, 1024], BF16)) for _ in range(NF)]
        wst = [sctx.enter_context(nc.sbuf_tensor(c.nid("wl1"), [128, FF], BF16)) for _ in range(2)]
        rbuf = [sctx.enter_context(nc.sbuf_tensor(c.nid("rb"), [128, SC], F32)) for _ in range(2)]
        obuf = [sctx.enter_context(nc.sbuf_tensor(c.nid("ob"), [128, SC], F32)) for _ in range(2)]
        ring = Ring(c, "l1f", 3, queues=("gp",))
        rring = Ring(c, "l1r", 2, queues=("sy",))
        oring = Ring(c, "l1o", 2, queues=("sy",))
        bt = []
        for fi in range(NF):
            bt.append(c.xr[fi // 4][:].bitcast(BF16)[:, (fi % 4) * 1024:(fi % 4 + 1) * 1024])

        for half in range(2):
            _l1_stage_in1024(c, ring, wst, w["w1t"], w["s1c"], u, half, mode="silu")
            _l1_stage_in1024(c, ring, wst, w["w2t"], w["s2c"], u, half, mode="mul")
            _l1_stage_in4096(c, ring, wst, w["h1t"], w["h1c"], u, bt, extra=[tspill])
            _l1_stage_in4096(c, ring, wst, w["h2t"], w["h2c"], bt, u)
            _l1_stage_spillout(c, ring, rring, oring, wst, rbuf, obuf,
                               w["w3t"], w["w3c"], u, half, xspill)

    _emit_spill(c, xspill, store=False)
    c.barrier()


def _emit_spill(c, xspill, store):
    qs = ("sy", "se", "gp")
    toks = []
    for di in range(ND):
        def fn(e, di=di, store=store):
            if store:
                return e.dma_start(out=xspill[di * 128:(di + 1) * 128, :], in_=c.xr[di][:])
            return e.dma_start(out=c.xr[di][:], in_=xspill[di * 128:(di + 1) * 128, :])

        toks.append(c.step(qs[di % 3], fn, sem=f"qsp{di % 3}", dma=True))
    return c.step("sy", lambda e: None, waits=toks)


def _l1_stage_in1024(c, ring, wst, wsrc, scol, dst, half, mode):
    etoks = []
    j = 0
    for fp in range(NF // 2):
        bw = [etoks[(fp - 2) * 4 + 3]] if fp >= 2 else []
        src = wsrc.rearrange("(dp p) f -> p dp f", p=128)[:, :, fp * 256:(fp + 1) * 256]
        lt = ring.load(lambda e, src=src, fp=fp: e.dma_start(
            out=wst[fp % 2][:, :2048], in_=src), waits=bw)
        for fo in range(2):
            fi = fp * 2 + fo
            ba, bb = j % 6, (j + 1) % 6

            def mmfn(te, fp=fp, fo=fo, half=half, ba=ba, bb=bb):
                for di in range(ND):
                    wt = wst[fp % 2][:, di * 256 + fo * 128: di * 256 + (fo + 1) * 128]
                    te.matmul(c.pb[ba][:], wt,
                              c.xl[di][:, (half * 2) * SC:(half * 2 + 1) * SC],
                              start=(di == 0), stop=(di == ND - 1), skip_group_check=True)
                    last = te.matmul(c.pb[bb][:], wt,
                                     c.xl[di][:, (half * 2 + 1) * SC:(half * 2 + 2) * SC],
                                     start=(di == 0), stop=(di == ND - 1), skip_group_check=True)
                return last

            mt = c.step("te", mmfn, waits=[lt] + c.bank(ba) + c.bank(bb))
            for k2, bx in ((0, ba), (1, bb)):
                if mode == "silu":
                    et = c.step("se", lambda se, fi=fi, k2=k2, bx=bx: se.activation(
                        dst[fi][:, k2 * SC:(k2 + 1) * SC], c.pb[bx][:],
                        AF.Silu, scale=scol[:, fi:fi + 1]), waits=[mt])
                else:
                    sl = slice(k2 * SC, (k2 + 1) * SC)
                    et = c.step("ve", lambda ve, fi=fi, sl=sl, bx=bx: ve.scalar_tensor_tensor(
                        dst[fi][:, sl], c.pb[bx][:], scol[:, fi:fi + 1],
                        dst[fi][:, sl], ALU.mult, ALU.mult), waits=[mt])
                c.bank_done(bx, et)
                etoks.append(et)
            j += 2
    c.barrier()


def _l1_stage_in4096(c, ring, wst, wsrc, scol, src_t, dst_t, extra=()):
    etoks = []
    for oi in range(NF):
        bw = [etoks[(oi - 2) * 2 + 1]] if oi >= 2 else []
        src = wsrc.rearrange("(fp p) o -> p fp o", p=128)[:, :, oi * 128:(oi + 1) * 128]
        lt = ring.load(lambda e, src=src, oi=oi: e.dma_start(
            out=wst[oi % 2][:], in_=src), waits=bw)
        ba, bb = (2 * oi) % 6, (2 * oi + 1) % 6

        def mmfn(te, oi=oi, ba=ba, bb=bb):
            for fi in range(NF):
                wt = wst[oi % 2][:, fi * 128:(fi + 1) * 128]
                te.matmul(c.pb[ba][:], wt, src_t[fi][:, 0:SC],
                          start=(fi == 0), stop=(fi == NF - 1), skip_group_check=True)
                last = te.matmul(c.pb[bb][:], wt, src_t[fi][:, SC:2 * SC],
                                 start=(fi == 0), stop=(fi == NF - 1), skip_group_check=True)
            return last

        mt = c.step("te", mmfn, waits=[lt] + c.bank(ba) + c.bank(bb))
        for k2, bx in ((0, ba), (1, bb)):
            et = c.step("se", lambda se, oi=oi, k2=k2, bx=bx: se.activation(
                dst_t[oi][:, k2 * SC:(k2 + 1) * SC], c.pb[bx][:],
                AF.Gelu, scale=scol[:, oi:oi + 1]), waits=[mt] + list(extra))
            c.bank_done(bx, et)
            etoks.append(et)
    c.barrier()


def _l1_stage_spillout(c, ring, rring, oring, wst, rbuf, obuf, wsrc, scol, src_t, half, xspill):
    stoks = []
    j = 0
    for di in range(ND):
        bw = [stoks[(di - 2) * 2 + 1]] if di >= 2 else []
        src = wsrc.rearrange("(fp p) d -> p fp d", p=128)[:, :, di * 128:(di + 1) * 128]
        lt = ring.load(lambda e, src=src, di=di: e.dma_start(
            out=wst[di % 2][:], in_=src), waits=bw)
        ba, bb = j % 6, (j + 1) % 6
        rts = []
        for k2 in range(2):
            sc = half * 2 + k2
            sl = slice(sc * SC, (sc + 1) * SC)
            rw = [stoks[j + k2 - 2]] if j + k2 >= 2 else []
            rts.append(rring.load(lambda e, di=di, sl=sl, jj=j + k2: e.dma_start(
                out=rbuf[jj % 2][:], in_=xspill[di * 128:(di + 1) * 128, sl]), waits=rw))

        def mmfn(te, di=di, ba=ba, bb=bb):
            for fi in range(NF):
                wt = wst[di % 2][:, fi * 128:(fi + 1) * 128]
                te.matmul(c.pb[ba][:], wt, src_t[fi][:, 0:SC],
                          start=(fi == 0), stop=(fi == NF - 1), skip_group_check=True)
                last = te.matmul(c.pb[bb][:], wt, src_t[fi][:, SC:2 * SC],
                                 start=(fi == 0), stop=(fi == NF - 1), skip_group_check=True)
            return last

        mt = c.step("te", mmfn, waits=[lt] + c.bank(ba) + c.bank(bb))
        for k2, bx in ((0, ba), (1, bb)):
            sc = half * 2 + k2
            sl = slice(sc * SC, (sc + 1) * SC)
            jj = j + k2
            ow = [stoks[jj - 2]] if jj >= 2 else []
            et = c.step("ve", lambda ve, di=di, jj=jj, bx=bx: ve.scalar_tensor_tensor(
                obuf[jj % 2][:], c.pb[bx][:], scol[:, di:di + 1],
                rbuf[jj % 2][:], ALU.mult, ALU.add), waits=[mt, rts[k2]] + ow)
            c.bank_done(bx, et)
            st = oring.load(lambda e, di=di, sl=sl, jj=jj: e.dma_start(
                out=xspill[di * 128:(di + 1) * 128, sl], in_=obuf[jj % 2][:]), waits=[et])
            stoks.append(st)
        j += 2
    c.barrier()


def emit_l0_attn_b(c, w, xspill):
    nc = c.nc
    emit_ln(c)
    tspill = _emit_spill(c, xspill, store=True)

    H = 16
    ASC = c.cc[:, CCOL[0.03125]:CCOL[0.03125] + 1]

    with ExitStack() as sctx:
        sb = lambda n, shp, dt: sctx.enter_context(nc.sbuf_tensor(c.nid(n), shp, dt))
        qraw = [sb("qraw", [128, S], BF16) for _ in range(ND)]
        lat = [c.xl[4], c.xl[5]]
        vext = [sb("vext", [128, H * 65], BF16) for _ in range(NT)]
        abf = sb("abf", [128, S], BF16)
        bbf = sb("bbf", [128, S], BF16)
        cdiag = sb("cdiag", [128, 128], BF16)
        qav = lambda oi: c.xl[oi // 4][:, (oi % 4) * SC:(oi % 4 + 1) * SC]
        qbv = lambda oi: c.xl[2 + oi // 4][:, (oi % 4) * SC:(oi % 4 + 1) * SC]
        at = [c.xl[5][:, i * 512:(i + 1) * 512] for i in range(4)] + \
             [sb("at4", [128, SC], BF16)]
        a2bc = c.xl[4][:, 1024:1536]
        dscb = c.xl[4][:, 1536:2048]
        tb2 = sb("tb2", [128, 16], F32)
        i16 = sb("i16", [128, 16], I32)
        onorms = [c.xl[7][:].bitcast(F32)[0:65, 512:1024],
                  c.xl[6][:].bitcast(F32)[0:65, 512:1024]]
        dsc = sb("dsc", [128, 128], F32)
        rbc = c.tscr[0:64, 0:SC]
        ao = sb("ao", [128, ND * SC], BF16)
        rbuf = [c.xl[6][:].bitcast(F32)[:, i * SC:(i + 1) * SC] for i in range(2)]
        obuf = [c.xl[7][:].bitcast(F32)[:, i * SC:(i + 1) * SC] for i in range(2)]
        ones65 = sb("on65b", [65, 64], F32)
        stage = [c.xl[4][0:64, 0:SC], c.xl[4][0:64, SC:2 * SC]]
        wkv = [sb("wkv", [128, 1024], BF16) for _ in range(ND)]
        wsm = ao[:, 0:2048]
        wsm2 = wsm
        ring = Ring(c, "ab", 3, queues=("gp",))
        iscr = [c.xl[0][:].bitcast(I32), c.xl[1][:].bitcast(I32)]
        fscr = [c.xl[2][:].bitcast(F32), c.xl[3][:].bitcast(F32)]

        t65 = None
        tvo = []
        for tj in range(NT):
            tvo.append(c.step("ve", lambda ve, tj=tj: ve.memset(
                vext[tj][:].rearrange("p (h e) -> p h e", e=65)[:, :, 64:65], 1.0)))

        kpv = [c.xr[oi // 2][:].bitcast(BF16)[:, (oi % 2) * S:(oi % 2 + 1) * S] for oi in range(ND)]
        kA = [c.xr[4 + oi // 2][:].bitcast(BF16)[:, (oi % 2) * S:(oi % 2 + 1) * S] for oi in range(ND)]
        kB = kpv

        wq_t = [ring.load(lambda e, di=di: e.dma_start(
            out=wkv[di][:], in_=w["Wq"][di * 128:(di + 1) * 128, :])) for di in range(ND)]

        j = 0
        for oi in range(ND):
            for sc in range(NSC):
                bidx = j % 6

                def mmfn(te, oi=oi, sc=sc, bidx=bidx):
                    for di in range(ND):
                        last = te.matmul(c.pb[bidx][:], wkv[di][:, oi * 128:(oi + 1) * 128],
                                         c.xl[di][:, sc * SC:(sc + 1) * SC],
                                         start=(di == 0), stop=(di == ND - 1))
                    return last

                mt = c.step("te", mmfn, waits=[wq_t[-1]] + c.bank(bidx))
                et = c.step("se", lambda se, oi=oi, sc=sc, bidx=bidx: se.activation(
                    qraw[oi][:, sc * SC:(sc + 1) * SC], c.pb[bidx][:], AF.Copy), waits=[mt])
                c.bank_done(bidx, et)
                j += 1
        c.barrier()

        for pi, (wn, first_) in enumerate((("Wk", True), ("Wv", False))):
            wt_ = [ring.load(lambda e, di=di, wn=wn: e.dma_start(
                out=wkv[di][:], in_=w[wn][di * 128:(di + 1) * 128, :])) for di in range(ND)]
            j = 0
            for oi in range(ND):
                for sc in range(NSC):
                    bidx = j % 6

                    def mmfn(te, oi=oi, sc=sc, bidx=bidx):
                        for di in range(ND):
                            last = te.matmul(c.pb[bidx][:], wkv[di][:, oi * 128:(oi + 1) * 128],
                                             c.xl[di][:, sc * SC:(sc + 1) * SC],
                                             start=(di == 0), stop=(di == ND - 1))
                        return last

                    mt = c.step("te", mmfn, waits=[wt_[-1], tspill] + c.bank(bidx))
                    if first_:
                        et = c.step("se", lambda se, oi=oi, sc=sc, bidx=bidx: se.activation(
                            kpv[oi][:, sc * SC:(sc + 1) * SC], c.pb[bidx][:], AF.Copy), waits=[mt])
                    else:
                        et = c.step("ve", lambda ve, oi=oi, sc=sc, bidx=bidx: ve.tensor_add(
                            kpv[oi][:, sc * SC:(sc + 1) * SC],
                            kpv[oi][:, sc * SC:(sc + 1) * SC],
                            c.pb[bidx][:]), waits=[mt])
                    c.bank_done(bidx, et)
                    j += 1
            c.barrier()

        tio = c.step("gp", lambda gp: (gp.iota(iscr[0][:], [[1, 1024]], base=0, channel_multiplier=0),
                                       gp.iota(iscr[1][:], [[1, 1024]], base=1024,
                                               channel_multiplier=0))[-1])
        tfs = c.step("ve", lambda ve: (ve.tensor_copy(fscr[0][:], iscr[0][:]),
                                       ve.tensor_copy(fscr[1][:], iscr[1][:]))[-1], waits=[tio])

        def tabfn(se):
            se.activation(abf[:, 0:1024], fscr[0][:], AF.Exp, scale=c.cc[:, CCOL[-1e-4]:CCOL[-1e-4] + 1])
            se.activation(abf[:, 1024:2048], fscr[1][:], AF.Exp, scale=c.cc[:, CCOL[-1e-4]:CCOL[-1e-4] + 1])
            se.activation(bbf[:, 0:1024], fscr[0][:], AF.Exp, scale=c.cc[:, CCOL[1e-4]:CCOL[1e-4] + 1])
            return se.activation(bbf[:, 1024:2048], fscr[1][:], AF.Exp, scale=c.cc[:, CCOL[1e-4]:CCOL[1e-4] + 1])

        tab = c.step("se", tabfn, waits=[tfs])
        tio2 = c.step("gp", lambda gp: gp.iota(iscr[0][:, 0:128], [[-1, 128]], base=0,
                                               channel_multiplier=1), waits=[tab])
        tcp2 = c.step("ve", lambda ve: ve.tensor_copy(fscr[0][:, 0:128], iscr[0][:, 0:128]),
                      waits=[tio2])
        tmn = c.step("ve", lambda ve: ve.tensor_scalar(dsc[:, 0:128], fscr[0][:, 0:128], 0.0, None, ALU.min),
                     waits=[tcp2])
        tcd = c.step("se", lambda se: se.activation(cdiag[:], dsc[:, 0:128], AF.Exp,
                                                    scale=c.cc[:, CCOL[2e-4]:CCOL[2e-4] + 1]),
                     waits=[tmn])

        tdn = ring.load(lambda e: e.dma_start(
            out=wsm[:].rearrange("p (d o) -> p d o", o=256),
            in_=w["Wdn"].rearrange("(d p) o -> p d o", p=128)))
        j = 0
        for oi2 in range(2):
            for sc in range(NSC):
                bidx = j % 6

                def mmfn(te, oi2=oi2, sc=sc, bidx=bidx):
                    for di in range(ND):
                        last = te.matmul(c.pb[bidx][:],
                                         wsm[:, di * 256 + oi2 * 128: di * 256 + (oi2 + 1) * 128],
                                         kpv[di][:, sc * SC:(sc + 1) * SC],
                                         start=(di == 0), stop=(di == ND - 1))
                    return last

                mt = c.step("te", mmfn, waits=[tdn] + c.bank(bidx))
                et = c.step("se", lambda se, oi2=oi2, sc=sc, bidx=bidx: se.activation(
                    lat[oi2][:, sc * SC:(sc + 1) * SC], c.pb[bidx][:], AF.Copy), waits=[mt])
                c.bank_done(bidx, et)
                j += 1
        c.barrier()

        tuk = ring.load(lambda e: e.dma_start(
            out=wsm2[:].rearrange("p (l o) -> p l o", o=1024),
            in_=w["Wuk"].rearrange("(l p) o -> p l o", p=128)))
        j = 0
        for oi in range(ND):
            for sc in range(NSC):
                bidx = j % 6
                sl = slice(sc * SC, (sc + 1) * SC)

                def mmfn(te, oi=oi, sc=sc, bidx=bidx):
                    for li in range(2):
                        last = te.matmul(c.pb[bidx][:],
                                         wsm2[:, li * 1024 + oi * 128: li * 1024 + (oi + 1) * 128],
                                         lat[li][:, sc * SC:(sc + 1) * SC],
                                         start=(li == 0), stop=(li == 1))
                    return last

                mt = c.step("te", mmfn, waits=[tuk, tab] + c.bank(bidx))
                ea = c.step("ve", lambda ve, oi=oi, sl=sl, bidx=bidx: ve.tensor_mul(
                    kA[oi][:, sl], c.pb[bidx][:], abf[:, sl]), waits=[mt])
                eb = c.step("ve", lambda ve, oi=oi, sl=sl, bidx=bidx: ve.tensor_mul(
                    kB[oi][:, sl], c.pb[bidx][:], bbf[:, sl]), waits=[ea])
                c.bank_done(bidx, eb)
                j += 1
        c.barrier()

        tuv = ring.load(lambda e: e.dma_start(
            out=wsm[:].rearrange("p (l o) -> p l o", o=1024),
            in_=w["Wuv"].rearrange("(l p) o -> p l o", p=128)))
        j = 0
        for tj in range(NT):
            for vo in range(2):
                bidx = j % 6

                def mmfn(te, tj=tj, vo=vo, bidx=bidx):
                    for li in range(2):
                        last = te.matmul(c.pb[bidx][:],
                                         lat[li][:, tj * 128:(tj + 1) * 128],
                                         wsm[:, li * 1024 + vo * 512: li * 1024 + (vo + 1) * 512],
                                         start=(li == 0), stop=(li == 1))
                    return last

                mt = c.step("te", mmfn, waits=[tuv, tvo[tj]] + c.bank(bidx))
                dst = vext[tj][:].rearrange("p (h e) -> p h e", e=65)[:, vo * 8:(vo + 1) * 8, 0:64]
                et = c.step("ve", lambda ve, dst=dst, bidx=bidx: ve.tensor_copy(
                    dst, c.pb[bidx][:].rearrange("p (h e) -> p h e", e=64)), waits=[mt])
                c.bank_done(bidx, et)
                j += 1
        c.barrier()

        wo_t = [ring.load(lambda e, di=di: e.dma_start(
            out=wkv[di][:], in_=w["Wo"][di * 128:(di + 1) * 128, :])) for di in range(ND)]
        t65 = c.step("ve", lambda ve: ve.memset(ones65[:], 1.0))
        tb2i = c.step("gp", lambda gp: gp.iota(i16[:], [[128, 16]], base=0, channel_multiplier=1))
        tb2f = c.step("ve", lambda ve: ve.tensor_copy(tb2[:], i16[:]), waits=[tb2i])
        tb2t = c.step("se", lambda se: se.activation(tb2[:], tb2[:], AF.Exp,
                                                     scale=c.cc[:, CCOL[2e-4]:CCOL[2e-4] + 1]),
                      waits=[tb2f])
        c.barrier()

        rW = [None, None]
        oW = [None, None]
        for qc in range(NSC):
            qsl = slice(qc * SC, (qc + 1) * SC)
            tq = None
            for oi in range(ND):
                c.step("ve", lambda ve, oi=oi, qsl=qsl: ve.tensor_mul(
                    qav(oi), qraw[oi][:, qsl], abf[:, qsl]))
                tq = c.step("ve", lambda ve, oi=oi, qsl=qsl: ve.tensor_mul(
                    qbv(oi), qraw[oi][:, qsl], bbf[:, qsl]))
            i512 = c.xl[6][:].bitcast(I32)[:, 0:512]
            f512 = c.xl[7][:].bitcast(F32)[:, 0:512]
            a2w = [t for t in (rW[0], oW[0]) if t]
            ta2i = c.step("gp", lambda gp, qc=qc: gp.iota(i512, [[1, 512]],
                                                          base=qc * SC, channel_multiplier=0),
                          waits=a2w)
            ta2f = c.step("ve", lambda ve: ve.tensor_copy(f512, i512), waits=[ta2i])
            ta2 = c.step("se", lambda se: se.activation(a2bc[:], f512, AF.Exp,
                                                        scale=c.cc[:, CCOL[-2e-4]:CCOL[-2e-4] + 1]),
                         waits=[ta2f])

            av_war = [None] * 5
            str_war = None
            last_tn = None
            LAG = 4
            norm_q = []

            def emit_norm(h, av_last):
                hi, hb = h // 2, (h % 2) * 64
                po = 5 + (h % 2)
                onorm = onorms[h % 2]
                onw = []
                if hasattr(c, "_ab_on") and c._ab_on.get(h % 2):
                    onw.append(c._ab_on[h % 2])
                if h < 2:
                    onw += [t for t in (rW[1], oW[1]) if t]
                tev = c.step("ve", lambda ve, po=po, onorm=onorm: ve.tensor_copy(
                    onorm, c.pb[po][0:65, :]), waits=[av_last] + onw)
                c.bank_done(po, tev)
                trep = c.step("te", lambda te, onorm=onorm: te.matmul(
                    c.pb[7][0:64, :], ones65[64:65, :], onorm[64:65, :],
                    start=True, stop=True), waits=[tev, t65] + c.bank(7))
                trc = c.step("ve", lambda ve: ve.reciprocal(rbc, c.pb[7][0:64, :]), waits=[trep])
                c.bank_done(7, trc)
                if hb == 0:
                    tn = c.step("ve", lambda ve, hi=hi, onorm=onorm: ve.tensor_mul(
                        ao[0:64, hi * SC:(hi + 1) * SC], onorm[0:64, :], rbc), waits=[trc])
                else:
                    sgi = (h // 2) % 2
                    sgw = [c._ab_sg[sgi]] if hasattr(c, "_ab_sg") and c._ab_sg.get(sgi) else []
                    tn0 = c.step("ve", lambda ve, sgi=sgi, onorm=onorm: ve.tensor_mul(
                        stage[sgi][:], onorm[0:64, :], rbc), waits=[trc] + sgw)
                    tn = c.step("sy", lambda e, hi=hi, sgi=sgi: e.dma_start(
                        out=ao[64:128, hi * SC:(hi + 1) * SC], in_=stage[sgi][:]),
                        sem="abs%d" % sgi, dma=True, waits=[tn0])
                    if not hasattr(c, "_ab_sg"):
                        c._ab_sg = {}
                    c._ab_sg[sgi] = tn
                if not hasattr(c, "_ab_on"):
                    c._ab_on = {}
                c._ab_on[h % 2] = tn
                return tn

            for h in range(H):
                hi, hb = h // 2, (h % 2) * 64
                po = 5 + (h % 2)
                av_last = None
                ets = [None] * NT
                for i in range(NT + LAG):
                    if i < NT:
                        kt = i
                        lo = kt * 128
                        slot = kt % 5
                        pw = c.pb[slot]
                        ab = at[kt % 5]
                        if lo >= qc * SC + SC:
                            kind = "below"
                        elif lo + 128 <= qc * SC:
                            kind = "above"
                        else:
                            kind = "straddle"
                        kv, qv = (kB, qav) if kind == "above" else (kA, qbv)
                        mt = c.step("te", lambda te, kv=kv, qv=qv, hi=hi, hb=hb, kt=kt, pw=pw:
                                    te.matmul(pw, kv[hi][hb:hb + 64, kt * 128:(kt + 1) * 128],
                                              qv(hi)[hb:hb + 64, :], start=True, stop=True),
                                    waits=[tq, ta2] + c.bank(slot))
                        eww = [av_war[kt % 5]] if av_war[kt % 5] else []
                        if kind != "straddle":
                            et = c.step("se", lambda se, ab=ab, pw=pw: se.activation(
                                ab[:], pw, AF.Exp, scale=ASC), waits=[mt] + eww)
                            c.bank_done(slot, et)
                        else:
                            dcol = lo - qc * SC
                            sw = [str_war] if str_war else []

                            def fixfn(ve, pw=pw, dcol=dcol, kt=kt):
                                last = ve.tensor_mul(dscb[:, dcol:dcol + 128],
                                                     pw[:, dcol:dcol + 128], cdiag[:])
                                if dcol + 128 < SC:
                                    last = ve.scalar_tensor_tensor(
                                        dscb[:, dcol + 128:SC], pw[:, dcol + 128:SC],
                                        tb2[:, kt:kt + 1], a2bc[:, dcol + 128:SC],
                                        ALU.mult, ALU.mult)
                                return last

                            tfix = c.step("ve", fixfn, waits=[mt, tcd, tb2t, ta2] + sw)

                            def epi(se, ab=ab, pw=pw, dcol=dcol):
                                if dcol > 0:
                                    se.activation(ab[:, 0:dcol], pw[:, 0:dcol], AF.Exp, scale=ASC)
                                return se.activation(ab[:, dcol:SC], dscb[:, dcol:SC],
                                                     AF.Exp, scale=ASC)

                            et = c.step("se", epi, waits=[tfix] + eww)
                            str_war = et
                            c.bank_done(slot, et)
                        ets[kt] = et
                    if i == 1 and h > 0:
                        norm_q.append(emit_norm(h - 1, prev_av))
                    if i >= LAG:
                        kt = i - LAG
                        ab = at[kt % 5]
                        avw = [ets[kt]] + (c.bank(po) if kt == 0 else [])
                        av_last = c.step("te", lambda te, kt=kt, h=h, ab=ab, po=po: te.matmul(
                            c.pb[po][0:65, :], vext[kt][:, h * 65:(h + 1) * 65], ab[:],
                            start=(kt == 0), stop=(kt == NT - 1)), waits=avw)
                        av_war[kt % 5] = av_last
                prev_av = av_last
            last_tn = emit_norm(H - 1, prev_av)

            aow_toks = [last_tn] + [t for t in (c._ab_sg.get(0), c._ab_sg.get(1)) if t]
            for di in range(ND):
                bidx = di % 4
                rw = [rW[di % 2]] if rW[di % 2] else []
                rt = c.step("sy", lambda e, di=di, qsl=qsl: e.dma_start(
                    out=rbuf[di % 2][:], in_=xspill[di * 128:(di + 1) * 128, qsl]),
                    sem="abr%d" % (di % 2), dma=True, waits=rw)

                def wofn(te, di=di, bidx=bidx):
                    for ii in range(ND):
                        last = te.matmul(c.pb[bidx][:], wkv[ii][:, di * 128:(di + 1) * 128],
                                         ao[:, ii * SC:(ii + 1) * SC],
                                         start=(ii == 0), stop=(ii == ND - 1))
                    return last

                mt = c.step("te", wofn, waits=aow_toks + [wo_t[-1]] + c.bank(bidx))
                ow = [oW[di % 2]] if oW[di % 2] else []
                ot = c.step("ve", lambda ve, di=di, bidx=bidx: ve.tensor_add(
                    obuf[di % 2][:], c.pb[bidx][:], rbuf[di % 2][:]), waits=[mt, rt] + ow)
                c.bank_done(bidx, ot)
                rW[di % 2] = ot
                st = c.step("sy", lambda e, di=di, qsl=qsl: e.dma_start(
                    out=xspill[di * 128:(di + 1) * 128, qsl], in_=obuf[di % 2][:]),
                    sem="abo%d" % (di % 2), dma=True, waits=[ot])
                oW[di % 2] = st
                aow_toks = aow_toks[:1]
            c.barrier()
            c._ab_on = {}
            c._ab_sg = {}


    _emit_spill(c, xspill, store=False)
    c.barrier()


def emit_l0_attn_a(c, w, xspill, xin, after_ln=None):
    nc = c.nc
    emit_ln(c)
    cast_tok = c._cast_tok
    H = 16
    with ExitStack() as sctx:
        sb = lambda n, shp, dt: sctx.enter_context(nc.sbuf_tensor(c.nid(n), shp, dt))
        wqsb = [sb("wq", [128, 1024], BF16) for _ in range(ND)]
        wkvsb = [sb("wkv", [128, 2048], BF16) for _ in range(ND)]
        wosb = [sb("wo", [128, 1024], BF16) for _ in range(2)]
        qsc = sb("qsc", [128, ND * SC], BF16)
        ksc = sb("ksc", [128, ND * SC], BF16)
        vsc = [sb("vsc", [128, H * 65], BF16) for _ in range(4)]
        kprev = sb("kpv", [128, 1024], BF16)
        vprev = sb("vpv", [128, H * 65], BF16)
        kz = sb("kz", [128, 1024], BF16)
        vz = sb("vz", [128, H * 65], BF16)
        atb = [sb("at", [128, 512], BF16) for _ in range(3)]
        cmask = sb("cm", [128, 128], BF16)
        cmi = sb("cmi", [128, 128], I32)
        aowsc = sb("aow", [128, ND * SC], BF16)
        stg = [sb("stg", [64, 128], BF16) for _ in range(2)]
        aonm = [sb("aonm", [65, 128], F32) for _ in range(2)]
        on65 = sb("on65", [65, 64], F32)
        ring = Ring(c, "aa", 3, queues=("gp",))

        for di in range(ND):
            ring.load(lambda e, di=di: e.dma_start(out=wqsb[di][:], in_=w["wq"][di * 128:(di + 1) * 128, :]))
            ring.load(lambda e, di=di: e.dma_start(out=wkvsb[di][:], in_=w["wkv"][di * 128:(di + 1) * 128, :]))
        c._qgen = after_ln() if after_ln is not None else None

        c._qtag = "a"

        def qdrive(n=1, quiet=False):
            g = c._qgen
            if g is None:
                return
            try:
                for _ in range(n):
                    if c._qtag == "q" and not quiet:
                        return
                    c._qtag = next(g)
            except StopIteration:
                c._qgen = None

        x7 = c.xr[7][:]
        arb = [x7[:, 0:512], x7[:, 512:1024]]
        aob = [x7[:, 1024:1536], x7[:, 1536:2048]]
        rW = {0: cast_tok[7], 1: cast_tok[7]}
        oW = {0: None, 1: None}

        def initfn(ve):
            ve.memset(kz[:], 0.0)
            ve.memset(vz[:], 0.0)
            ve.memset(vz[:].rearrange("p (h e) -> p h e", e=65)[:, :, 64:65], 1.0)
            ve.memset(vprev[:].rearrange("p (h e) -> p h e", e=65)[:, :, 64:65], 1.0)
            for i in range(4):
                ve.memset(vsc[i][:].rearrange("p (h e) -> p h e", e=65)[:, :, 64:65], 1.0)
            return ve.memset(on65[:], 1.0)

        c.step("ve", initfn)
        tcmi = c.step("gp", lambda gp: gp.iota(cmi[:], [[1, 128]], base=0, channel_multiplier=-1))
        c.step("ve", lambda ve: ve.tensor_scalar(cmask[:], cmi[:], 0.0, None, ALU.is_ge),
               waits=[tcmi])
        c.barrier()
        qdrive(4, quiet=True)

        for scn in range(NSC):
            csl = slice(scn * SC, (scn + 1) * SC)
            for kind in range(2):
                dst = qsc if kind == 0 else ksc
                for oi in range(ND):
                    bidx = oi % 6

                    def mmfn(te, oi=oi, csl=csl, kind=kind, bidx=bidx):
                        for di in range(ND):
                            src_ = wqsb[di][:, oi * 128:(oi + 1) * 128] if kind == 0 \
                                else wkvsb[di][:, oi * 128:(oi + 1) * 128]
                            last = te.matmul(c.pb[bidx][:], src_, c.xl[di][:, csl],
                                             start=(di == 0), stop=(di == ND - 1))
                        return last

                    mt = c.step("te", mmfn, waits=c.bank(bidx))
                    et = c.step("se", lambda se, oi=oi, dst=dst, bidx=bidx: se.activation(
                        dst[:, oi * SC:(oi + 1) * SC], c.pb[bidx][:], AF.Copy), waits=[mt])
                    c.bank_done(bidx, et)
                    qdrive()
            for wj in range(4):
                for vo in range(2):
                    bidx = 6 + vo

                    def mmfn(te, wj=wj, vo=vo, scn=scn, bidx=bidx):
                        wsl = slice(scn * SC + wj * 128, scn * SC + (wj + 1) * 128)
                        for di in range(ND):
                            last = te.matmul(c.pb[bidx][:], c.xl[di][:, wsl],
                                             wkvsb[di][:, 1024 + vo * 512:1024 + (vo + 1) * 512],
                                             start=(di == 0), stop=(di == ND - 1))
                        return last

                    mt = c.step("te", mmfn, waits=c.bank(bidx))
                    dst = vsc[wj][:].rearrange("p (h e) -> p h e", e=65)[:, vo * 8:(vo + 1) * 8, 0:64]
                    et = c.step("ve", lambda ve, dst=dst, bidx=bidx: ve.tensor_copy(
                        dst, c.pb[bidx][:].rearrange("p (h e) -> p h e", e=64)), waits=[mt])
                    c.bank_done(bidx, et)
                    qdrive()
            c.barrier()

            sgW = {}
            for wj in range(4):
                qdrive(2)
                w_ = scn * 4 + wj
                kp = kz if w_ == 0 else (kprev if wj == 0 else None)
                vp = vz if w_ == 0 else (vprev if wj == 0 else None)
                LAG2 = 2
                ets = [None] * (H // 2)
                avpairs = [None] * (H // 2)
                for i in range(H // 2 + LAG2 + 1):
                    if i < H // 2:
                        hp = i
                        h0 = hp * 2
                        b0 = (hp * 2) % 6
                        b1 = (hp * 2 + 1) % 6
                        slot = hp % 3

                        def smm(te, wj=wj, h0=h0, b0=b0, b1=b1, kp=kp):
                            last = None
                            for j2 in range(2):
                                h = h0 + j2
                                hi, hb = h // 2, (h % 2) * 64
                                bx = b0 if j2 == 0 else b1
                                kpap = kp[hb:hb + 64, hi * 128:(hi + 1) * 128] if kp is not None \
                                    else ksc[hb:hb + 64, hi * SC + (wj - 1) * 128:hi * SC + wj * 128]
                                te.matmul(c.pb[bx][:, 0:128], kpap,
                                          qsc[hb:hb + 64, hi * SC + wj * 128:hi * SC + (wj + 1) * 128],
                                          start=True, stop=True)
                                last = te.matmul(c.pb[bx][:, 128:256],
                                                 ksc[hb:hb + 64, hi * SC + wj * 128:hi * SC + (wj + 1) * 128],
                                                 qsc[hb:hb + 64, hi * SC + wj * 128:hi * SC + (wj + 1) * 128],
                                                 start=True, stop=True)
                            return last

                        mt = c.step("te", smm, waits=c.bank(b0) + c.bank(b1))
                        eww = sgW.get(("at", slot))

                        def expfn(se, slot=slot, b0=b0, b1=b1):
                            se.activation(atb[slot][:, 0:256], c.pb[b0][:, 0:256], AF.Exp,
                                          scale=c.cc[:, CCOL[0.125]:CCOL[0.125] + 1])
                            return se.activation(atb[slot][:, 256:512], c.pb[b1][:, 0:256], AF.Exp,
                                                 scale=c.cc[:, CCOL[0.125]:CCOL[0.125] + 1])

                        et = c.step("se", expfn, waits=[mt] + ([eww] if eww else []))
                        c.bank_done(b0, et)
                        c.bank_done(b1, et)

                        def maskfn(ve, slot=slot):
                            ve.tensor_mul(atb[slot][:, 128:256], atb[slot][:, 128:256], cmask[:])
                            return ve.tensor_mul(atb[slot][:, 384:512], atb[slot][:, 384:512], cmask[:])

                        ets[hp] = c.step("ve", maskfn, waits=[et])
                    if LAG2 + 1 <= i:
                        hp = i - LAG2 - 1
                        if hp < H // 2:
                            avs = avpairs[hp]
                            h0 = hp * 2
                            trc_prev = sgW.get(("trc",))
                            for j2 in range(2):
                                h = h0 + j2
                                hi, hb = h // 2, (h % 2) * 64
                                po = 6 + j2
                                tev = c.step("ve", lambda ve, po=po, j2=j2: ve.tensor_copy(
                                    aonm[j2][:], c.pb[po][0:65, 0:128]),
                                    waits=[avs[j2]] + ([sgW.get(("nm", j2))] if sgW.get(("nm", j2)) else []))
                                c.bank_done(po, tev)
                                trep = c.step("te", lambda te, j2=j2: te.matmul(
                                    c.pb[5][0:64, j2 * 128:(j2 + 1) * 128], on65[64:65, :],
                                    aonm[j2][64:65, :], start=True, stop=True),
                                    waits=[tev] + (c.bank(5) if trc_prev is None else [trc_prev]))
                                trc = c.step("ve", lambda ve, j2=j2: ve.reciprocal(
                                    c.tscr[0:64, j2 * 128:(j2 + 1) * 128],
                                    c.pb[5][0:64, j2 * 128:(j2 + 1) * 128]), waits=[trep])
                                trc_prev = trc
                                sgW[("trc",)] = trc
                                c.bank_done(5, trc)
                                if hb == 0:
                                    tn = c.step("ve", lambda ve, hi=hi, wj=wj, j2=j2: ve.tensor_mul(
                                        aowsc[0:64, hi * SC + wj * 128:hi * SC + (wj + 1) * 128],
                                        aonm[j2][0:64, :], c.tscr[0:64, j2 * 128:(j2 + 1) * 128]),
                                        waits=[trc])
                                else:
                                    sgi = j2
                                    sgw = sgW.get(("sg", sgi))
                                    tn0 = c.step("ve", lambda ve, sgi=sgi, j2=j2: ve.tensor_mul(
                                        stg[sgi][:], aonm[j2][0:64, :],
                                        c.tscr[0:64, j2 * 128:(j2 + 1) * 128]),
                                        waits=[trc] + ([sgw] if sgw else []))
                                    tn = c.step("sy", lambda e, hi=hi, wj=wj, sgi=sgi: e.dma_start(
                                        out=aowsc[64:128, hi * SC + wj * 128:hi * SC + (wj + 1) * 128],
                                        in_=stg[sgi][:]), sem="aas%d" % sgi, dma=True, waits=[tn0])
                                    sgW[("sg", sgi)] = tn
                                sgW[("nm", j2)] = tn
                                sgW[("tn",)] = tn

                    if LAG2 <= i < H // 2 + LAG2:
                        hp = i - LAG2
                        h0 = hp * 2
                        slot = hp % 3
                        avs = []
                        for j2 in range(2):
                            h = h0 + j2
                            po = 6 + j2

                            def avfn(te, wj=wj, h=h, j2=j2, slot=slot, po=po, vp=vp):
                                vpap = vp[:, h * 65:(h + 1) * 65] if vp is not None \
                                    else vsc[wj - 1][:, h * 65:(h + 1) * 65]
                                te.matmul(c.pb[po][0:65, 0:128], vpap,
                                          atb[slot][:, j2 * 256:j2 * 256 + 128],
                                          start=True, stop=False)
                                return te.matmul(c.pb[po][0:65, 0:128], vsc[wj][:, h * 65:(h + 1) * 65],
                                                 atb[slot][:, j2 * 256 + 128:j2 * 256 + 256],
                                                 start=False, stop=True)

                            avs.append(c.step("te", avfn, waits=[ets[hp]] + c.bank(po)))
                        sgW[("at", slot)] = avs[-1]
                        avpairs[hp] = avs
            if scn < NSC - 1:
                tk = c.step("sy", lambda e, scn=scn: e.dma_start(
                    out=kprev[:], in_=ksc[:].rearrange("p (o w) -> p o w", w=SC)[:, :, 384:512]),
                    sem="aakp", dma=True, waits=[sgW[("tn",)]])
                tv2 = c.step("sy", lambda e: e.dma_start(out=vprev[:], in_=vsc[3][:]),
                             sem="aakp", dma=True)
            aow_toks = [t for kk, t in sgW.items() if kk[0] in ("tn", "sg") and t]
            mt = None
            for ii in range(ND):
                wt_ = ring.load(lambda e, ii=ii: e.dma_start(
                    out=wosb[ii % 2][:], in_=w["wo"][ii * 128:(ii + 1) * 128, :]),
                    waits=[mt] if ii >= 2 else [])

                def wofn(te, ii=ii):
                    for di in range(ND):
                        last = te.matmul(c.pb[di][:], wosb[ii % 2][:, di * 128:(di + 1) * 128],
                                         aowsc[:, ii * SC:(ii + 1) * SC],
                                         start=(ii == 0), stop=(ii == ND - 1))
                    return last

                bw = sum((c.bank(di) for di in range(ND)), []) if ii == 0 else []
                mt = c.step("te", wofn, waits=[wt_] + aow_toks + bw)
                aow_toks = []
                qdrive()
            for di in range(ND):
                j = di % 2
                rt = c.step("sy", lambda e, di=di, csl=csl, j=j: e.dma_start(
                    out=arb[j], in_=xin[di * 128:(di + 1) * 128, csl]),
                    sem="aar%d" % j, dma=True, waits=[t for t in (rW[j],) if t])
                ot = c.step("ve", lambda ve, di=di, j=j: ve.scalar_tensor_tensor(
                    aob[j], c.pb[di][:], w["boc"][:, di:di + 1], arb[j],
                    ALU.add, ALU.add), waits=[mt, rt] + ([oW[j]] if oW[j] else []))
                c.bank_done(di, ot)
                rW[j] = ot
                st = c.step("sy", lambda e, di=di, csl=csl, j=j: e.dma_start(
                    out=xspill[di * 128:(di + 1) * 128, csl], in_=aob[j]),
                    sem="aao%d" % j, dma=True, waits=[ot])
                oW[j] = st
            c.barrier()
            qdrive(4, quiet=True)
        while c._qgen is not None:
            qdrive(50, quiet=True)
    c.barrier()
    _emit_spill(c, xspill, store=False)
    c.barrier()


def emit_l1_attn_a(c, w, xtok):
    nc = c.nc
    emit_ln(c)
    H, POS, SS = 8, 16, 33
    with ExitStack() as sctx:
        sb = lambda n, shp, dt: sctx.enter_context(nc.sbuf_tensor(c.nid(n), shp, dt))
        wwv = sb("wwv", [128, ND * 24], BF16)
        wqr = sb("wqr", [128, ND * 128], BF16)
        wse1 = sb("ws1", [128, ND * 256], BF16)
        wse2 = sb("ws2", [128, 2 * 1024], BF16)
        wout = [sb("wo", [128, 1024], BF16) for _ in range(ND)]
        kwcol = sb("kwc", [128, 8], BF16)
        wp = sb("wp", [24, S], F32)
        tB = sb("tb", [24, S], F32)
        qT = sb("qt", [128, S], BF16)
        meta = sb("meta", [32, S], F32)
        mtok = [sb("mtok", [128, 32], F32) for _ in range(2)]
        qk8 = sb("qk8", [8, SC], F32)
        sel = sb("sel", [24, 4], F32)
        seli = sb("seli", [24, 4], I32)
        stok = [sb("stok", [128, 128], BF16) for _ in range(2)]
        stride_bc = sb("stb", [128, SS], F32)
        strabs = sb("sta", [128, SS], F32)
        colsc = sb("colsc", [128, 8], F32)
        coli = sb("coli", [128, 1], I32)
        sc_all = sb("sca", [128, H * SS], F32)
        env = sb("env", [128, SS], F32)
        mval = sb("mva", [128, SS], F32)
        mbc = sb("mbc", [128, H * SS], F32)
        mu8 = sb("mu8", [128, H * SS], U8)
        negt = sb("neg", [128, H * SS], F32)
        mx8 = sb("mx8", [128, H], F32)
        attnb = [sb("attn", [128, H * SS], F32) for _ in range(2)]
        spf = sb("spf", [128, SS], F32)
        spc = sb("spc", [128, SS], F32)
        idxb = [sb("idxi", [128, SS], I32) for _ in range(2)]
        G = [sb("G", [128, 1024], BF16) for _ in range(4)]
        gtmp = sb("gtmp", [128, 256], F32)
        acc = sb("acc", [128, 1024], F32)
        tmul = [sb("tmul", [128, 1024], BF16) for _ in range(2)]
        identb = sb("idb", [128, 128], BF16)
        otT = sb("otT", [128, 1024], BF16)
        zT = sb("zT", [128, 2 * 128], BF16)
        seT = sb("seT", [128, ND * 128], BF16)
        gT = sb("gT", [128, ND * 128], BF16)
        sef = sb("sef", [128, 256], F32)
        ring = Ring(c, "wa", 3, queues=("gp",))
        gring = Ring(c, "ga", 4, queues=("gp",))
        tring = Ring(c, "tr", 2, queues=("sy",))
        sring = Ring(c, "st", 2, queues=("se",))

        lt = []
        lt.append(ring.load(lambda e: e.dma_start(
            out=wwv[:].rearrange("p (d o) -> p d o", o=24),
            in_=w["w_wave"].rearrange("(d p) o -> p d o", p=128))))
        lt.append(ring.load(lambda e: e.dma_start(
            out=wqr[:].rearrange("p (d o) -> p d o", o=128),
            in_=w["w_query"].rearrange("(d p) o -> p d o", p=128))))
        lt.append(ring.load(lambda e: e.dma_start(
            out=wse1[:].rearrange("p (d o) -> p d o", o=256),
            in_=w["w_se1"].rearrange("(d p) o -> p d o", p=128))))
        lt.append(ring.load(lambda e: e.dma_start(
            out=wse2[:].rearrange("p (l o) -> p l o", o=1024),
            in_=w["w_se2"].rearrange("(l p) o -> p l o", p=128))))
        for di in range(ND):
            lt.append(ring.load(lambda e, di=di: e.dma_start(
                out=wout[di][:], in_=w["w_out"][di * 128:(di + 1) * 128, :])))
        tkz = c.step("ve", lambda ve: ve.memset(kwcol[:], 0.0))
        tkw = []
        for h in range(H):
            tkw.append(c.step("gp", lambda e, h=h: e.dma_start(
                out=kwcol[h * 16:(h + 1) * 16, h:h + 1], in_=w["key_weight"][:, None],
                allow_slow_non_contiguous=True), sem="kwl", dma=True, waits=[tkz]))
        tng = c.step("ve", lambda ve: ve.memset(negt[:], -1e30))
        c.step("ve", lambda ve: ve.tensor_scalar(identb[:], c.identi[:], 0, None,
                                                 ALU.is_equal), waits=[c.tok_ident])

        tsel = []
        for jcol, off in ((0, 0), (1, 16), (2, 8)):
            ti = c.step("gp", lambda gp, jcol=jcol, off=off: gp.iota(
                seli[:, jcol:jcol + 1], [[0, 1]], base=-off, channel_multiplier=1))
            t1 = c.step("ve", lambda ve, jcol=jcol: ve.tensor_copy(
                sel[:, jcol:jcol + 1], seli[:, jcol:jcol + 1]), waits=[ti])
            t2 = c.step("ve", lambda ve, jcol=jcol: ve.tensor_scalar(
                sel[:, jcol:jcol + 1], sel[:, jcol:jcol + 1], 2.0, -7.0, ALU.mult, ALU.add),
                waits=[t1])
            t3 = c.step("ve", lambda ve, jcol=jcol: ve.tensor_mul(
                sel[:, 3:4], sel[:, jcol:jcol + 1], sel[:, jcol:jcol + 1]), waits=[t2])
            t4 = c.step("ve", lambda ve, jcol=jcol: ve.tensor_scalar(
                sel[:, jcol:jcol + 1], sel[:, 3:4], 49.5, None, ALU.is_le), waits=[t3])
            tsel.append(t4)

        ti = c.step("gp", lambda gp: gp.iota(idxb[0][:, 0:SS], [[1, SS]], base=-16,
                                             channel_multiplier=0))
        t1 = c.step("ve", lambda ve: ve.tensor_copy(stride_bc[:], idxb[0][:, 0:SS]), waits=[ti])
        t2 = c.step("ve", lambda ve: ve.tensor_scalar(strabs[:], stride_bc[:], -1.0, None, ALU.mult),
                    waits=[t1])
        tstr = c.step("ve", lambda ve: ve.tensor_max(strabs[:], strabs[:], stride_bc[:]), waits=[t2])
        c.barrier()

        ttok = []
        for tj in range(NT):
            for di in range(ND):
                k = tj * ND + di
                bw = [ttok[k - 2]] if k >= 2 else []
                t1 = tring.load(lambda e, tj=tj, di=di, k=k: e.dma_start(
                    out=stok[k % 2][:], in_=c.xl[di][:, tj * 128:(tj + 1) * 128],
                    transpose=True), waits=bw)
                t2 = sring.load(lambda e, tj=tj, di=di, k=k: e.dma_start(
                    out=xtok[tj * 128:(tj + 1) * 128, di * 128:(di + 1) * 128],
                    in_=stok[k % 2][:]), waits=[t1])
                ttok.append(t2)
        c.barrier()

        for sc in range(NSC):
            ssl = slice(sc * SC, (sc + 1) * SC)
            mt = c.step("te", lambda te, ssl=ssl: [te.matmul(
                c.pb[0][0:24, :], wwv[:, di * 24:(di + 1) * 24], c.xl[di][:, ssl],
                start=(di == 0), stop=(di == ND - 1)) for di in range(ND)][-1],
                waits=c.bank(0))
            et = c.step("se", lambda se, ssl=ssl: se.activation(
                wp[:, ssl], c.pb[0][0:24, :], AF.Silu, bias=w["bwvc"][:]), waits=[mt])
            c.bank_done(0, et)
            mt2 = c.step("te", lambda te, ssl=ssl: [te.matmul(
                c.pb[1][:], wqr[:, di * 128:(di + 1) * 128], c.xl[di][:, ssl],
                start=(di == 0), stop=(di == ND - 1)) for di in range(ND)][-1],
                waits=c.bank(1))
            et2 = c.step("se", lambda se, ssl=ssl: se.activation(
                qT[:, ssl], c.pb[1][:], AF.Silu, bias=w["bqc"][:]), waits=[mt2])
            c.bank_done(1, et2)
        tA = c.step("se", lambda se: se.activation(tB[:], wp[:], AF.Tanh))
        tS = c.step("se", lambda se: se.activation(wp[:], wp[:], AF.Sigmoid))

        for sc in range(NSC):
            ssl = slice(sc * SC, (sc + 1) * SC)
            mt = c.step("te", lambda te, ssl=ssl: (
                te.matmul(c.pb[2][0:1, :], sel[:, 0:1], wp[0:24, ssl], start=True, stop=True),
                te.matmul(c.pb[3][0:1, :], sel[:, 1:2], wp[0:24, ssl], start=True, stop=True),
                te.matmul(c.pb[4][0:1, :], sel[:, 2:3], tB[0:24, ssl], start=True, stop=True),
                te.matmul(c.pb[5][0:8, :], kwcol[:], qT[:, ssl], start=True, stop=True))[-1],
                waits=[tS, tA] + tsel + c.bank(2) + c.bank(3) + c.bank(4) + c.bank(5))
            tv = c.step("ve", lambda ve: (
                ve.tensor_scalar(c.srow[0][:], c.pb[2][0:1, :], 15.0 / 8, 1.0, ALU.mult, ALU.add),
                ve.tensor_scalar(c.srow[1][:], c.pb[3][0:1, :], 9.5 / 8, 0.5, ALU.mult, ALU.add),
                ve.tensor_scalar(c.mean_row[:], c.pb[4][0:1, :], 16.0 / 8, None, ALU.mult),
                ve.tensor_copy(qk8[:], c.pb[5][0:8, :]))[-1], waits=[mt])
            for b_ in (2, 3, 4, 5):
                c.bank_done(b_, tv)
            td0 = c.step("sy", lambda e, ssl=ssl: e.dma_start(out=meta[0:1, ssl], in_=c.srow[0][:]),
                         sem="mrow", dma=True, waits=[tv])
            td1 = c.step("sy", lambda e, ssl=ssl: e.dma_start(out=meta[2:3, ssl], in_=c.srow[1][:]),
                         sem="mrow", dma=True, waits=[tv])
            td2 = c.step("sy", lambda e, ssl=ssl: e.dma_start(out=meta[1:2, ssl], in_=c.mean_row[:]),
                         sem="mrow", dma=True, waits=[tv])
            td3 = c.step("sy", lambda e, ssl=ssl: e.dma_start(out=meta[3:11, ssl], in_=qk8[:]),
                         sem="mrow", dma=True, waits=[tv])
        c.barrier()

        mtokW = [None, None]
        gW = [None] * 4
        tmW = [None, None]
        idxW = [None, None]
        attnW = [None, None]
        X = {"te4": None, "tx3": None, "accR": None, "otTR": None, "zTR": None,
             "tgR": None, "gTR": None, "exprev": None, "tc1": None}

        def emitA(tj):
            ix = idxb[tj % 2]
            at = attnb[tj % 2]
            mw = [mtokW[tj % 2]] if mtokW[tj % 2] else []
            mt = c.step("te", lambda te, tj=tj: te.transpose(
                c.pb[0][:, 0:32], meta[:, tj * 128:(tj + 1) * 128], c.ident[0:32, 0:32]),
                waits=c.bank(0) + mw)
            tmt = c.step("ve", lambda ve, tj=tj: ve.tensor_copy(mtok[tj % 2][:], c.pb[0][:, 0:32]),
                         waits=[mt])
            c.bank_done(0, tmt)
            MT = mtok[tj % 2][:]
            pv = [X["te4"]] if X["te4"] else []
            tc0 = c.step("gp", lambda gp, tj=tj: gp.iota(coli[:], [[0, 1]], base=tj * 128,
                                                         channel_multiplier=1),
                         waits=[X["tc1"]] if X["tc1"] else [])
            tc1 = c.step("ve", lambda ve: ve.tensor_copy(colsc[:, 0:1], coli[:]),
                         waits=[tc0] + pv)
            X["tc1"] = tc1
            tc2 = c.step("ve", lambda ve, MT=MT: ve.tensor_add(colsc[:, 1:2], colsc[:, 0:1],
                                                               MT[:, 1:2]), waits=[tc1, tmt])
            ts1 = c.step("ve", lambda ve, MT=MT: ve.tensor_scalar(spf[:], stride_bc[:], MT[:, 0:1],
                                                                  None, ALU.mult),
                         waits=[tmt, tstr] + pv)
            ts2 = c.step("ve", lambda ve: ve.tensor_scalar(spf[:], spf[:], colsc[:, 1:2], None,
                                                           ALU.add), waits=[ts1, tc2])
            tm1 = c.step("ve", lambda ve: ve.tensor_scalar(mval[:], spf[:], 0.0, None, ALU.is_ge),
                         waits=[ts2])
            tm2 = c.step("ve", lambda ve: ve.tensor_scalar(spc[:], spf[:], 2048.0, None, ALU.is_lt),
                         waits=[ts2] + pv)
            tm3 = c.step("ve", lambda ve: ve.tensor_mul(mval[:], mval[:], spc[:]), waits=[tm1, tm2])
            tq1 = c.step("ve", lambda ve: ve.tensor_scalar(spc[:], spf[:], 0.0, 2047.0, ALU.max,
                                                           ALU.min), waits=[tm3])
            iw = idxW[tj % 2] or []
            tq2 = c.step("ve", lambda ve, ix=ix: ve.tensor_copy(ix[:], spc[:]), waits=[tq1] + iw)
            tq3 = c.step("ve", lambda ve, ix=ix: ve.tensor_copy(spf[:], ix[:]), waits=[tq2])
            tq4 = c.step("ve", lambda ve: ve.tensor_tensor(mbc[:, 0:SS], spf[:], spc[:], ALU.is_gt),
                         waits=[tq3])
            tq5 = c.step("ve", lambda ve: ve.tensor_sub(spf[:], spf[:], mbc[:, 0:SS]), waits=[tq4])
            tidx = c.step("ve", lambda ve, ix=ix: ve.tensor_copy(ix[:], spf[:]), waits=[tq5])
            tr1 = c.step("ve", lambda ve, MT=MT: ve.tensor_scalar(spc[:], strabs[:], MT[:, 0:1],
                                                                  None, ALU.mult),
                         waits=[tidx, tmt])
            tsc = None
            psc = [X["tx3"]] if X["tx3"] else []
            for h in range(H):
                tsc = c.step("ve", lambda ve, h=h, MT=MT: ve.tensor_scalar(
                    sc_all[:, h * SS:(h + 1) * SS], spc[:], MT[:, 3 + h:4 + h], 0.25,
                    ALU.mult, ALU.mult), waits=[tr1] + (psc if h == 0 else []))
            te1 = c.step("ve", lambda ve, MT=MT: ve.tensor_scalar(colsc[:, 3:4], MT[:, 2:3], 0.1,
                                                                  None, ALU.max), waits=[tmt])
            te2 = c.step("ve", lambda ve: ve.reciprocal(colsc[:, 4:5], colsc[:, 3:4]), waits=[te1])
            te3 = c.step("ve", lambda ve: ve.tensor_scalar(colsc[:, 5:6], colsc[:, 4:5], -1.0, None,
                                                           ALU.mult), waits=[te2])
            te4 = c.step("se", lambda se: se.activation(env[:], spc[:], AF.Exp,
                                                        scale=colsc[:, 5:6]), waits=[te3, tr1])
            mtokW[tj % 2] = te1
            X["te4"] = te4
            tb1 = c.step("ve", lambda ve: ve.tensor_copy(
                mbc[:], mval[:].unsqueeze(1).broadcast_to([128, H, SS])), waits=[tm3, tq4])
            tb2 = c.step("ve", lambda ve: ve.tensor_copy(mu8[:], mbc[:]), waits=[tb1])
            aw = attnW[tj % 2] or []
            tb3 = c.step("ve", lambda ve, at=at: ve.select(at[:], mu8[:], sc_all[:], negt[:]),
                         waits=[tb2, tsc, tng] + aw)
            tx1 = c.step("ve", lambda ve, at=at: ve.reduce_max(
                mx8[:].unsqueeze(2), at[:].rearrange("p (h s) -> p h s", s=SS), AX.X,
                negate=True), waits=[tb3])
            tx2 = c.step("ve", lambda ve, at=at: ve.tensor_add(
                sc_all[:].rearrange("p (h s) -> p h s", s=SS),
                at[:].rearrange("p (h s) -> p h s", s=SS),
                mx8[:].unsqueeze(2).broadcast_to([128, H, SS])), waits=[tx1])
            tx3 = c.step("se", lambda se, at=at: se.activation(at[:], sc_all[:], AF.Exp),
                         waits=[tx2])
            X["tx3"] = tx3
            tx4 = c.step("ve", lambda ve, at=at: ve.reduce_sum(
                mx8[:].unsqueeze(2), at[:].rearrange("p (h s) -> p h s", s=SS), AX.X),
                waits=[tx3])
            tx5 = c.step("ve", lambda ve: ve.reciprocal(mx8[:], mx8[:]), waits=[tx4])
            tx6 = c.step("ve", lambda ve, at=at: ve.tensor_mul(
                at[:].rearrange("p (h s) -> p h s", s=SS),
                at[:].rearrange("p (h s) -> p h s", s=SS),
                mx8[:].unsqueeze(2).broadcast_to([128, H, SS])), waits=[tx5])
            tx7 = c.step("ve", lambda ve, at=at: ve.tensor_mul(
                at[:].rearrange("p (h s) -> p h s", s=SS),
                at[:].rearrange("p (h s) -> p h s", s=SS),
                env[:].unsqueeze(1).broadcast_to([128, H, SS])), waits=[tx6, te4])
            tx8 = c.step("ve", lambda ve, at=at: ve.tensor_mul(at[:], at[:], mbc[:]), waits=[tx7])
            tx9 = c.step("ve", lambda ve, at=at: ve.reduce_sum(
                mx8[:].unsqueeze(2), at[:].rearrange("p (h s) -> p h s", s=SS), AX.X),
                waits=[tx8])
            tx10 = c.step("ve", lambda ve: ve.tensor_scalar(mx8[:], mx8[:], 1e-8, None, ALU.add),
                          waits=[tx9])
            tx11 = c.step("ve", lambda ve: ve.reciprocal(mx8[:], mx8[:]), waits=[tx10])
            tx12 = c.step("ve", lambda ve, at=at: ve.tensor_mul(
                at[:].rearrange("p (h s) -> p h s", s=SS),
                at[:].rearrange("p (h s) -> p h s", s=SS),
                mx8[:].unsqueeze(2).broadcast_to([128, H, SS])), waits=[tx11])
            return {"tidx": tidx, "tx12": tx12, "ix": ix, "at": at}

        def emitB(tj, a):
            tjs = slice(tj * 128, (tj + 1) * 128)
            ix, at = a["ix"], a["at"]
            tm = None
            gts = []
            ta1 = ta2 = None
            for s_ in range(SS):
                gw = list(gW[s_ % 4]) if gW[s_ % 4] else []
                gt = gring.load(lambda e, s_=s_, ix=ix: e.indirect_dma_start(
                    out=G[s_ % 4][:], out_offset=None, in_=xtok,
                    in_offset=IndirectOffsetOnAxis(ap=ix[:, s_:s_ + 1], axis=0)),
                    waits=gw + [a["tidx"]])
                gts.append(gt)

                def scfn_ve(ve, s_=s_, at=at):
                    last = None
                    for h in range(5):
                        last = ve.tensor_scalar(
                            tmul[s_ % 2][:, h * 128:(h + 1) * 128],
                            G[s_ % 4][:, h * 128:(h + 1) * 128],
                            at[:, h * SS + s_:h * SS + s_ + 1], None, ALU.mult)
                    return last

                def scfn_se(se, s_=s_, at=at):
                    last = None
                    for h in range(5, H):
                        last = se.activation(
                            tmul[s_ % 2][:, h * 128:(h + 1) * 128],
                            G[s_ % 4][:, h * 128:(h + 1) * 128], AF.Copy,
                            scale=at[:, h * SS + s_:h * SS + s_ + 1])
                    return last

                tw = [tmW[s_ % 2]] if tmW[s_ % 2] else []
                ta1 = c.step("ve", scfn_ve, waits=[gt, a["tx12"]] + tw)
                ta2 = c.step("se", scfn_se, waits=[gt, a["tx12"]] + tw)
                bw = (c.bank(4) + c.bank(5)) if s_ == 0 else []

                def accfn(te, s_=s_):
                    last = None
                    for b in range(2):
                        last = te.matmul(c.pb[4 + b][:], identb[:],
                                         tmul[s_ % 2][:, b * 512:(b + 1) * 512],
                                         start=(s_ == 0), stop=(s_ == SS - 1))
                    return last

                tm = c.step("te", accfn, waits=[ta1, ta2] + bw)
                tmW[s_ % 2] = tm
                gW[s_ % 4] = [ta1, ta2]
            idxW[tj % 2] = gts[-4:]
            attnW[tj % 2] = [ta1, ta2]

            def evfn(se):
                se.activation(acc[:, 0:512], c.pb[4][:], AF.Copy)
                return se.activation(acc[:, 512:1024], c.pb[5][:], AF.Copy)

            ar = [X["accR"]] if X["accR"] else []
            tmv = c.step("se", evfn, waits=[tm] + ar)
            c.bank_done(4, tmv)
            c.bank_done(5, tmv)
            otr = X["otTR"] or []
            mt = et = None
            for oi in range(ND):
                bidx = oi % 2
                mt = c.step("te", lambda te, oi=oi, bidx=bidx: te.transpose(
                    c.pb[bidx][:, 0:128], acc[:, oi * 128:(oi + 1) * 128], c.ident[:]),
                    waits=[tmv] + c.bank(bidx))
                et = c.step("se", lambda se, oi=oi, bidx=bidx: se.activation(
                    otT[:, oi * 128:(oi + 1) * 128], c.pb[bidx][:, 0:128], AF.Copy),
                    waits=[mt] + (otr if oi == 0 else []))
                c.bank_done(bidx, et)
            X["accR"] = mt
            zr = [X["zTR"]] if X["zTR"] else []
            mt1 = None
            for oi2 in range(2):
                bidx = 2 + oi2
                mt1 = c.step("te", lambda te, oi2=oi2, bidx=bidx: [te.matmul(
                    c.pb[bidx][:, 0:128], wse1[:, di * 256 + oi2 * 128:di * 256 + (oi2 + 1) * 128],
                    otT[:, di * 128:(di + 1) * 128], start=(di == 0), stop=(di == ND - 1))
                    for di in range(ND)][-1], waits=[et] + c.bank(bidx))
                ez = c.step("se", lambda se, oi2=oi2, bidx=bidx: se.activation(
                    zT[:, oi2 * 128:(oi2 + 1) * 128], c.pb[bidx][:, 0:128], AF.Silu,
                    bias=w["bs1c"][:, oi2:oi2 + 1]), waits=[mt1] + (zr if oi2 == 0 else []))
                c.bank_done(bidx, ez)
            tgr = [X["tgR"]] if X["tgR"] else []
            mt2 = None
            for oi in range(ND):
                bidx = 4 + oi % 2
                mt2 = c.step("te", lambda te, oi=oi, bidx=bidx: [te.matmul(
                    c.pb[bidx][:, 0:128], wse2[:, li * 1024 + oi * 128:li * 1024 + (oi + 1) * 128],
                    zT[:, li * 128:(li + 1) * 128], start=(li == 0), stop=(li == 1))
                    for li in range(2)][-1], waits=[ez] + c.bank(bidx))
                es = c.step("se", lambda se, oi=oi, bidx=bidx: se.activation(
                    seT[:, oi * 128:(oi + 1) * 128], c.pb[bidx][:, 0:128], AF.Sigmoid,
                    bias=w["bs2c"][:, oi:oi + 1]), waits=[mt2] + (tgr if oi == 0 else []))
                c.bank_done(bidx, es)
            X["zTR"] = mt2
            X["otTR"] = [mt1]
            gr = [X["gTR"]] if X["gTR"] else []
            tg = c.step("ve", lambda ve: ve.tensor_mul(gT[:], otT[:], seT[:]), waits=[es] + gr)
            X["otTR"].append(tg)
            X["tgR"] = tg
            exprev = X["exprev"]
            mtw = None
            for di in range(ND):
                bidx = 6 + di % 2
                mtw = c.step("te", lambda te, di=di, bidx=bidx: [te.matmul(
                    c.pb[bidx][:, 0:128], wout[ii][:, di * 128:(di + 1) * 128],
                    gT[:, ii * 128:(ii + 1) * 128], start=(ii == 0), stop=(ii == ND - 1))
                    for ii in range(ND)][-1], waits=[tg] + c.bank(bidx))
                ef = c.step("se", lambda se, bidx=bidx, di=di: se.activation(
                    sef[:, (di % 2) * 128:(di % 2 + 1) * 128], c.pb[bidx][:, 0:128], AF.Silu),
                    waits=[mtw] + ([exprev] if exprev else []))
                c.bank_done(bidx, ef)
                ex = c.step("ve", lambda ve, di=di, tjs=tjs: ve.tensor_add(
                    c.xr[di][:, tjs], c.xr[di][:, tjs], sef[:, (di % 2) * 128:(di % 2 + 1) * 128]),
                    waits=[ef])
                if di % 2 == 1:
                    exprev = ex
            X["exprev"] = exprev
            X["gTR"] = mtw

        Aq = emitA(0)
        for tj in range(NT):
            nxt = emitA(tj + 1) if tj + 1 < NT else None
            emitB(tj, Aq)
            Aq = nxt
    c.barrier()


def emit_l1_attn_b(c, w):
    nc = c.nc
    with ExitStack() as sctx:
        xlf = [sctx.enter_context(nc.sbuf_tensor(c.nid("xlf"), [128, S], F32)) for _ in range(ND)]
        emit_ln(c, out_tiles=xlf, out_dtype=F32)
        B = [sctx.enter_context(nc.sbuf_tensor(c.nid("sbuf"), [128, 1024], F32)) for _ in range(7)]
        carry = sctx.enter_context(nc.sbuf_tensor(c.nid("carry"), [128, 1], F32))
        msk = sctx.enter_context(nc.sbuf_tensor(c.nid("msk"), [128, 1024], U8))
        CM1 = c.cc[:, CCOL[-1.0]:CCOL[-1.0] + 1]
        C1 = c.cc[0:1, CCOL[1.0]:CCOL[1.0] + 1]

        for di in range(ND):
            prev_scan = None
            for hh in range(2):
                sl = slice(hh * 1024, (hh + 1) * 1024)
                xg = xlf[di]
                t1 = c.step("ve", lambda ve, xg=xg, sl=sl, di=di: (
                    ve.tensor_scalar(B[0][:], xg[:, sl], w["wfc"][:, di:di + 1], None, ALU.mult),
                    ve.tensor_scalar(B[1][:], xg[:, sl], w["wic"][:, di:di + 1], None, ALU.mult),
                    ve.tensor_scalar(B[2][:], xg[:, sl], w["whc"][:, di:di + 1], None, ALU.mult))[-1])
                sp = []
                for gi in (0, 1):
                    ta1 = c.step("se", lambda se, gi=gi: se.activation(B[3][:], B[gi][:], AF.Abs),
                                 waits=[t1])
                    ta2 = c.step("se", lambda se: se.activation(B[3][:], B[3][:], AF.Exp, scale=CM1))
                    ta3 = c.step("se", lambda se: se.activation(B[3][:], B[3][:], AF.Ln,
                                                                bias=c.cc[:, CCOL[1.0]:CCOL[1.0] + 1]))
                    tr = c.step("ve", lambda ve, gi=gi: ve.tensor_scalar(
                        B[4][:], B[gi][:], -1.0, 0.0, ALU.mult, ALU.max), waits=[t1])
                    tsp = c.step("gp", lambda gp, gi=gi: gp.tensor_add(B[5 + gi][:], B[4][:], B[3][:]),
                                 waits=[ta3, tr])
                    sp.append(tsp)
                tdiff = c.step("gp", lambda gp: gp.tensor_sub(B[0][:], B[5][:], B[6][:]), waits=sp)
                tf = c.step("se", lambda se: se.activation(B[5][:], B[0][:], AF.Sigmoid, scale=CM1),
                            waits=[tdiff])
                ti = c.step("se", lambda se: se.activation(B[6][:], B[0][:], AF.Sigmoid),
                            waits=[tdiff])
                tg1 = c.step("ve", lambda ve: ve.tensor_scalar(B[1][:], B[2][:], 0.5, None, ALU.add),
                             waits=[tdiff])
                tg2 = c.step("se", lambda se: se.activation(B[0][:], B[2][:], AF.Sigmoid), waits=[tg1])
                tm = c.step("ve", lambda ve: ve.tensor_scalar(msk[:], B[2][:], 0.0, None, ALU.is_ge),
                            waits=[tg1])
                tsel = c.step("ve", lambda ve: ve.select(B[4][:], msk[:], B[1][:], B[0][:]),
                              waits=[tg2, tm])
                tv = c.step("gp", lambda gp: gp.tensor_mul(B[4][:], B[6][:], B[4][:]),
                            waits=[tsel, ti])
                init = 0.0 if hh == 0 else carry[:]
                scw = [tv, tf] + ([prev_scan] if prev_scan else [])
                tscan = c.step("ve", lambda ve, init=init: ve.tensor_tensor_scan(
                    B[1][:], B[5][:], B[4][:], init, ALU.mult, ALU.add), waits=scw)
                tres = c.step("ve", lambda ve, di=di, sl=sl: ve.tensor_add(
                    c.xr[di][:, sl], c.xr[di][:, sl], B[1][:]), waits=[tscan])
                if hh == 0:
                    prev_scan = c.step("ve", lambda ve: ve.tensor_copy(
                        carry[:], B[1][:, 1023:1024]), waits=[tscan])
                else:
                    prev_scan = None
    c.barrier()


def emit_load_x(c, xin):
    qs = ("sy", "se", "gp")
    c._x_toks = [c.step(qs[di % 3], lambda e, di=di: e.dma_start(
        out=c.xr[di][:], in_=xin[di * 128:(di + 1) * 128, :]),
        sem=f"qx{di % 3}", dma=True) for di in range(ND)]


def emit_store_x(c, xout):
    c.barrier()
    c.sem("qx")

    def stfn(sy):
        for di in range(ND):
            sy.dma_start(out=xout[di * 128:(di + 1) * 128, :], in_=c.xr[di][:]).then_inc(c.sems["qx"], 16)
            c.cnt["qx"] += 16
        sy.wait_ge(c.sems["qx"], c.cnt["qx"])
        return None

    c.step("sy", stfn)


def emit_load_cols(c, pairs):
    c.sem("qx")

    def ldfn(sy):
        for t, d in pairs:
            sy.dma_start(out=t[:], in_=d).then_inc(c.sems["qx"], 16)
            c.cnt["qx"] += 16
        sy.wait_ge(c.sems["qx"], c.cnt["qx"])
        return None

    c.step("sy", ldfn)
    c.barrier()


def _prep(inputs):
    p = {}
    p["w1tf"] = np.ascontiguousarray(inputs["L0_ffn_W1"].T)
    p["w2tf"] = np.ascontiguousarray(inputs["L0_ffn_W2"].T)
    p["l0b1c"] = np.ascontiguousarray(inputs["L0_ffn_b1"].reshape(NF, 128).T)
    p["l0b2c"] = np.ascontiguousarray(inputs["L0_ffn_b2"].reshape(ND, 128).T)
    p["l1w1t"] = np.ascontiguousarray(inputs["L1_ffn_w1_w"].T)
    p["l1w2t"] = np.ascontiguousarray(inputs["L1_ffn_w2_w"].T)
    p["l1h1t"] = np.ascontiguousarray(inputs["L1_ffn_h1_w"].T)
    p["l1h2t"] = np.ascontiguousarray(inputs["L1_ffn_h2_w"].T)
    p["l1w3t"] = np.ascontiguousarray(inputs["L1_ffn_w3_w"].T)
    p["l1s1c"] = np.ascontiguousarray(inputs["L1_ffn_w1_s"].reshape(NF, 128).T)
    p["l1s2c"] = np.ascontiguousarray(inputs["L1_ffn_w2_s"].reshape(NF, 128).T)
    p["l1h1c"] = np.ascontiguousarray(inputs["L1_ffn_h1_s"].reshape(NF, 128).T)
    p["l1h2c"] = np.ascontiguousarray(inputs["L1_ffn_h2_s"].reshape(NF, 128).T)
    p["l1w3c"] = np.ascontiguousarray(inputs["L1_ffn_w3_s"].reshape(ND, 128).T)
    p["boc"] = np.ascontiguousarray(inputs["L0_attn_a_bo"].reshape(ND, 128).T)
    p["bwvc"] = np.ascontiguousarray(inputs["L1_attn_a_b_wave"].reshape(24, 1))
    p["bqc"] = np.ascontiguousarray(inputs["L1_attn_a_b_query"].reshape(128, 1))
    p["bs1c"] = np.ascontiguousarray(inputs["L1_attn_a_b_se1"].reshape(2, 128).T)
    p["bs2c"] = np.ascontiguousarray(inputs["L1_attn_a_b_se2"].reshape(ND, 128).T)
    p["kwr"] = np.ascontiguousarray(inputs["L1_attn_a_key_weight"].reshape(16))
    p["wic"] = np.ascontiguousarray(inputs["L1_attn_b_w_i"].reshape(ND, 128).T)
    p["wfc"] = np.ascontiguousarray(inputs["L1_attn_b_w_f"].reshape(ND, 128).T)
    p["whc"] = np.ascontiguousarray(inputs["L1_attn_b_w_h"].reshape(ND, 128).T)
    return p


def build_l0_ffn(first=True):
    c = C()
    nc = c.nc
    xin = nc.dram_tensor("xin", [DM, S], F32, kind="ExternalInput").ap()
    xout = nc.dram_tensor("xout", [DM, S], F32, kind="ExternalOutput").ap()
    w = {}
    if first:
        w["w1tf"] = nc.dram_tensor("w1tf", [DM, FF], F32, kind="ExternalInput").ap()
        w["w2tf"] = nc.dram_tensor("w2tf", [FF, DM], F32, kind="ExternalInput").ap()
        w["w1t"] = nc.dram_tensor("w1t", [DM, FF], BF16, kind="ExternalOutput").ap()
        w["w2t"] = nc.dram_tensor("w2t", [FF, DM], BF16, kind="ExternalOutput").ap()
    else:
        w["w1t"] = nc.dram_tensor("w1t", [DM, FF], BF16, kind="ExternalInput").ap()
        w["w2t"] = nc.dram_tensor("w2t", [FF, DM], BF16, kind="ExternalInput").ap()
    b1c_d = nc.dram_tensor("b1c", [128, NF], F32, kind="ExternalInput").ap()
    b2c_d = nc.dram_tensor("b2c", [128, ND], F32, kind="ExternalInput").ap()
    c.setup()
    w["b1c"] = c.sb("b1cs", [128, NF], F32)
    w["b2c"] = c.sb("b2cs", [128, ND], F32)
    emit_load_cols(c, [(w["b1c"], b1c_d), (w["b2c"], b2c_d)])
    emit_load_x(c, xin)
    if first:
        xsp = nc.dram_tensor("xqsp", [DM, S], F32, kind="Internal").ap()
        emit_ln(c)
        _emit_spill(c, xsp, store=True)
        c.barrier()
        for _ in emit_l0_ffn_quant(c, w):
            pass
        c.barrier()
        _emit_spill(c, xsp, store=False)
        c.barrier()
    emit_l0_ffn(c, w, False)
    emit_store_x(c, xout)
    c.flush()
    c.ctx.close()
    return nc


def build_l1_ffn():
    c = C()
    nc = c.nc
    xin = nc.dram_tensor("xin", [DM, S], F32, kind="ExternalInput").ap()
    xout = nc.dram_tensor("xout", [DM, S], F32, kind="ExternalOutput").ap()
    w = {}
    for n, shp in (("w1t", [DM, FF]), ("w2t", [DM, FF]), ("h1t", [FF, FF]),
                   ("h2t", [FF, FF]), ("w3t", [FF, DM])):
        w[n] = nc.dram_tensor(n, shp, F32, kind="ExternalInput").ap()
    cols_d = []
    for n, k in (("s1c", NF), ("s2c", NF), ("h1c", NF), ("h2c", NF), ("w3c", ND)):
        d = nc.dram_tensor(n, [128, k], F32, kind="ExternalInput").ap()
        cols_d.append((n, k, d))
    xspill = nc.dram_tensor("xspill", [DM, S], F32, kind="Internal").ap()
    c.setup()
    pairs = []
    for n, k, d in cols_d:
        w[n] = c.sb(n + "s", [128, k], F32)
        pairs.append((w[n], d))
    emit_load_cols(c, pairs)
    emit_load_x(c, xin)
    emit_l1_ffn(c, w, xspill)
    emit_store_x(c, xout)
    c.flush()
    c.ctx.close()
    return nc


def build_l0_attn_a():
    c = C()
    nc = c.nc
    xin = nc.dram_tensor("xin", [DM, S], F32, kind="ExternalInput").ap()
    xout = nc.dram_tensor("xout", [DM, S], F32, kind="ExternalOutput").ap()
    w = {}
    for n, shp in (("wq", [DM, DM]), ("wkv", [DM, 2 * DM]), ("wo", [DM, DM])):
        w[n] = nc.dram_tensor(n, shp, F32, kind="ExternalInput").ap()
    boc_d = nc.dram_tensor("boc", [128, ND], F32, kind="ExternalInput").ap()
    xspill = nc.dram_tensor("xspill", [DM, S], F32, kind="Internal").ap()
    c.setup()
    w["boc"] = c.sb("bocs", [128, ND], F32)
    emit_load_cols(c, [(w["boc"], boc_d)])
    emit_load_x(c, xin)
    emit_l0_attn_a(c, w, xspill, xin)
    emit_store_x(c, xout)
    c.flush()
    c.ctx.close()
    return nc


def build_l1_attn_a():
    c = C()
    nc = c.nc
    xin = nc.dram_tensor("xin", [DM, S], F32, kind="ExternalInput").ap()
    xout = nc.dram_tensor("xout", [DM, S], F32, kind="ExternalOutput").ap()
    w = {}
    for n, shp in (("w_wave", [DM, 24]), ("w_query", [DM, 128]), ("w_se1", [DM, 256]),
                   ("w_se2", [256, DM]), ("w_out", [DM, DM]), ("key_weight", [16]),
                   ("bwvc", [24, 1]), ("bqc", [128, 1]), ("bs1c", [128, 2]), ("bs2c", [128, ND])):
        w[n] = nc.dram_tensor(n, shp, F32, kind="ExternalInput").ap()
    xtok = nc.dram_tensor("xtok", [S, DM], BF16, kind="Internal").ap()
    c.setup()
    cols = []
    for n in ("bwvc", "bqc", "bs1c", "bs2c"):
        t = c.sb(n + "s", list(w[n].shape), F32)
        cols.append((t, w[n]))
        w[n] = t
    emit_load_cols(c, cols)
    emit_load_x(c, xin)
    emit_l1_attn_a(c, w, xtok)
    emit_store_x(c, xout)
    c.flush()
    c.ctx.close()
    return nc


def build_l0_attn_b():
    c = C()
    nc = c.nc
    xin = nc.dram_tensor("xin", [DM, S], F32, kind="ExternalInput").ap()
    xout = nc.dram_tensor("xout", [DM, S], F32, kind="ExternalOutput").ap()
    w = {}
    for n, shp in (("Wq", [DM, DM]), ("Wk", [DM, DM]), ("Wv", [DM, DM]), ("Wo", [DM, DM]),
                   ("Wdn", [DM, 256]), ("Wuk", [256, DM]), ("Wuv", [256, DM])):
        w[n] = nc.dram_tensor(n, shp, F32, kind="ExternalInput").ap()
    xspill = nc.dram_tensor("xspill", [DM, S], F32, kind="Internal").ap()
    c.setup()
    emit_load_x(c, xin)
    emit_l0_attn_b(c, w, xspill)
    emit_store_x(c, xout)
    c.flush()
    c.ctx.close()
    return nc


def build_l1_attn_b():
    c = C()
    nc = c.nc
    xin = nc.dram_tensor("xin", [DM, S], F32, kind="ExternalInput").ap()
    xout = nc.dram_tensor("xout", [DM, S], F32, kind="ExternalOutput").ap()
    cols_d = [(n, nc.dram_tensor(n, [128, ND], F32, kind="ExternalInput").ap())
              for n in ("wic", "wfc", "whc")]
    c.setup()
    w = {}
    pairs = []
    for n, d in cols_d:
        w[n] = c.sb(n + "s", [128, ND], F32)
        pairs.append((w[n], d))
    emit_load_cols(c, pairs)
    emit_load_x(c, xin)
    emit_l1_attn_b(c, w)
    emit_store_x(c, xout)
    c.flush()
    c.ctx.close()
    return nc


def build_fused():
    c = C()
    nc = c.nc
    xin = nc.dram_tensor("xin", [DM, S], F32, kind="ExternalInput").ap()
    xout = nc.dram_tensor("xout", [DM, S], F32, kind="ExternalOutput").ap()
    dt_ = lambda n, shp, dtp=F32, kind="ExternalInput": nc.dram_tensor(n, shp, dtp, kind=kind).ap()

    wa = {"wq": dt_("l0a_wq", [DM, DM]), "wkv": dt_("l0a_wkv", [DM, 2 * DM]),
          "wo": dt_("l0a_wo", [DM, DM])}
    wf0 = {"w1tf": dt_("l0f_w1tf", [DM, FF]), "w2tf": dt_("l0f_w2tf", [FF, DM]),
           "w1t": dt_("l0f_w1t", [DM, FF], BF16, "Internal"),
           "w2t": dt_("l0f_w2t", [FF, DM], BF16, "Internal")}
    wb = {n: dt_("l0b_" + n, shp) for n, shp in (("Wq", [DM, DM]), ("Wk", [DM, DM]),
          ("Wv", [DM, DM]), ("Wo", [DM, DM]), ("Wdn", [DM, 256]), ("Wuk", [256, DM]),
          ("Wuv", [256, DM]))}
    wa1 = {n: dt_("l1a_" + n, shp) for n, shp in (("w_wave", [DM, 24]), ("w_query", [DM, 128]),
           ("w_se1", [DM, 256]), ("w_se2", [256, DM]), ("w_out", [DM, DM]), ("key_weight", [16]))}
    wf1 = {n: dt_("l1f_" + n, shp) for n, shp in (("w1t", [DM, FF]), ("w2t", [DM, FF]),
           ("h1t", [FF, FF]), ("h2t", [FF, FF]), ("w3t", [FF, DM]))}
    coldefs = (("l0a_boc", [128, ND]), ("l0f_b1c", [128, NF]), ("l0f_b2c", [128, ND]),
               ("l1a_bwvc", [24, 1]), ("l1a_bqc", [128, 1]), ("l1a_bs1c", [128, 2]),
               ("l1a_bs2c", [128, ND]), ("l1f_s1c", [128, NF]), ("l1f_s2c", [128, NF]),
               ("l1f_h1c", [128, NF]), ("l1f_h2c", [128, NF]), ("l1f_w3c", [128, ND]),
               ("l1b_wic", [128, ND]), ("l1b_wfc", [128, ND]), ("l1b_whc", [128, ND]))
    cold = {n: dt_(n, shp) for n, shp in coldefs}
    xspill = dt_("xspill", [DM, S], F32, "Internal")
    xtok = dt_("xtok", [S, DM], BF16, "Internal")

    c.setup()
    pairs = []
    cols = {}
    for n, shp in coldefs:
        t = c.sb(n + "s", shp, F32)
        cols[n] = t
        pairs.append((t, cold[n]))
    emit_load_cols(c, pairs)
    wa["boc"] = cols["l0a_boc"]
    wf0["b1c"], wf0["b2c"] = cols["l0f_b1c"], cols["l0f_b2c"]
    wa1.update({"bwvc": cols["l1a_bwvc"], "bqc": cols["l1a_bqc"],
                "bs1c": cols["l1a_bs1c"], "bs2c": cols["l1a_bs2c"]})
    wf1.update({"s1c": cols["l1f_s1c"], "s2c": cols["l1f_s2c"], "h1c": cols["l1f_h1c"],
                "h2c": cols["l1f_h2c"], "w3c": cols["l1f_w3c"]})
    wb1 = {"wic": cols["l1b_wic"], "wfc": cols["l1b_wfc"], "whc": cols["l1b_whc"]}

    emit_load_x(c, xin)
    emit_l0_attn_a(c, wa, xspill, xin, after_ln=lambda: emit_l0_ffn_quant(c, wf0))
    emit_l0_ffn(c, wf0, first=False)
    emit_l0_attn_b(c, wb, xspill)
    emit_l0_ffn(c, wf0, first=False)
    emit_l1_attn_a(c, wa1, xtok)
    emit_l1_ffn(c, wf1, xspill)
    emit_l1_attn_b(c, wb1)
    emit_l1_ffn(c, wf1, xspill)
    emit_store_x(c, xout)
    c.flush()
    c.ctx.close()
    return nc


def kernel(**inputs):
    B = 8
    x = inputs["L0_attn_a_x"]
    xt = [np.ascontiguousarray(np.asarray(x[i]).T).astype(np.float32) for i in range(B)]
    p = _prep(inputs)
    nc = build_fused()
    common = {
        "l0a_wq": inputs["L0_attn_a_wq"], "l0a_wkv": inputs["L0_attn_a_wkv"],
        "l0a_wo": inputs["L0_attn_a_wo"], "l0a_boc": p["boc"],
        "l0f_w1tf": p["w1tf"], "l0f_w2tf": p["w2tf"],
        "l0f_b1c": p["l0b1c"], "l0f_b2c": p["l0b2c"],
        "l0b_Wq": inputs["L0_attn_b_Wq"], "l0b_Wk": inputs["L0_attn_b_Wk"],
        "l0b_Wv": inputs["L0_attn_b_Wv"], "l0b_Wo": inputs["L0_attn_b_Wo"],
        "l0b_Wdn": inputs["L0_attn_b_W_down_kv"], "l0b_Wuk": inputs["L0_attn_b_W_up_k"],
        "l0b_Wuv": inputs["L0_attn_b_W_up_v"],
        "l1a_w_wave": inputs["L1_attn_a_w_wave"], "l1a_w_query": inputs["L1_attn_a_w_query"],
        "l1a_w_se1": inputs["L1_attn_a_w_se1"], "l1a_w_se2": inputs["L1_attn_a_w_se2"],
        "l1a_w_out": inputs["L1_attn_a_w_out"], "l1a_key_weight": p["kwr"],
        "l1a_bwvc": p["bwvc"], "l1a_bqc": p["bqc"], "l1a_bs1c": p["bs1c"], "l1a_bs2c": p["bs2c"],
        "l1f_w1t": p["l1w1t"], "l1f_w2t": p["l1w2t"], "l1f_h1t": p["l1h1t"],
        "l1f_h2t": p["l1h2t"], "l1f_w3t": p["l1w3t"],
        "l1f_s1c": p["l1s1c"], "l1f_s2c": p["l1s2c"], "l1f_h1c": p["l1h1c"],
        "l1f_h2c": p["l1h2c"], "l1f_w3c": p["l1w3c"],
        "l1b_wic": p["wic"], "l1b_wfc": p["wfc"], "l1b_whc": p["whc"],
    }
    maps = [dict(common, xin=xt[i]) for i in range(B)]
    res = run_bass_kernel_spmd(nc, maps, core_ids=list(range(B)))
    return np.stack([np.ascontiguousarray(np.asarray(res.results[i]["xout"], np.float32).T)
                     for i in range(B)])


# version 29
# speedup vs baseline: 1.0134x; 1.0134x over previous
import numpy as np
import concourse.bass as bass
import concourse.mybir as mybir
from concourse.bass import IndirectOffsetOnAxis
from concourse.bass_utils import run_bass_kernel_spmd
from contextlib import ExitStack

F32 = mybir.dt.float32
BF16 = mybir.dt.bfloat16
I32 = mybir.dt.int32
U8 = mybir.dt.uint8
AF = mybir.ActivationFunctionType
ALU = mybir.AluOpType
AX = mybir.AxisListType

S = 2048
DM = 1024
ND = 8
NSC = 4
SC = 512
NT = 16
FF = 4096
NF = 32

CCOL_VALS = (1e-5, 1.0, -1.0, 0.5, 2047.0, 0.0, -1e-4, 1e-4, 2e-4, -2e-4, 0.03125, 0.125)
CCOL = {v: i for i, v in enumerate(CCOL_VALS)}

ENGBODY = {"te": "tensor", "ve": "vector", "se": "scalar", "gp": "gpsimd", "sy": "sync"}
ENGSEM = {"te": "pe", "ve": "ve", "se": "se", "gp": "gp", "sy": "sy"}


class C:
    def __init__(self):
        self.nc = bass.Bass("TRN2", target_bir_lowering=False, debug=False)
        self.ctx = ExitStack()
        self.cnt = {}
        self.sems = {}
        self.pending = {}
        self.watermark = {}
        self.own_count = {}
        self.uid = 0

    def nid(self, p):
        self.uid += 1
        return f"{p}{self.uid}"

    def sem(self, name):
        if name not in self.sems:
            self.sems[name] = self.ctx.enter_context(self.nc.semaphore(name))
            self.cnt[name] = 0
        return self.sems[name]

    def inc(self, instr, name, n=1):
        instr.then_inc(self.sems[name], n)
        self.cnt[name] += n
        return (name, self.cnt[name])

    def sb(self, name, shape, dtype):
        return self.ctx.enter_context(self.nc.sbuf_tensor(self.nid(name), shape, dtype))

    def ps(self, name, shape=(128, 512), dtype=F32):
        return self.ctx.enter_context(self.nc.psum_tensor(self.nid(name), list(shape), dtype))

    def step(self, eng, fn, waits=(), sem=None, dma=False):
        semn = sem or ENGSEM[eng]
        self.sem(semn)
        n = 16 if dma else 1
        self.cnt[semn] += n
        tok = (semn, self.cnt[semn])
        q = self.pending.setdefault(eng, [])
        q.append((fn, tuple(waits), semn, n))
        if len(q) >= 600:
            self.flush(eng)
        return tok

    def flush(self, eng=None):
        if eng is None:
            for e in list(self.pending):
                self.flush(e)
            return
        q = self.pending.get(eng)
        if not q:
            return
        self.pending[eng] = []
        body = getattr(self.blk, ENGBODY[eng])
        wm = self.watermark.setdefault(eng, {})

        def b(e):
            for fn, waits, semn, n in q:
                for (s, v) in waits:
                    if v > 0 and v > wm.get(s, 0):
                        e.wait_ge(self.sems[s], v)
                        wm[s] = v
                last = fn(e)
                if last is None:
                    last = e.nop()
                last.then_inc(self.sems[semn], n)
            return None

        body(b)

    def setup(self):
        nc = self.nc
        self.xr = [self.sb("xr", [128, S], F32) for _ in range(ND)]
        self.xl = [self.sb("xl", [128, S], BF16) for _ in range(ND)]
        self.PB = self.ps("PB", (128, 4096))
        self.pb = [self.PB[:, i * 512:(i + 1) * 512] for i in range(8)]
        self.pbank_tok = [None] * 8
        for s in ("pe", "ve", "se", "gp", "sy"):
            self.sem(s)
        self.ones_bf = self.sb("onesbf", [128, 1], BF16)
        self.ones_colf = self.sb("onescf", [128, 1], F32)
        self.ones_row = self.sb("onesrow", [1, 128], F32)
        self.ident = self.sb("ident", [128, 128], F32)
        self.identi = self.sb("identi", [128, 128], I32)
        self.mean_row = self.sb("meanrow", [1, SC], F32)
        self.rstd_row = self.sb("rstdrow", [1, SC], F32)
        self.srow = [self.sb("srow", [1, SC], F32) for _ in range(2)]
        self.tscr = self.sb("tscr", [128, SC], F32)
        self.cc = self.sb("cc", [128, len(CCOL_VALS)], F32)
        self.blk = self.ctx.enter_context(nc.Block("main"))

        def init_ve(ve):
            ve.memset(self.ones_bf[:], 1.0)
            ve.memset(self.ones_colf[:], 1.0)
            for j, v in enumerate(CCOL_VALS):
                ve.memset(self.cc[:, j:j + 1], v)
            return ve.memset(self.ones_row[:], 1.0)

        tv = self.step("ve", init_ve)

        def init_gp(gp):
            return gp.iota(self.identi[:], [[-1, 128]], base=0, channel_multiplier=1)

        tg = self.step("gp", init_gp)
        self.tok_ident = self.step("ve", lambda ve: ve.tensor_scalar(
            self.ident[:], self.identi[:], 0, None, ALU.is_equal), waits=[tg])
        self.tok_consts = tv

    def bank(self, i):
        t = self.pbank_tok[i]
        return [t] if t is not None else []

    def bank_done(self, i, tok):
        self.pbank_tok[i] = tok

    def barrier(self):
        self.flush()
        vals = [(s, self.cnt[s]) for s in self.sems]
        for e in ("te", "ve", "se", "gp", "sy"):
            own = ENGSEM[e]

            def b(eng, own=own):
                for (s, v) in vals:
                    if s != own and v > 0:
                        eng.wait_ge(self.sems[s], v)
                return None

            self.step(e, b)
        self.flush()


class Ring:

    def __init__(self, c, name, nslot, queues=("sy", "se")):
        self.c = c
        self.name = name
        self.nslot = nslot
        self.queues = queues
        self.i = 0
        for i in range(nslot):
            c.sem(f"{name}_{i}")

    def load(self, fn, waits=()):
        s = f"{self.name}_{self.i % self.nslot}"
        q = self.queues[self.i % len(self.queues)]
        self.i += 1
        return self.c.step(q, fn, waits=waits, sem=s, dma=True)


def emit_ln(c, out_tiles=None, out_dtype=BF16, trange=(0, S), sq_space=None):
    if out_tiles is None:
        out_tiles = c.xl
    xb = c.xl
    t0, t1 = trange
    nsc = (t1 - t0) // SC

    cast_waits = getattr(c, "_x_toks", None)
    cast_tok = [None] * ND
    c._x_toks = None

    with ExitStack() as sctx:
        if sq_space is not None:
            sq = [sq_space[:, 0:SC], sq_space[:, SC:2 * SC]]
        else:
            sq = [sctx.enter_context(c.nc.sbuf_tensor(c.nid("lnsq"), [128, SC], BF16)) for _ in range(2)]
        sq_cons = [None, None]
        prev_trep = None
        for sci in range(nsc):
            sc0 = t0 + sci * SC
            sb_ = sci % 2
            stat_tok = None
            for di in range(ND):
                cw = [cast_waits[di]] if (sci == 0 and cast_waits is not None) else []
                cast_tok[di] = c.step("se", lambda se, di=di, sc0=sc0: se.activation(
                    xb[di][:, sc0:sc0 + SC], c.xr[di][:, sc0:sc0 + SC], AF.Copy), waits=cw)
                w = []
                if sq_cons[di % 2] is not None:
                    w.append(sq_cons[di % 2])
                tsq = c.step("se", lambda se, di=di, sc0=sc0: se.activation(
                    sq[di % 2][:], xb[di][:, sc0:sc0 + SC], AF.Square), waits=w)

                def statfn(te, di=di, sc0=sc0, sb_=sb_):
                    te.matmul(c.pb[sb_][0:1, :], c.ones_bf[:], xb[di][:, sc0:sc0 + SC],
                              start=(di == 0), stop=(di == ND - 1))
                    return te.matmul(c.pb[sb_ + 2][0:1, :], c.ones_bf[:], sq[di % 2][:],
                                     start=(di == 0), stop=(di == ND - 1))

                bw = (c.bank(sb_) + c.bank(sb_ + 2)) if di == 0 else []
                stat_tok = c.step("te", statfn, waits=[tsq] + bw)
                sq_cons[di % 2] = stat_tok

            def rowsfn(ve, sb_=sb_):
                ve.tensor_scalar(c.mean_row[:], c.pb[sb_][0:1, :], 1.0 / DM, None, ALU.mult)
                ve.tensor_scalar(c.srow[0][:], c.pb[sb_ + 2][0:1, :], 1.0 / DM, None, ALU.mult)
                ve.tensor_mul(c.srow[1][:], c.mean_row[:], c.mean_row[:])
                return ve.tensor_sub(c.srow[0][:], c.srow[0][:], c.srow[1][:])

            trows = c.step("ve", rowsfn, waits=[stat_tok])
            c.bank_done(sb_, trows)
            c.bank_done(sb_ + 2, trows)
            tsqrt = c.step("se", lambda se: se.activation(
                c.srow[1][:], c.srow[0][:], AF.Sqrt, bias=c.cc[0:1, 0:1]), waits=[trows])
            trstd = c.step("ve", lambda ve: ve.reciprocal(c.rstd_row[:], c.srow[1][:]),
                           waits=[tsqrt])

            bm, br = 4 + (sci % 2) * 2, 5 + (sci % 2) * 2

            def repfn(te, bm=bm, br=br):
                te.matmul(c.pb[bm][:], c.ones_row[:], c.mean_row[:], start=True, stop=True)
                return te.matmul(c.pb[br][:], c.ones_row[:], c.rstd_row[:], start=True, stop=True)

            trep = c.step("te", repfn, waits=[trstd] + c.bank(bm) + c.bank(br))

            def normfn(ve, sc0=sc0, bm=bm, br=br):
                for di in range(ND):
                    ve.tensor_sub(c.tscr[:], c.xr[di][:, sc0:sc0 + SC], c.pb[bm][:])
                    last = ve.tensor_mul(out_tiles[di][:, sc0:sc0 + SC], c.tscr[:], c.pb[br][:])
                return last

            tnorm = c.step("ve", normfn, waits=[trep])
            c.bank_done(bm, tnorm)
            c.bank_done(br, tnorm)
    c._cast_tok = cast_tok
    c.barrier()


def emit_l0_ffn_quant(c, w):
    nc = c.nc
    ct = c._cast_tok
    wfv = [c.xr[0][:], c.xr[1][:], c.xr[5][:], c.xr[6][:]]
    wfct = [ct[0], ct[1], ct[5], ct[6]]
    x2b = c.xr[2][:].bitcast(BF16)
    wtv = [x2b[:, 0:2048], x2b[:, 2048:4096]]
    dump = c.xr[3][:]
    onesf = c.xr[4][:, 0:128]
    acc = c.xr[4][:, 128:168]
    inv = c.xr[4][:, 168:170]
    ring = Ring(c, "l0q", 4)
    st_ring = Ring(c, "l0qs", 2, queues=("gp",))
    tof = c.step("ve", lambda ve: ve.memset(onesf, 1.0), waits=[ct[4]])
    q_cons = [None, None]
    tq_slot = [None] * 4
    prev_red = None
    for wi, (wsrc, wdst, nrow) in enumerate(((w["w1tf"], w["w1t"], ND),
                                             (w["w2tf"], w["w2t"], NF))):
        ncol = wsrc.shape[1]
        nh = 2 if ncol > 2048 else 1
        cw = min(ncol, 2048)
        nu = nrow * nh
        vts = []
        for u in range(nu):
            r, hc = u // nh, u % nh
            src_ = wsrc[r * 128:(r + 1) * 128, hc * cw:(hc + 1) * cw]
            if u >= 4:
                bw = [vts[u - 4]]
            elif wi == 0:
                bw = [wfct[u]]
            else:
                bw = [t for t in (tq_slot[u % 4],) if t]
            lt = ring.load(lambda e, u=u, src_=src_, cw=cw: e.dma_start(
                out=wfv[u % 4][:, :cw], in_=src_), waits=bw)
            aw = [lt]
            if u == 0 and wi == 1 and prev_red is not None:
                aw.append(prev_red[0])
            vts.append(c.step("se", lambda se, u=u, cw=cw: se.activation(
                dump[:, :cw], wfv[u % 4][:, :cw], AF.Abs,
                accum_out=acc[:, u:u + 1]), waits=aw))
            yield "q" if u == nu - 1 else "a"
        trw = [prev_red[1]] if prev_red is not None else []
        tred = c.step("ve", lambda ve, nu=nu: ve.reduce_sum(
            acc[:, 39:40], acc[:, 0:nu], AX.X), waits=[vts[-1]] + trw)
        tsum = c.step("te", lambda te, wi=wi: te.matmul(
            c.pb[6][:, wi:wi + 1], onesf, acc[:, 39:40], start=True, stop=True),
            waits=[tred, tof] + c.bank(6))
        nel = float(wsrc.shape[0] * ncol)
        tscl0 = c.step("ve", lambda ve, nel=nel, wi=wi: ve.tensor_scalar(
            acc[:, 38:39], c.pb[6][:, wi:wi + 1],
            1.0 / nel, 1e-8, ALU.mult, ALU.max), waits=[tsum])
        c.bank_done(6, tscl0)
        yield "a"
        tinv = c.step("ve", lambda ve, wi=wi: ve.reciprocal(
            inv[:, wi:wi + 1], acc[:, 38:39]), waits=[tscl0])
        prev_red = (tred, tsum)
        yield "a"
        for u in range(nu):
            r, hc = u // nh, u % nh
            src_ = wsrc[r * 128:(r + 1) * 128, hc * cw:(hc + 1) * cw]
            bw = [tq_slot[u % 4]] if tq_slot[u % 4] else []
            if u < 4:
                bw.append(vts[-1])
            lt = ring.load(lambda e, u=u, src_=src_, cw=cw: e.dma_start(
                out=wfv[u % 4][:, :cw], in_=src_), waits=bw)

            def qfn(ve, u=u, wi=wi, cw=cw):
                v = wfv[u % 4][:, :cw]
                t = wtv[u % 2][:, :cw]
                ve.tensor_scalar(v, v, inv[:, wi:wi + 1], None, ALU.mult)
                ve.tensor_scalar(t, v, -0.5, None, ALU.is_le)
                return ve.scalar_tensor_tensor(t, v, 0.5, t, ALU.is_ge, ALU.subtract)

            qw = [q_cons[u % 2]] if q_cons[u % 2] else []
            if u < 2 and wi == 0:
                qw.append(ct[2])
            tq = c.step("ve", qfn, waits=[lt, tinv] + qw)
            tq_slot[u % 4] = tq
            tst = st_ring.load(lambda e, wdst=wdst, r=r, hc=hc, cw=cw, u=u: e.dma_start(
                out=wdst[r * 128:(r + 1) * 128, hc * cw:(hc + 1) * cw],
                in_=wtv[u % 2][:, :cw]), waits=[tq])
            q_cons[u % 2] = tst
            yield "a"
    c._q_prev = (q_cons[0], q_cons[1], tq)


def emit_l0_ffn(c, w, first):
    nc = c.nc
    emit_ln(c)

    with ExitStack() as sctx:
        hT = [sctx.enter_context(nc.sbuf_tensor(c.nid("hT"), [128, 1024], BF16)) for _ in range(NF)]
        wbuf = [sctx.enter_context(nc.sbuf_tensor(c.nid("wst"), [128, 2048], BF16)) for _ in range(2)]
        w2buf = [sctx.enter_context(nc.sbuf_tensor(c.nid("w2st"), [128, FF], BF16)) for _ in range(2)]
        ring = Ring(c, "l0f", 2)

        for half in range(2):
            scs = [2 * half, 2 * half + 1]
            wtoks, cons_of_load = [], [None] * 99
            j = 0
            etoks = []
            for fp in range(NF // 2):
                bw = [etoks[(fp - 2) * 4 + 3]] if fp >= 2 else []
                src = w["w1t"].rearrange("(dp p) f -> p dp f", p=128)[:, :, fp * 256:(fp + 1) * 256]
                lt = ring.load(lambda e, src=src, fp=fp: e.dma_start(
                    out=wbuf[fp % 2][:], in_=src), waits=bw)
                for fo in range(2):
                    fi = fp * 2 + fo
                    ba, bb = j % 6, (j + 1) % 6

                    def mmfn(te, fp=fp, fo=fo, scs=scs, ba=ba, bb=bb):
                        for di in range(ND):
                            wt = wbuf[fp % 2][:, di * 256 + fo * 128: di * 256 + (fo + 1) * 128]
                            te.matmul(c.pb[ba][:], wt,
                                      c.xl[di][:, scs[0] * SC:(scs[0] + 1) * SC],
                                      start=(di == 0), stop=(di == ND - 1), skip_group_check=True)
                            last = te.matmul(c.pb[bb][:], wt,
                                             c.xl[di][:, scs[1] * SC:(scs[1] + 1) * SC],
                                             start=(di == 0), stop=(di == ND - 1), skip_group_check=True)
                        return last

                    mt = c.step("te", mmfn, waits=[lt] + c.bank(ba) + c.bank(bb))
                    for k2, bx in ((0, ba), (1, bb)):
                        et = c.step("se", lambda se, fi=fi, k2=k2, bx=bx: se.activation(
                            hT[fi][:, k2 * SC:(k2 + 1) * SC], c.pb[bx][:],
                            AF.Gelu, bias=w["b1c"][:, fi:fi + 1]), waits=[mt])
                        c.bank_done(bx, et)
                        etoks.append(et)
                    j += 2

            j = 0
            etoks2 = []
            for di in range(ND):
                bw = [etoks2[(di - 2) * 2 + 1]] if di >= 2 else []
                src = w["w2t"].rearrange("(fp p) d -> p fp d", p=128)[:, :, di * 128:(di + 1) * 128]
                lt = ring.load(lambda e, src=src, di=di: e.dma_start(
                    out=w2buf[di % 2][:], in_=src), waits=bw + [etoks[-1]])
                ba, bb = j % 6, (j + 1) % 6

                def mmfn(te, di=di, ba=ba, bb=bb):
                    for fi in range(NF):
                        wt = w2buf[di % 2][:, fi * 128:(fi + 1) * 128]
                        te.matmul(c.pb[ba][:], wt, hT[fi][:, 0:SC],
                                  start=(fi == 0), stop=(fi == NF - 1), skip_group_check=True)
                        last = te.matmul(c.pb[bb][:], wt, hT[fi][:, SC:2 * SC],
                                         start=(fi == 0), stop=(fi == NF - 1), skip_group_check=True)
                    return last

                mt = c.step("te", mmfn, waits=[lt] + c.bank(ba) + c.bank(bb))
                for k2, bx in ((0, ba), (1, bb)):
                    sl = slice(scs[k2] * SC, (scs[k2] + 1) * SC)
                    et = c.step("ve", lambda ve, di=di, sl=sl, bx=bx: ve.scalar_tensor_tensor(
                        c.xr[di][:, sl], c.pb[bx][:], w["b2c"][:, di:di + 1],
                        c.xr[di][:, sl], ALU.add, ALU.add), waits=[mt])
                    c.bank_done(bx, et)
                    etoks2.append(et)
                j += 2
            c.barrier()


def emit_l1_ffn(c, w, xspill):
    nc = c.nc
    tspill = _emit_spill(c, xspill, store=True)
    emit_ln(c)

    with ExitStack() as sctx:
        u = [sctx.enter_context(nc.sbuf_tensor(c.nid("u"), [128, 1024], BF16)) for _ in range(NF)]
        wst = [sctx.enter_context(nc.sbuf_tensor(c.nid("wl1"), [128, FF], BF16)) for _ in range(2)]
        rbuf = [sctx.enter_context(nc.sbuf_tensor(c.nid("rb"), [128, SC], F32)) for _ in range(2)]
        obuf = [sctx.enter_context(nc.sbuf_tensor(c.nid("ob"), [128, SC], F32)) for _ in range(2)]
        ring = Ring(c, "l1f", 3, queues=("gp",))
        rring = Ring(c, "l1r", 2, queues=("sy",))
        oring = Ring(c, "l1o", 2, queues=("sy",))
        bt = []
        for fi in range(NF):
            bt.append(c.xr[fi // 4][:].bitcast(BF16)[:, (fi % 4) * 1024:(fi % 4 + 1) * 1024])

        for half in range(2):
            _l1_stage_in1024(c, ring, wst, w["w1t"], w["s1c"], u, half, mode="silu")
            _l1_stage_in1024(c, ring, wst, w["w2t"], w["s2c"], u, half, mode="mul")
            _l1_stage_in4096(c, ring, wst, w["h1t"], w["h1c"], u, bt, extra=[tspill])
            _l1_stage_in4096(c, ring, wst, w["h2t"], w["h2c"], bt, u)
            _l1_stage_spillout(c, ring, rring, oring, wst, rbuf, obuf,
                               w["w3t"], w["w3c"], u, half, xspill)

    _emit_spill(c, xspill, store=False)
    c.barrier()


def _emit_spill(c, xspill, store):
    qs = ("sy", "se", "gp")
    toks = []
    for di in range(ND):
        def fn(e, di=di, store=store):
            if store:
                return e.dma_start(out=xspill[di * 128:(di + 1) * 128, :], in_=c.xr[di][:])
            return e.dma_start(out=c.xr[di][:], in_=xspill[di * 128:(di + 1) * 128, :])

        toks.append(c.step(qs[di % 3], fn, sem=f"qsp{di % 3}", dma=True))
    return c.step("sy", lambda e: None, waits=toks)


def _l1_stage_in1024(c, ring, wst, wsrc, scol, dst, half, mode):
    etoks = []
    j = 0
    for fp in range(NF // 2):
        bw = [etoks[(fp - 2) * 4 + 3]] if fp >= 2 else []
        src = wsrc.rearrange("(dp p) f -> p dp f", p=128)[:, :, fp * 256:(fp + 1) * 256]
        lt = ring.load(lambda e, src=src, fp=fp: e.dma_start(
            out=wst[fp % 2][:, :2048], in_=src), waits=bw)
        for fo in range(2):
            fi = fp * 2 + fo
            ba, bb = j % 6, (j + 1) % 6

            def mmfn(te, fp=fp, fo=fo, half=half, ba=ba, bb=bb):
                for di in range(ND):
                    wt = wst[fp % 2][:, di * 256 + fo * 128: di * 256 + (fo + 1) * 128]
                    te.matmul(c.pb[ba][:], wt,
                              c.xl[di][:, (half * 2) * SC:(half * 2 + 1) * SC],
                              start=(di == 0), stop=(di == ND - 1), skip_group_check=True)
                    last = te.matmul(c.pb[bb][:], wt,
                                     c.xl[di][:, (half * 2 + 1) * SC:(half * 2 + 2) * SC],
                                     start=(di == 0), stop=(di == ND - 1), skip_group_check=True)
                return last

            mt = c.step("te", mmfn, waits=[lt] + c.bank(ba) + c.bank(bb))
            for k2, bx in ((0, ba), (1, bb)):
                if mode == "silu":
                    et = c.step("se", lambda se, fi=fi, k2=k2, bx=bx: se.activation(
                        dst[fi][:, k2 * SC:(k2 + 1) * SC], c.pb[bx][:],
                        AF.Silu, scale=scol[:, fi:fi + 1]), waits=[mt])
                else:
                    sl = slice(k2 * SC, (k2 + 1) * SC)
                    et = c.step("ve", lambda ve, fi=fi, sl=sl, bx=bx: ve.scalar_tensor_tensor(
                        dst[fi][:, sl], c.pb[bx][:], scol[:, fi:fi + 1],
                        dst[fi][:, sl], ALU.mult, ALU.mult), waits=[mt])
                c.bank_done(bx, et)
                etoks.append(et)
            j += 2
    c.barrier()


def _l1_stage_in4096(c, ring, wst, wsrc, scol, src_t, dst_t, extra=()):
    etoks = []
    for oi in range(NF):
        bw = [etoks[(oi - 2) * 2 + 1]] if oi >= 2 else []
        src = wsrc.rearrange("(fp p) o -> p fp o", p=128)[:, :, oi * 128:(oi + 1) * 128]
        lt = ring.load(lambda e, src=src, oi=oi: e.dma_start(
            out=wst[oi % 2][:], in_=src), waits=bw)
        ba, bb = (2 * oi) % 6, (2 * oi + 1) % 6

        def mmfn(te, oi=oi, ba=ba, bb=bb):
            for fi in range(NF):
                wt = wst[oi % 2][:, fi * 128:(fi + 1) * 128]
                te.matmul(c.pb[ba][:], wt, src_t[fi][:, 0:SC],
                          start=(fi == 0), stop=(fi == NF - 1), skip_group_check=True)
                last = te.matmul(c.pb[bb][:], wt, src_t[fi][:, SC:2 * SC],
                                 start=(fi == 0), stop=(fi == NF - 1), skip_group_check=True)
            return last

        mt = c.step("te", mmfn, waits=[lt] + c.bank(ba) + c.bank(bb))
        for k2, bx in ((0, ba), (1, bb)):
            et = c.step("se", lambda se, oi=oi, k2=k2, bx=bx: se.activation(
                dst_t[oi][:, k2 * SC:(k2 + 1) * SC], c.pb[bx][:],
                AF.Gelu, scale=scol[:, oi:oi + 1]), waits=[mt] + list(extra))
            c.bank_done(bx, et)
            etoks.append(et)
    c.barrier()


def _l1_stage_spillout(c, ring, rring, oring, wst, rbuf, obuf, wsrc, scol, src_t, half, xspill):
    stoks = []
    j = 0
    for di in range(ND):
        bw = [stoks[(di - 2) * 2 + 1]] if di >= 2 else []
        src = wsrc.rearrange("(fp p) d -> p fp d", p=128)[:, :, di * 128:(di + 1) * 128]
        lt = ring.load(lambda e, src=src, di=di: e.dma_start(
            out=wst[di % 2][:], in_=src), waits=bw)
        ba, bb = j % 6, (j + 1) % 6
        rts = []
        for k2 in range(2):
            sc = half * 2 + k2
            sl = slice(sc * SC, (sc + 1) * SC)
            rw = [stoks[j + k2 - 2]] if j + k2 >= 2 else []
            rts.append(rring.load(lambda e, di=di, sl=sl, jj=j + k2: e.dma_start(
                out=rbuf[jj % 2][:], in_=xspill[di * 128:(di + 1) * 128, sl]), waits=rw))

        def mmfn(te, di=di, ba=ba, bb=bb):
            for fi in range(NF):
                wt = wst[di % 2][:, fi * 128:(fi + 1) * 128]
                te.matmul(c.pb[ba][:], wt, src_t[fi][:, 0:SC],
                          start=(fi == 0), stop=(fi == NF - 1), skip_group_check=True)
                last = te.matmul(c.pb[bb][:], wt, src_t[fi][:, SC:2 * SC],
                                 start=(fi == 0), stop=(fi == NF - 1), skip_group_check=True)
            return last

        mt = c.step("te", mmfn, waits=[lt] + c.bank(ba) + c.bank(bb))
        for k2, bx in ((0, ba), (1, bb)):
            sc = half * 2 + k2
            sl = slice(sc * SC, (sc + 1) * SC)
            jj = j + k2
            ow = [stoks[jj - 2]] if jj >= 2 else []
            et = c.step("ve", lambda ve, di=di, jj=jj, bx=bx: ve.scalar_tensor_tensor(
                obuf[jj % 2][:], c.pb[bx][:], scol[:, di:di + 1],
                rbuf[jj % 2][:], ALU.mult, ALU.add), waits=[mt, rts[k2]] + ow)
            c.bank_done(bx, et)
            st = oring.load(lambda e, di=di, sl=sl, jj=jj: e.dma_start(
                out=xspill[di * 128:(di + 1) * 128, sl], in_=obuf[jj % 2][:]), waits=[et])
            stoks.append(st)
        j += 2
    c.barrier()


def emit_l0_attn_b(c, w, xspill):
    nc = c.nc
    H = 16
    ASC = c.cc[:, CCOL[0.03125]:CCOL[0.03125] + 1]

    with ExitStack() as sctx:
        sb = lambda n, shp, dt: sctx.enter_context(nc.sbuf_tensor(c.nid(n), shp, dt))
        qraw = [sb("qraw", [128, S], BF16) for _ in range(ND)]
        lat = [c.xl[4], c.xl[5]]
        vext = [sb("vext", [128, H * 65], BF16) for _ in range(NT)]
        abf = sb("abf", [128, S], BF16)
        bbf = sb("bbf", [128, S], BF16)
        cdiag = sb("cdiag", [128, 128], BF16)
        qav = lambda oi: c.xl[oi // 4][:, (oi % 4) * SC:(oi % 4 + 1) * SC]
        qbv = lambda oi: c.xl[2 + oi // 4][:, (oi % 4) * SC:(oi % 4 + 1) * SC]
        at = [c.xl[5][:, i * 512:(i + 1) * 512] for i in range(4)] + \
             [sb("at4", [128, SC], BF16)]
        a2bc = c.xl[4][:, 1024:1536]
        dscb = c.xl[4][:, 1536:2048]
        tb2 = sb("tb2", [128, 16], F32)
        i16 = sb("i16", [128, 16], I32)
        onorms = [c.xl[7][:].bitcast(F32)[0:65, 512:1024],
                  c.xl[6][:].bitcast(F32)[0:65, 512:1024]]
        dsc = sb("dsc", [128, 128], F32)
        rbc = c.tscr[0:64, 0:SC]
        ao = sb("ao", [128, ND * SC], BF16)
        rbuf = [c.xl[6][:].bitcast(F32)[:, i * SC:(i + 1) * SC] for i in range(2)]
        obuf = [c.xl[7][:].bitcast(F32)[:, i * SC:(i + 1) * SC] for i in range(2)]
        ones65 = sb("on65b", [65, 64], F32)
        stage = [c.xl[4][0:64, 0:SC], c.xl[4][0:64, SC:2 * SC]]
        wkv = [sb("wkv", [128, 1024], BF16) for _ in range(ND)]
        wsm = ao[:, 0:2048]
        wsm2 = wsm
        ring = Ring(c, "ab", 3, queues=("gp",))
        iscr = [c.xl[0][:].bitcast(I32), c.xl[1][:].bitcast(I32)]
        fscr = [c.xl[2][:].bitcast(F32), c.xl[3][:].bitcast(F32)]

        t65 = None
        tvo = []
        for tj in range(NT):
            tvo.append(c.step("ve", lambda ve, tj=tj: ve.memset(
                vext[tj][:].rearrange("p (h e) -> p h e", e=65)[:, :, 64:65], 1.0)))

        kpv = [c.xr[oi // 2][:].bitcast(BF16)[:, (oi % 2) * S:(oi % 2 + 1) * S] for oi in range(ND)]
        kA = [c.xr[4 + oi // 2][:].bitcast(BF16)[:, (oi % 2) * S:(oi % 2 + 1) * S] for oi in range(ND)]
        kB = kpv

        wq_t = [ring.load(lambda e, di=di: e.dma_start(
            out=wkv[di][:], in_=w["Wq"][di * 128:(di + 1) * 128, :])) for di in range(ND)]
        tspill = _emit_spill(c, xspill, store=True)
        emit_ln(c, sq_space=qraw[0][:, 0:2 * SC])

        j = 0
        for oi in range(ND):
            for sc in range(NSC):
                bidx = j % 6

                def mmfn(te, oi=oi, sc=sc, bidx=bidx):
                    for di in range(ND):
                        last = te.matmul(c.pb[bidx][:], wkv[di][:, oi * 128:(oi + 1) * 128],
                                         c.xl[di][:, sc * SC:(sc + 1) * SC],
                                         start=(di == 0), stop=(di == ND - 1))
                    return last

                mt = c.step("te", mmfn, waits=[wq_t[-1]] + c.bank(bidx))
                et = c.step("se", lambda se, oi=oi, sc=sc, bidx=bidx: se.activation(
                    qraw[oi][:, sc * SC:(sc + 1) * SC], c.pb[bidx][:], AF.Copy), waits=[mt])
                c.bank_done(bidx, et)
                j += 1
        c.barrier()

        for pi, (wn, first_) in enumerate((("Wk", True), ("Wv", False))):
            wt_ = [ring.load(lambda e, di=di, wn=wn: e.dma_start(
                out=wkv[di][:], in_=w[wn][di * 128:(di + 1) * 128, :])) for di in range(ND)]
            j = 0
            for oi in range(ND):
                for sc in range(NSC):
                    bidx = j % 6

                    def mmfn(te, oi=oi, sc=sc, bidx=bidx):
                        for di in range(ND):
                            last = te.matmul(c.pb[bidx][:], wkv[di][:, oi * 128:(oi + 1) * 128],
                                             c.xl[di][:, sc * SC:(sc + 1) * SC],
                                             start=(di == 0), stop=(di == ND - 1))
                        return last

                    mt = c.step("te", mmfn, waits=[wt_[-1], tspill] + c.bank(bidx))
                    if first_:
                        et = c.step("se", lambda se, oi=oi, sc=sc, bidx=bidx: se.activation(
                            kpv[oi][:, sc * SC:(sc + 1) * SC], c.pb[bidx][:], AF.Copy), waits=[mt])
                    else:
                        et = c.step("ve", lambda ve, oi=oi, sc=sc, bidx=bidx: ve.tensor_add(
                            kpv[oi][:, sc * SC:(sc + 1) * SC],
                            kpv[oi][:, sc * SC:(sc + 1) * SC],
                            c.pb[bidx][:]), waits=[mt])
                    c.bank_done(bidx, et)
                    j += 1
            c.barrier()

        tio = c.step("gp", lambda gp: (gp.iota(iscr[0][:], [[1, 1024]], base=0, channel_multiplier=0),
                                       gp.iota(iscr[1][:], [[1, 1024]], base=1024,
                                               channel_multiplier=0))[-1])
        tfs = c.step("ve", lambda ve: (ve.tensor_copy(fscr[0][:], iscr[0][:]),
                                       ve.tensor_copy(fscr[1][:], iscr[1][:]))[-1], waits=[tio])

        def tabfn(se):
            se.activation(abf[:, 0:1024], fscr[0][:], AF.Exp, scale=c.cc[:, CCOL[-1e-4]:CCOL[-1e-4] + 1])
            se.activation(abf[:, 1024:2048], fscr[1][:], AF.Exp, scale=c.cc[:, CCOL[-1e-4]:CCOL[-1e-4] + 1])
            se.activation(bbf[:, 0:1024], fscr[0][:], AF.Exp, scale=c.cc[:, CCOL[1e-4]:CCOL[1e-4] + 1])
            return se.activation(bbf[:, 1024:2048], fscr[1][:], AF.Exp, scale=c.cc[:, CCOL[1e-4]:CCOL[1e-4] + 1])

        tab = c.step("se", tabfn, waits=[tfs])
        tio2 = c.step("gp", lambda gp: gp.iota(iscr[0][:, 0:128], [[-1, 128]], base=0,
                                               channel_multiplier=1), waits=[tab])
        tcp2 = c.step("ve", lambda ve: ve.tensor_copy(fscr[0][:, 0:128], iscr[0][:, 0:128]),
                      waits=[tio2])
        tmn = c.step("ve", lambda ve: ve.tensor_scalar(dsc[:, 0:128], fscr[0][:, 0:128], 0.0, None, ALU.min),
                     waits=[tcp2])
        tcd = c.step("se", lambda se: se.activation(cdiag[:], dsc[:, 0:128], AF.Exp,
                                                    scale=c.cc[:, CCOL[2e-4]:CCOL[2e-4] + 1]),
                     waits=[tmn])

        tdn = ring.load(lambda e: e.dma_start(
            out=wsm[:].rearrange("p (d o) -> p d o", o=256),
            in_=w["Wdn"].rearrange("(d p) o -> p d o", p=128)))
        j = 0
        for oi2 in range(2):
            for sc in range(NSC):
                bidx = j % 6

                def mmfn(te, oi2=oi2, sc=sc, bidx=bidx):
                    for di in range(ND):
                        last = te.matmul(c.pb[bidx][:],
                                         wsm[:, di * 256 + oi2 * 128: di * 256 + (oi2 + 1) * 128],
                                         kpv[di][:, sc * SC:(sc + 1) * SC],
                                         start=(di == 0), stop=(di == ND - 1))
                    return last

                mt = c.step("te", mmfn, waits=[tdn] + c.bank(bidx))
                et = c.step("se", lambda se, oi2=oi2, sc=sc, bidx=bidx: se.activation(
                    lat[oi2][:, sc * SC:(sc + 1) * SC], c.pb[bidx][:], AF.Copy), waits=[mt])
                c.bank_done(bidx, et)
                j += 1
        c.barrier()

        tuk = ring.load(lambda e: e.dma_start(
            out=wsm2[:].rearrange("p (l o) -> p l o", o=1024),
            in_=w["Wuk"].rearrange("(l p) o -> p l o", p=128)))
        j = 0
        for oi in range(ND):
            for sc in range(NSC):
                bidx = j % 6
                sl = slice(sc * SC, (sc + 1) * SC)

                def mmfn(te, oi=oi, sc=sc, bidx=bidx):
                    for li in range(2):
                        last = te.matmul(c.pb[bidx][:],
                                         wsm2[:, li * 1024 + oi * 128: li * 1024 + (oi + 1) * 128],
                                         lat[li][:, sc * SC:(sc + 1) * SC],
                                         start=(li == 0), stop=(li == 1))
                    return last

                mt = c.step("te", mmfn, waits=[tuk, tab] + c.bank(bidx))
                ea = c.step("ve", lambda ve, oi=oi, sl=sl, bidx=bidx: ve.tensor_mul(
                    kA[oi][:, sl], c.pb[bidx][:], abf[:, sl]), waits=[mt])
                eb = c.step("ve", lambda ve, oi=oi, sl=sl, bidx=bidx: ve.tensor_mul(
                    kB[oi][:, sl], c.pb[bidx][:], bbf[:, sl]), waits=[ea])
                c.bank_done(bidx, eb)
                j += 1
        c.barrier()

        tuv = ring.load(lambda e: e.dma_start(
            out=wsm[:].rearrange("p (l o) -> p l o", o=1024),
            in_=w["Wuv"].rearrange("(l p) o -> p l o", p=128)))
        j = 0
        for tj in range(NT):
            for vo in range(2):
                bidx = j % 6

                def mmfn(te, tj=tj, vo=vo, bidx=bidx):
                    for li in range(2):
                        last = te.matmul(c.pb[bidx][:],
                                         lat[li][:, tj * 128:(tj + 1) * 128],
                                         wsm[:, li * 1024 + vo * 512: li * 1024 + (vo + 1) * 512],
                                         start=(li == 0), stop=(li == 1))
                    return last

                mt = c.step("te", mmfn, waits=[tuv, tvo[tj]] + c.bank(bidx))
                dst = vext[tj][:].rearrange("p (h e) -> p h e", e=65)[:, vo * 8:(vo + 1) * 8, 0:64]
                et = c.step("ve", lambda ve, dst=dst, bidx=bidx: ve.tensor_copy(
                    dst, c.pb[bidx][:].rearrange("p (h e) -> p h e", e=64)), waits=[mt])
                c.bank_done(bidx, et)
                j += 1
        c.barrier()

        wo_t = [ring.load(lambda e, di=di: e.dma_start(
            out=wkv[di][:], in_=w["Wo"][di * 128:(di + 1) * 128, :])) for di in range(ND)]
        t65 = c.step("ve", lambda ve: ve.memset(ones65[:], 1.0))
        tb2i = c.step("gp", lambda gp: gp.iota(i16[:], [[128, 16]], base=0, channel_multiplier=1))
        tb2f = c.step("ve", lambda ve: ve.tensor_copy(tb2[:], i16[:]), waits=[tb2i])
        tb2t = c.step("se", lambda se: se.activation(tb2[:], tb2[:], AF.Exp,
                                                     scale=c.cc[:, CCOL[2e-4]:CCOL[2e-4] + 1]),
                      waits=[tb2f])
        c.barrier()

        rW = [None, None]
        oW = [None, None]
        for qc in range(NSC):
            qsl = slice(qc * SC, (qc + 1) * SC)
            tq = None
            for oi in range(ND):
                c.step("ve", lambda ve, oi=oi, qsl=qsl: ve.tensor_mul(
                    qav(oi), qraw[oi][:, qsl], abf[:, qsl]))
                tq = c.step("ve", lambda ve, oi=oi, qsl=qsl: ve.tensor_mul(
                    qbv(oi), qraw[oi][:, qsl], bbf[:, qsl]))
            i512 = c.xl[6][:].bitcast(I32)[:, 0:512]
            f512 = c.xl[7][:].bitcast(F32)[:, 0:512]
            a2w = [t for t in (rW[0], oW[0]) if t]
            ta2i = c.step("gp", lambda gp, qc=qc: gp.iota(i512, [[1, 512]],
                                                          base=qc * SC, channel_multiplier=0),
                          waits=a2w)
            ta2f = c.step("ve", lambda ve: ve.tensor_copy(f512, i512), waits=[ta2i])
            ta2 = c.step("se", lambda se: se.activation(a2bc[:], f512, AF.Exp,
                                                        scale=c.cc[:, CCOL[-2e-4]:CCOL[-2e-4] + 1]),
                         waits=[ta2f])

            av_war = [None] * 5
            str_war = None
            last_tn = None
            LAG = 4
            norm_q = []

            def emit_norm(h, av_last):
                hi, hb = h // 2, (h % 2) * 64
                po = 5 + (h % 2)
                onorm = onorms[h % 2]
                onw = []
                if hasattr(c, "_ab_on") and c._ab_on.get(h % 2):
                    onw.append(c._ab_on[h % 2])
                if h < 2:
                    onw += [t for t in (rW[1], oW[1]) if t]
                tev = c.step("ve", lambda ve, po=po, onorm=onorm: ve.tensor_copy(
                    onorm, c.pb[po][0:65, :]), waits=[av_last] + onw)
                c.bank_done(po, tev)
                trep = c.step("te", lambda te, onorm=onorm: te.matmul(
                    c.pb[7][0:64, :], ones65[64:65, :], onorm[64:65, :],
                    start=True, stop=True), waits=[tev, t65] + c.bank(7))
                trc = c.step("ve", lambda ve: ve.reciprocal(rbc, c.pb[7][0:64, :]), waits=[trep])
                c.bank_done(7, trc)
                if hb == 0:
                    tn = c.step("ve", lambda ve, hi=hi, onorm=onorm: ve.tensor_mul(
                        ao[0:64, hi * SC:(hi + 1) * SC], onorm[0:64, :], rbc), waits=[trc])
                else:
                    sgi = (h // 2) % 2
                    sgw = [c._ab_sg[sgi]] if hasattr(c, "_ab_sg") and c._ab_sg.get(sgi) else []
                    tn0 = c.step("ve", lambda ve, sgi=sgi, onorm=onorm: ve.tensor_mul(
                        stage[sgi][:], onorm[0:64, :], rbc), waits=[trc] + sgw)
                    tn = c.step("sy", lambda e, hi=hi, sgi=sgi: e.dma_start(
                        out=ao[64:128, hi * SC:(hi + 1) * SC], in_=stage[sgi][:]),
                        sem="abs%d" % sgi, dma=True, waits=[tn0])
                    if not hasattr(c, "_ab_sg"):
                        c._ab_sg = {}
                    c._ab_sg[sgi] = tn
                if not hasattr(c, "_ab_on"):
                    c._ab_on = {}
                c._ab_on[h % 2] = tn
                return tn

            for h in range(H):
                hi, hb = h // 2, (h % 2) * 64
                po = 5 + (h % 2)
                av_last = None
                ets = [None] * NT
                for i in range(NT + LAG):
                    if i < NT:
                        kt = i
                        lo = kt * 128
                        slot = kt % 5
                        pw = c.pb[slot]
                        ab = at[kt % 5]
                        if lo >= qc * SC + SC:
                            kind = "below"
                        elif lo + 128 <= qc * SC:
                            kind = "above"
                        else:
                            kind = "straddle"
                        kv, qv = (kB, qav) if kind == "above" else (kA, qbv)
                        mt = c.step("te", lambda te, kv=kv, qv=qv, hi=hi, hb=hb, kt=kt, pw=pw:
                                    te.matmul(pw, kv[hi][hb:hb + 64, kt * 128:(kt + 1) * 128],
                                              qv(hi)[hb:hb + 64, :], start=True, stop=True),
                                    waits=[tq, ta2] + c.bank(slot))
                        eww = [av_war[kt % 5]] if av_war[kt % 5] else []
                        if kind != "straddle":
                            et = c.step("se", lambda se, ab=ab, pw=pw: se.activation(
                                ab[:], pw, AF.Exp, scale=ASC), waits=[mt] + eww)
                            c.bank_done(slot, et)
                        else:
                            dcol = lo - qc * SC
                            sw = [str_war] if str_war else []

                            def fixfn(ve, pw=pw, dcol=dcol, kt=kt):
                                last = ve.tensor_mul(dscb[:, dcol:dcol + 128],
                                                     pw[:, dcol:dcol + 128], cdiag[:])
                                if dcol + 128 < SC:
                                    last = ve.scalar_tensor_tensor(
                                        dscb[:, dcol + 128:SC], pw[:, dcol + 128:SC],
                                        tb2[:, kt:kt + 1], a2bc[:, dcol + 128:SC],
                                        ALU.mult, ALU.mult)
                                return last

                            tfix = c.step("ve", fixfn, waits=[mt, tcd, tb2t, ta2] + sw)

                            def epi(se, ab=ab, pw=pw, dcol=dcol):
                                if dcol > 0:
                                    se.activation(ab[:, 0:dcol], pw[:, 0:dcol], AF.Exp, scale=ASC)
                                return se.activation(ab[:, dcol:SC], dscb[:, dcol:SC],
                                                     AF.Exp, scale=ASC)

                            et = c.step("se", epi, waits=[tfix] + eww)
                            str_war = et
                            c.bank_done(slot, et)
                        ets[kt] = et
                    if i == 1 and h > 0:
                        norm_q.append(emit_norm(h - 1, prev_av))
                    if i >= LAG:
                        kt = i - LAG
                        ab = at[kt % 5]
                        avw = [ets[kt]] + (c.bank(po) if kt == 0 else [])
                        av_last = c.step("te", lambda te, kt=kt, h=h, ab=ab, po=po: te.matmul(
                            c.pb[po][0:65, :], vext[kt][:, h * 65:(h + 1) * 65], ab[:],
                            start=(kt == 0), stop=(kt == NT - 1)), waits=avw)
                        av_war[kt % 5] = av_last
                prev_av = av_last
            last_tn = emit_norm(H - 1, prev_av)

            aow_toks = [last_tn] + [t for t in (c._ab_sg.get(0), c._ab_sg.get(1)) if t]
            for di in range(ND):
                bidx = di % 4
                rw = [rW[di % 2]] if rW[di % 2] else []
                rt = c.step("sy", lambda e, di=di, qsl=qsl: e.dma_start(
                    out=rbuf[di % 2][:], in_=xspill[di * 128:(di + 1) * 128, qsl]),
                    sem="abr%d" % (di % 2), dma=True, waits=rw)

                def wofn(te, di=di, bidx=bidx):
                    for ii in range(ND):
                        last = te.matmul(c.pb[bidx][:], wkv[ii][:, di * 128:(di + 1) * 128],
                                         ao[:, ii * SC:(ii + 1) * SC],
                                         start=(ii == 0), stop=(ii == ND - 1))
                    return last

                mt = c.step("te", wofn, waits=aow_toks + [wo_t[-1]] + c.bank(bidx))
                ow = [oW[di % 2]] if oW[di % 2] else []
                ot = c.step("ve", lambda ve, di=di, bidx=bidx: ve.tensor_add(
                    obuf[di % 2][:], c.pb[bidx][:], rbuf[di % 2][:]), waits=[mt, rt] + ow)
                c.bank_done(bidx, ot)
                rW[di % 2] = ot
                st = c.step("sy", lambda e, di=di, qsl=qsl: e.dma_start(
                    out=xspill[di * 128:(di + 1) * 128, qsl], in_=obuf[di % 2][:]),
                    sem="abo%d" % (di % 2), dma=True, waits=[ot])
                oW[di % 2] = st
                aow_toks = aow_toks[:1]
            c.barrier()
            c._ab_on = {}
            c._ab_sg = {}


    _emit_spill(c, xspill, store=False)
    c.barrier()


def emit_l0_attn_a(c, w, xspill, xin, after_ln=None):
    nc = c.nc
    H = 16
    with ExitStack() as sctx:
        sb = lambda n, shp, dt: sctx.enter_context(nc.sbuf_tensor(c.nid(n), shp, dt))
        wqsb = [sb("wq", [128, 1024], BF16) for _ in range(ND)]
        wkvsb = [sb("wkv", [128, 2048], BF16) for _ in range(ND)]
        wosb = [sb("wo", [128, 1024], BF16) for _ in range(2)]
        qsc = sb("qsc", [128, ND * SC], BF16)
        ksc = sb("ksc", [128, ND * SC], BF16)
        vsc = [sb("vsc", [128, H * 65], BF16) for _ in range(4)]
        kprev = sb("kpv", [128, 1024], BF16)
        vprev = sb("vpv", [128, H * 65], BF16)
        kz = sb("kz", [128, 1024], BF16)
        vz = sb("vz", [128, H * 65], BF16)
        atb = [sb("at", [128, 512], BF16) for _ in range(3)]
        cmask = sb("cm", [128, 128], BF16)
        cmi = sb("cmi", [128, 128], I32)
        aowsc = sb("aow", [128, ND * SC], BF16)
        stg = [sb("stg", [64, 128], BF16) for _ in range(2)]
        aonm = [sb("aonm", [65, 128], F32) for _ in range(2)]
        on65 = sb("on65", [65, 64], F32)
        ring = Ring(c, "aa", 3, queues=("gp",))

        for di in range(ND):
            ring.load(lambda e, di=di: e.dma_start(out=wqsb[di][:], in_=w["wq"][di * 128:(di + 1) * 128, :]))
            ring.load(lambda e, di=di: e.dma_start(out=wkvsb[di][:], in_=w["wkv"][di * 128:(di + 1) * 128, :]))
        emit_ln(c, sq_space=ksc[:, 0:2 * SC])
        cast_tok = c._cast_tok
        c._qgen = after_ln() if after_ln is not None else None

        c._qtag = "a"

        def qdrive(n=1, quiet=False):
            g = c._qgen
            if g is None:
                return
            try:
                for _ in range(n):
                    if c._qtag == "q" and not quiet:
                        return
                    c._qtag = next(g)
            except StopIteration:
                c._qgen = None

        x7 = c.xr[7][:]
        arb = [x7[:, 0:512], x7[:, 512:1024]]
        aob = [x7[:, 1024:1536], x7[:, 1536:2048]]
        rW = {0: cast_tok[7], 1: cast_tok[7]}
        oW = {0: None, 1: None}

        def initfn(ve):
            ve.memset(kz[:], 0.0)
            ve.memset(vz[:], 0.0)
            ve.memset(vz[:].rearrange("p (h e) -> p h e", e=65)[:, :, 64:65], 1.0)
            ve.memset(vprev[:].rearrange("p (h e) -> p h e", e=65)[:, :, 64:65], 1.0)
            for i in range(4):
                ve.memset(vsc[i][:].rearrange("p (h e) -> p h e", e=65)[:, :, 64:65], 1.0)
            return ve.memset(on65[:], 1.0)

        c.step("ve", initfn)
        tcmi = c.step("gp", lambda gp: gp.iota(cmi[:], [[1, 128]], base=0, channel_multiplier=-1))
        c.step("ve", lambda ve: ve.tensor_scalar(cmask[:], cmi[:], 0.0, None, ALU.is_ge),
               waits=[tcmi])
        c.barrier()
        qdrive(4, quiet=True)

        for scn in range(NSC):
            csl = slice(scn * SC, (scn + 1) * SC)
            for kind in range(2):
                dst = qsc if kind == 0 else ksc
                for oi in range(ND):
                    bidx = oi % 6

                    def mmfn(te, oi=oi, csl=csl, kind=kind, bidx=bidx):
                        for di in range(ND):
                            src_ = wqsb[di][:, oi * 128:(oi + 1) * 128] if kind == 0 \
                                else wkvsb[di][:, oi * 128:(oi + 1) * 128]
                            last = te.matmul(c.pb[bidx][:], src_, c.xl[di][:, csl],
                                             start=(di == 0), stop=(di == ND - 1))
                        return last

                    mt = c.step("te", mmfn, waits=c.bank(bidx))
                    et = c.step("se", lambda se, oi=oi, dst=dst, bidx=bidx: se.activation(
                        dst[:, oi * SC:(oi + 1) * SC], c.pb[bidx][:], AF.Copy), waits=[mt])
                    c.bank_done(bidx, et)
                    qdrive()
            for wj in range(4):
                for vo in range(2):
                    bidx = 6 + vo

                    def mmfn(te, wj=wj, vo=vo, scn=scn, bidx=bidx):
                        wsl = slice(scn * SC + wj * 128, scn * SC + (wj + 1) * 128)
                        for di in range(ND):
                            last = te.matmul(c.pb[bidx][:], c.xl[di][:, wsl],
                                             wkvsb[di][:, 1024 + vo * 512:1024 + (vo + 1) * 512],
                                             start=(di == 0), stop=(di == ND - 1))
                        return last

                    mt = c.step("te", mmfn, waits=c.bank(bidx))
                    dst = vsc[wj][:].rearrange("p (h e) -> p h e", e=65)[:, vo * 8:(vo + 1) * 8, 0:64]
                    et = c.step("ve", lambda ve, dst=dst, bidx=bidx: ve.tensor_copy(
                        dst, c.pb[bidx][:].rearrange("p (h e) -> p h e", e=64)), waits=[mt])
                    c.bank_done(bidx, et)
                    qdrive()
            c.barrier()

            sgW = {}
            for wj in range(4):
                qdrive(2)
                w_ = scn * 4 + wj
                kp = kz if w_ == 0 else (kprev if wj == 0 else None)
                vp = vz if w_ == 0 else (vprev if wj == 0 else None)
                LAG2 = 2
                ets = [None] * (H // 2)
                avpairs = [None] * (H // 2)
                for i in range(H // 2 + LAG2 + 1):
                    if i < H // 2:
                        hp = i
                        h0 = hp * 2
                        b0 = (hp * 2) % 6
                        b1 = (hp * 2 + 1) % 6
                        slot = hp % 3

                        def smm(te, wj=wj, h0=h0, b0=b0, b1=b1, kp=kp):
                            last = None
                            for j2 in range(2):
                                h = h0 + j2
                                hi, hb = h // 2, (h % 2) * 64
                                bx = b0 if j2 == 0 else b1
                                kpap = kp[hb:hb + 64, hi * 128:(hi + 1) * 128] if kp is not None \
                                    else ksc[hb:hb + 64, hi * SC + (wj - 1) * 128:hi * SC + wj * 128]
                                te.matmul(c.pb[bx][:, 0:128], kpap,
                                          qsc[hb:hb + 64, hi * SC + wj * 128:hi * SC + (wj + 1) * 128],
                                          start=True, stop=True)
                                last = te.matmul(c.pb[bx][:, 128:256],
                                                 ksc[hb:hb + 64, hi * SC + wj * 128:hi * SC + (wj + 1) * 128],
                                                 qsc[hb:hb + 64, hi * SC + wj * 128:hi * SC + (wj + 1) * 128],
                                                 start=True, stop=True)
                            return last

                        mt = c.step("te", smm, waits=c.bank(b0) + c.bank(b1))
                        eww = sgW.get(("at", slot))

                        def expfn(se, slot=slot, b0=b0, b1=b1):
                            se.activation(atb[slot][:, 0:256], c.pb[b0][:, 0:256], AF.Exp,
                                          scale=c.cc[:, CCOL[0.125]:CCOL[0.125] + 1])
                            return se.activation(atb[slot][:, 256:512], c.pb[b1][:, 0:256], AF.Exp,
                                                 scale=c.cc[:, CCOL[0.125]:CCOL[0.125] + 1])

                        et = c.step("se", expfn, waits=[mt] + ([eww] if eww else []))
                        c.bank_done(b0, et)
                        c.bank_done(b1, et)

                        def maskfn(ve, slot=slot):
                            ve.tensor_mul(atb[slot][:, 128:256], atb[slot][:, 128:256], cmask[:])
                            return ve.tensor_mul(atb[slot][:, 384:512], atb[slot][:, 384:512], cmask[:])

                        ets[hp] = c.step("ve", maskfn, waits=[et])
                    if LAG2 + 1 <= i:
                        hp = i - LAG2 - 1
                        if hp < H // 2:
                            avs = avpairs[hp]
                            h0 = hp * 2
                            trc_prev = sgW.get(("trc",))
                            for j2 in range(2):
                                h = h0 + j2
                                hi, hb = h // 2, (h % 2) * 64
                                po = 6 + j2
                                tev = c.step("ve", lambda ve, po=po, j2=j2: ve.tensor_copy(
                                    aonm[j2][:], c.pb[po][0:65, 0:128]),
                                    waits=[avs[j2]] + ([sgW.get(("nm", j2))] if sgW.get(("nm", j2)) else []))
                                c.bank_done(po, tev)
                                trep = c.step("te", lambda te, j2=j2: te.matmul(
                                    c.pb[5][0:64, j2 * 128:(j2 + 1) * 128], on65[64:65, :],
                                    aonm[j2][64:65, :], start=True, stop=True),
                                    waits=[tev] + (c.bank(5) if trc_prev is None else [trc_prev]))
                                trc = c.step("ve", lambda ve, j2=j2: ve.reciprocal(
                                    c.tscr[0:64, j2 * 128:(j2 + 1) * 128],
                                    c.pb[5][0:64, j2 * 128:(j2 + 1) * 128]), waits=[trep])
                                trc_prev = trc
                                sgW[("trc",)] = trc
                                c.bank_done(5, trc)
                                if hb == 0:
                                    tn = c.step("ve", lambda ve, hi=hi, wj=wj, j2=j2: ve.tensor_mul(
                                        aowsc[0:64, hi * SC + wj * 128:hi * SC + (wj + 1) * 128],
                                        aonm[j2][0:64, :], c.tscr[0:64, j2 * 128:(j2 + 1) * 128]),
                                        waits=[trc])
                                else:
                                    sgi = j2
                                    sgw = sgW.get(("sg", sgi))
                                    tn0 = c.step("ve", lambda ve, sgi=sgi, j2=j2: ve.tensor_mul(
                                        stg[sgi][:], aonm[j2][0:64, :],
                                        c.tscr[0:64, j2 * 128:(j2 + 1) * 128]),
                                        waits=[trc] + ([sgw] if sgw else []))
                                    tn = c.step("sy", lambda e, hi=hi, wj=wj, sgi=sgi: e.dma_start(
                                        out=aowsc[64:128, hi * SC + wj * 128:hi * SC + (wj + 1) * 128],
                                        in_=stg[sgi][:]), sem="aas%d" % sgi, dma=True, waits=[tn0])
                                    sgW[("sg", sgi)] = tn
                                sgW[("nm", j2)] = tn
                                sgW[("tn",)] = tn

                    if LAG2 <= i < H // 2 + LAG2:
                        hp = i - LAG2
                        h0 = hp * 2
                        slot = hp % 3
                        avs = []
                        for j2 in range(2):
                            h = h0 + j2
                            po = 6 + j2

                            def avfn(te, wj=wj, h=h, j2=j2, slot=slot, po=po, vp=vp):
                                vpap = vp[:, h * 65:(h + 1) * 65] if vp is not None \
                                    else vsc[wj - 1][:, h * 65:(h + 1) * 65]
                                te.matmul(c.pb[po][0:65, 0:128], vpap,
                                          atb[slot][:, j2 * 256:j2 * 256 + 128],
                                          start=True, stop=False)
                                return te.matmul(c.pb[po][0:65, 0:128], vsc[wj][:, h * 65:(h + 1) * 65],
                                                 atb[slot][:, j2 * 256 + 128:j2 * 256 + 256],
                                                 start=False, stop=True)

                            avs.append(c.step("te", avfn, waits=[ets[hp]] + c.bank(po)))
                        sgW[("at", slot)] = avs[-1]
                        avpairs[hp] = avs
            if scn < NSC - 1:
                tk = c.step("sy", lambda e, scn=scn: e.dma_start(
                    out=kprev[:], in_=ksc[:].rearrange("p (o w) -> p o w", w=SC)[:, :, 384:512]),
                    sem="aakp", dma=True, waits=[sgW[("tn",)]])
                tv2 = c.step("sy", lambda e: e.dma_start(out=vprev[:], in_=vsc[3][:]),
                             sem="aakp", dma=True)
            aow_toks = [t for kk, t in sgW.items() if kk[0] in ("tn", "sg") and t]
            mt = None
            for ii in range(ND):
                wt_ = ring.load(lambda e, ii=ii: e.dma_start(
                    out=wosb[ii % 2][:], in_=w["wo"][ii * 128:(ii + 1) * 128, :]),
                    waits=[mt] if ii >= 2 else [])

                def wofn(te, ii=ii):
                    for di in range(ND):
                        last = te.matmul(c.pb[di][:], wosb[ii % 2][:, di * 128:(di + 1) * 128],
                                         aowsc[:, ii * SC:(ii + 1) * SC],
                                         start=(ii == 0), stop=(ii == ND - 1))
                    return last

                bw = sum((c.bank(di) for di in range(ND)), []) if ii == 0 else []
                mt = c.step("te", wofn, waits=[wt_] + aow_toks + bw)
                aow_toks = []
                qdrive()
            for di in range(ND):
                j = di % 2
                rt = c.step("sy", lambda e, di=di, csl=csl, j=j: e.dma_start(
                    out=arb[j], in_=xin[di * 128:(di + 1) * 128, csl]),
                    sem="aar%d" % j, dma=True, waits=[t for t in (rW[j],) if t])
                ot = c.step("ve", lambda ve, di=di, j=j: ve.scalar_tensor_tensor(
                    aob[j], c.pb[di][:], w["boc"][:, di:di + 1], arb[j],
                    ALU.add, ALU.add), waits=[mt, rt] + ([oW[j]] if oW[j] else []))
                c.bank_done(di, ot)
                rW[j] = ot
                st = c.step("sy", lambda e, di=di, csl=csl, j=j: e.dma_start(
                    out=xspill[di * 128:(di + 1) * 128, csl], in_=aob[j]),
                    sem="aao%d" % j, dma=True, waits=[ot])
                oW[j] = st
            c.barrier()
            qdrive(4, quiet=True)
        while c._qgen is not None:
            qdrive(50, quiet=True)
    c.barrier()
    _emit_spill(c, xspill, store=False)
    c.barrier()


def emit_l1_attn_a(c, w, xtok):
    nc = c.nc
    H, POS, SS = 8, 16, 33
    with ExitStack() as sctx:
        sb = lambda n, shp, dt: sctx.enter_context(nc.sbuf_tensor(c.nid(n), shp, dt))
        wwv = sb("wwv", [128, ND * 24], BF16)
        wqr = sb("wqr", [128, ND * 128], BF16)
        wse1 = sb("ws1", [128, ND * 256], BF16)
        wse2 = sb("ws2", [128, 2 * 1024], BF16)
        wout = [sb("wo", [128, 1024], BF16) for _ in range(ND)]
        kwcol = sb("kwc", [128, 8], BF16)
        wp = sb("wp", [24, S], F32)
        tB = sb("tb", [24, S], F32)
        qT = sb("qt", [128, S], BF16)
        meta = sb("meta", [32, S], F32)
        mtok = [sb("mtok", [128, 32], F32) for _ in range(2)]
        qk8 = sb("qk8", [8, SC], F32)
        sel = sb("sel", [24, 4], F32)
        seli = sb("seli", [24, 4], I32)
        stok = [sb("stok", [128, 128], BF16) for _ in range(2)]
        stride_bc = sb("stb", [128, SS], F32)
        strabs = sb("sta", [128, SS], F32)
        colsc = sb("colsc", [128, 8], F32)
        coli = sb("coli", [128, 1], I32)
        sc_all = sb("sca", [128, H * SS], F32)
        env = sb("env", [128, SS], F32)
        mval = sb("mva", [128, SS], F32)
        mbc = sb("mbc", [128, H * SS], F32)
        mu8 = sb("mu8", [128, H * SS], U8)
        negt = sb("neg", [128, H * SS], F32)
        mx8 = sb("mx8", [128, H], F32)
        attnb = [sb("attn", [128, H * SS], F32) for _ in range(2)]
        spf = sb("spf", [128, SS], F32)
        spc = sb("spc", [128, SS], F32)
        idxb = [sb("idxi", [128, SS], I32) for _ in range(2)]
        G = [sb("G", [128, 1024], BF16) for _ in range(4)]
        gtmp = sb("gtmp", [128, 256], F32)
        acc = sb("acc", [128, 1024], F32)
        tmul = [sb("tmul", [128, 1024], BF16) for _ in range(2)]
        identb = sb("idb", [128, 128], BF16)
        otT = sb("otT", [128, 1024], BF16)
        zT = sb("zT", [128, 2 * 128], BF16)
        seT = sb("seT", [128, ND * 128], BF16)
        gT = sb("gT", [128, ND * 128], BF16)
        sef = sb("sef", [128, 256], F32)
        ring = Ring(c, "wa", 3, queues=("gp",))
        gring = Ring(c, "ga", 4, queues=("gp",))
        tring = Ring(c, "tr", 2, queues=("sy",))
        sring = Ring(c, "st", 2, queues=("se",))

        lt = []
        lt.append(ring.load(lambda e: e.dma_start(
            out=wwv[:].rearrange("p (d o) -> p d o", o=24),
            in_=w["w_wave"].rearrange("(d p) o -> p d o", p=128))))
        lt.append(ring.load(lambda e: e.dma_start(
            out=wqr[:].rearrange("p (d o) -> p d o", o=128),
            in_=w["w_query"].rearrange("(d p) o -> p d o", p=128))))
        lt.append(ring.load(lambda e: e.dma_start(
            out=wse1[:].rearrange("p (d o) -> p d o", o=256),
            in_=w["w_se1"].rearrange("(d p) o -> p d o", p=128))))
        lt.append(ring.load(lambda e: e.dma_start(
            out=wse2[:].rearrange("p (l o) -> p l o", o=1024),
            in_=w["w_se2"].rearrange("(l p) o -> p l o", p=128))))
        for di in range(ND):
            lt.append(ring.load(lambda e, di=di: e.dma_start(
                out=wout[di][:], in_=w["w_out"][di * 128:(di + 1) * 128, :])))
        emit_ln(c, sq_space=G[0][:])
        tkz = c.step("ve", lambda ve: ve.memset(kwcol[:], 0.0))
        tkw = []
        for h in range(H):
            tkw.append(c.step("gp", lambda e, h=h: e.dma_start(
                out=kwcol[h * 16:(h + 1) * 16, h:h + 1], in_=w["key_weight"][:, None],
                allow_slow_non_contiguous=True), sem="kwl", dma=True, waits=[tkz]))
        tng = c.step("ve", lambda ve: ve.memset(negt[:], -1e30))
        c.step("ve", lambda ve: ve.tensor_scalar(identb[:], c.identi[:], 0, None,
                                                 ALU.is_equal), waits=[c.tok_ident])

        tsel = []
        for jcol, off in ((0, 0), (1, 16), (2, 8)):
            ti = c.step("gp", lambda gp, jcol=jcol, off=off: gp.iota(
                seli[:, jcol:jcol + 1], [[0, 1]], base=-off, channel_multiplier=1))
            t1 = c.step("ve", lambda ve, jcol=jcol: ve.tensor_copy(
                sel[:, jcol:jcol + 1], seli[:, jcol:jcol + 1]), waits=[ti])
            t2 = c.step("ve", lambda ve, jcol=jcol: ve.tensor_scalar(
                sel[:, jcol:jcol + 1], sel[:, jcol:jcol + 1], 2.0, -7.0, ALU.mult, ALU.add),
                waits=[t1])
            t3 = c.step("ve", lambda ve, jcol=jcol: ve.tensor_mul(
                sel[:, 3:4], sel[:, jcol:jcol + 1], sel[:, jcol:jcol + 1]), waits=[t2])
            t4 = c.step("ve", lambda ve, jcol=jcol: ve.tensor_scalar(
                sel[:, jcol:jcol + 1], sel[:, 3:4], 49.5, None, ALU.is_le), waits=[t3])
            tsel.append(t4)

        ti = c.step("gp", lambda gp: gp.iota(idxb[0][:, 0:SS], [[1, SS]], base=-16,
                                             channel_multiplier=0))
        t1 = c.step("ve", lambda ve: ve.tensor_copy(stride_bc[:], idxb[0][:, 0:SS]), waits=[ti])
        t2 = c.step("ve", lambda ve: ve.tensor_scalar(strabs[:], stride_bc[:], -1.0, None, ALU.mult),
                    waits=[t1])
        tstr = c.step("ve", lambda ve: ve.tensor_max(strabs[:], strabs[:], stride_bc[:]), waits=[t2])
        c.barrier()

        ttok = []
        for tj in range(NT):
            for di in range(ND):
                k = tj * ND + di
                bw = [ttok[k - 2]] if k >= 2 else []
                t1 = tring.load(lambda e, tj=tj, di=di, k=k: e.dma_start(
                    out=stok[k % 2][:], in_=c.xl[di][:, tj * 128:(tj + 1) * 128],
                    transpose=True), waits=bw)
                t2 = sring.load(lambda e, tj=tj, di=di, k=k: e.dma_start(
                    out=xtok[tj * 128:(tj + 1) * 128, di * 128:(di + 1) * 128],
                    in_=stok[k % 2][:]), waits=[t1])
                ttok.append(t2)
        c.barrier()

        for sc in range(NSC):
            ssl = slice(sc * SC, (sc + 1) * SC)
            mt = c.step("te", lambda te, ssl=ssl: [te.matmul(
                c.pb[0][0:24, :], wwv[:, di * 24:(di + 1) * 24], c.xl[di][:, ssl],
                start=(di == 0), stop=(di == ND - 1)) for di in range(ND)][-1],
                waits=c.bank(0))
            et = c.step("se", lambda se, ssl=ssl: se.activation(
                wp[:, ssl], c.pb[0][0:24, :], AF.Silu, bias=w["bwvc"][:]), waits=[mt])
            c.bank_done(0, et)
            mt2 = c.step("te", lambda te, ssl=ssl: [te.matmul(
                c.pb[1][:], wqr[:, di * 128:(di + 1) * 128], c.xl[di][:, ssl],
                start=(di == 0), stop=(di == ND - 1)) for di in range(ND)][-1],
                waits=c.bank(1))
            et2 = c.step("se", lambda se, ssl=ssl: se.activation(
                qT[:, ssl], c.pb[1][:], AF.Silu, bias=w["bqc"][:]), waits=[mt2])
            c.bank_done(1, et2)
        tA = c.step("se", lambda se: se.activation(tB[:], wp[:], AF.Tanh))
        tS = c.step("se", lambda se: se.activation(wp[:], wp[:], AF.Sigmoid))

        for sc in range(NSC):
            ssl = slice(sc * SC, (sc + 1) * SC)
            mt = c.step("te", lambda te, ssl=ssl: (
                te.matmul(c.pb[2][0:1, :], sel[:, 0:1], wp[0:24, ssl], start=True, stop=True),
                te.matmul(c.pb[3][0:1, :], sel[:, 1:2], wp[0:24, ssl], start=True, stop=True),
                te.matmul(c.pb[4][0:1, :], sel[:, 2:3], tB[0:24, ssl], start=True, stop=True),
                te.matmul(c.pb[5][0:8, :], kwcol[:], qT[:, ssl], start=True, stop=True))[-1],
                waits=[tS, tA] + tsel + c.bank(2) + c.bank(3) + c.bank(4) + c.bank(5))
            tv = c.step("ve", lambda ve: (
                ve.tensor_scalar(c.srow[0][:], c.pb[2][0:1, :], 15.0 / 8, 1.0, ALU.mult, ALU.add),
                ve.tensor_scalar(c.srow[1][:], c.pb[3][0:1, :], 9.5 / 8, 0.5, ALU.mult, ALU.add),
                ve.tensor_scalar(c.mean_row[:], c.pb[4][0:1, :], 16.0 / 8, None, ALU.mult),
                ve.tensor_copy(qk8[:], c.pb[5][0:8, :]))[-1], waits=[mt])
            for b_ in (2, 3, 4, 5):
                c.bank_done(b_, tv)
            td0 = c.step("sy", lambda e, ssl=ssl: e.dma_start(out=meta[0:1, ssl], in_=c.srow[0][:]),
                         sem="mrow", dma=True, waits=[tv])
            td1 = c.step("sy", lambda e, ssl=ssl: e.dma_start(out=meta[2:3, ssl], in_=c.srow[1][:]),
                         sem="mrow", dma=True, waits=[tv])
            td2 = c.step("sy", lambda e, ssl=ssl: e.dma_start(out=meta[1:2, ssl], in_=c.mean_row[:]),
                         sem="mrow", dma=True, waits=[tv])
            td3 = c.step("sy", lambda e, ssl=ssl: e.dma_start(out=meta[3:11, ssl], in_=qk8[:]),
                         sem="mrow", dma=True, waits=[tv])
        c.barrier()

        mtokW = [None, None]
        gW = [None] * 4
        tmW = [None, None]
        idxW = [None, None]
        attnW = [None, None]
        X = {"te4": None, "tx3": None, "accR": None, "otTR": None, "zTR": None,
             "tgR": None, "gTR": None, "exprev": None, "tc1": None}

        def emitA(tj):
            ix = idxb[tj % 2]
            at = attnb[tj % 2]
            mw = [mtokW[tj % 2]] if mtokW[tj % 2] else []
            mt = c.step("te", lambda te, tj=tj: te.transpose(
                c.pb[0][:, 0:32], meta[:, tj * 128:(tj + 1) * 128], c.ident[0:32, 0:32]),
                waits=c.bank(0) + mw)
            tmt = c.step("ve", lambda ve, tj=tj: ve.tensor_copy(mtok[tj % 2][:], c.pb[0][:, 0:32]),
                         waits=[mt])
            c.bank_done(0, tmt)
            MT = mtok[tj % 2][:]
            pv = [X["te4"]] if X["te4"] else []
            tc0 = c.step("gp", lambda gp, tj=tj: gp.iota(coli[:], [[0, 1]], base=tj * 128,
                                                         channel_multiplier=1),
                         waits=[X["tc1"]] if X["tc1"] else [])
            tc1 = c.step("ve", lambda ve: ve.tensor_copy(colsc[:, 0:1], coli[:]),
                         waits=[tc0] + pv)
            X["tc1"] = tc1
            tc2 = c.step("ve", lambda ve, MT=MT: ve.tensor_add(colsc[:, 1:2], colsc[:, 0:1],
                                                               MT[:, 1:2]), waits=[tc1, tmt])
            ts1 = c.step("ve", lambda ve, MT=MT: ve.tensor_scalar(spf[:], stride_bc[:], MT[:, 0:1],
                                                                  None, ALU.mult),
                         waits=[tmt, tstr] + pv)
            ts2 = c.step("ve", lambda ve: ve.tensor_scalar(spf[:], spf[:], colsc[:, 1:2], None,
                                                           ALU.add), waits=[ts1, tc2])
            tm1 = c.step("ve", lambda ve: ve.tensor_scalar(mval[:], spf[:], 0.0, None, ALU.is_ge),
                         waits=[ts2])
            tm2 = c.step("ve", lambda ve: ve.tensor_scalar(spc[:], spf[:], 2048.0, None, ALU.is_lt),
                         waits=[ts2] + pv)
            tm3 = c.step("ve", lambda ve: ve.tensor_mul(mval[:], mval[:], spc[:]), waits=[tm1, tm2])
            tq1 = c.step("ve", lambda ve: ve.tensor_scalar(spc[:], spf[:], 0.0, 2047.0, ALU.max,
                                                           ALU.min), waits=[tm3])
            iw = idxW[tj % 2] or []
            tq2 = c.step("ve", lambda ve, ix=ix: ve.tensor_copy(ix[:], spc[:]), waits=[tq1] + iw)
            tq3 = c.step("ve", lambda ve, ix=ix: ve.tensor_copy(spf[:], ix[:]), waits=[tq2])
            tq4 = c.step("ve", lambda ve: ve.tensor_tensor(mbc[:, 0:SS], spf[:], spc[:], ALU.is_gt),
                         waits=[tq3])
            tq5 = c.step("ve", lambda ve: ve.tensor_sub(spf[:], spf[:], mbc[:, 0:SS]), waits=[tq4])
            tidx = c.step("ve", lambda ve, ix=ix: ve.tensor_copy(ix[:], spf[:]), waits=[tq5])
            tr1 = c.step("ve", lambda ve, MT=MT: ve.tensor_scalar(spc[:], strabs[:], MT[:, 0:1],
                                                                  None, ALU.mult),
                         waits=[tidx, tmt])
            tsc = None
            psc = [X["tx3"]] if X["tx3"] else []
            for h in range(H):
                tsc = c.step("ve", lambda ve, h=h, MT=MT: ve.tensor_scalar(
                    sc_all[:, h * SS:(h + 1) * SS], spc[:], MT[:, 3 + h:4 + h], 0.25,
                    ALU.mult, ALU.mult), waits=[tr1] + (psc if h == 0 else []))
            te1 = c.step("ve", lambda ve, MT=MT: ve.tensor_scalar(colsc[:, 3:4], MT[:, 2:3], 0.1,
                                                                  None, ALU.max), waits=[tmt])
            te2 = c.step("ve", lambda ve: ve.reciprocal(colsc[:, 4:5], colsc[:, 3:4]), waits=[te1])
            te3 = c.step("ve", lambda ve: ve.tensor_scalar(colsc[:, 5:6], colsc[:, 4:5], -1.0, None,
                                                           ALU.mult), waits=[te2])
            te4 = c.step("se", lambda se: se.activation(env[:], spc[:], AF.Exp,
                                                        scale=colsc[:, 5:6]), waits=[te3, tr1])
            mtokW[tj % 2] = te1
            X["te4"] = te4
            tb1 = c.step("ve", lambda ve: ve.tensor_copy(
                mbc[:], mval[:].unsqueeze(1).broadcast_to([128, H, SS])), waits=[tm3, tq4])
            tb2 = c.step("ve", lambda ve: ve.tensor_copy(mu8[:], mbc[:]), waits=[tb1])
            aw = attnW[tj % 2] or []
            tb3 = c.step("ve", lambda ve, at=at: ve.select(at[:], mu8[:], sc_all[:], negt[:]),
                         waits=[tb2, tsc, tng] + aw)
            tx1 = c.step("ve", lambda ve, at=at: ve.reduce_max(
                mx8[:].unsqueeze(2), at[:].rearrange("p (h s) -> p h s", s=SS), AX.X,
                negate=True), waits=[tb3])
            tx2 = c.step("ve", lambda ve, at=at: ve.tensor_add(
                sc_all[:].rearrange("p (h s) -> p h s", s=SS),
                at[:].rearrange("p (h s) -> p h s", s=SS),
                mx8[:].unsqueeze(2).broadcast_to([128, H, SS])), waits=[tx1])
            tx3 = c.step("se", lambda se, at=at: se.activation(at[:], sc_all[:], AF.Exp),
                         waits=[tx2])
            X["tx3"] = tx3
            tx4 = c.step("ve", lambda ve, at=at: ve.reduce_sum(
                mx8[:].unsqueeze(2), at[:].rearrange("p (h s) -> p h s", s=SS), AX.X),
                waits=[tx3])
            tx5 = c.step("ve", lambda ve: ve.reciprocal(mx8[:], mx8[:]), waits=[tx4])
            tx6 = c.step("ve", lambda ve, at=at: ve.tensor_mul(
                at[:].rearrange("p (h s) -> p h s", s=SS),
                at[:].rearrange("p (h s) -> p h s", s=SS),
                mx8[:].unsqueeze(2).broadcast_to([128, H, SS])), waits=[tx5])
            tx7 = c.step("ve", lambda ve, at=at: ve.tensor_mul(
                at[:].rearrange("p (h s) -> p h s", s=SS),
                at[:].rearrange("p (h s) -> p h s", s=SS),
                env[:].unsqueeze(1).broadcast_to([128, H, SS])), waits=[tx6, te4])
            tx8 = c.step("ve", lambda ve, at=at: ve.tensor_mul(at[:], at[:], mbc[:]), waits=[tx7])
            tx9 = c.step("ve", lambda ve, at=at: ve.reduce_sum(
                mx8[:].unsqueeze(2), at[:].rearrange("p (h s) -> p h s", s=SS), AX.X),
                waits=[tx8])
            tx10 = c.step("ve", lambda ve: ve.tensor_scalar(mx8[:], mx8[:], 1e-8, None, ALU.add),
                          waits=[tx9])
            tx11 = c.step("ve", lambda ve: ve.reciprocal(mx8[:], mx8[:]), waits=[tx10])
            tx12 = c.step("ve", lambda ve, at=at: ve.tensor_mul(
                at[:].rearrange("p (h s) -> p h s", s=SS),
                at[:].rearrange("p (h s) -> p h s", s=SS),
                mx8[:].unsqueeze(2).broadcast_to([128, H, SS])), waits=[tx11])
            return {"tidx": tidx, "tx12": tx12, "ix": ix, "at": at}

        def emitB(tj, a):
            tjs = slice(tj * 128, (tj + 1) * 128)
            ix, at = a["ix"], a["at"]
            tm = None
            gts = []
            ta1 = ta2 = None
            for s_ in range(SS):
                gw = list(gW[s_ % 4]) if gW[s_ % 4] else []
                gt = gring.load(lambda e, s_=s_, ix=ix: e.indirect_dma_start(
                    out=G[s_ % 4][:], out_offset=None, in_=xtok,
                    in_offset=IndirectOffsetOnAxis(ap=ix[:, s_:s_ + 1], axis=0)),
                    waits=gw + [a["tidx"]])
                gts.append(gt)

                def scfn_ve(ve, s_=s_, at=at):
                    last = None
                    for h in range(4):
                        last = ve.tensor_scalar(
                            tmul[s_ % 2][:, h * 128:(h + 1) * 128],
                            G[s_ % 4][:, h * 128:(h + 1) * 128],
                            at[:, h * SS + s_:h * SS + s_ + 1], None, ALU.mult)
                    return last

                def scfn_se(se, s_=s_, at=at):
                    last = None
                    for h in range(4, H):
                        last = se.activation(
                            tmul[s_ % 2][:, h * 128:(h + 1) * 128],
                            G[s_ % 4][:, h * 128:(h + 1) * 128], AF.Copy,
                            scale=at[:, h * SS + s_:h * SS + s_ + 1])
                    return last

                tw = [tmW[s_ % 2]] if tmW[s_ % 2] else []
                ta1 = c.step("ve", scfn_ve, waits=[gt, a["tx12"]] + tw)
                ta2 = c.step("se", scfn_se, waits=[gt, a["tx12"]] + tw)
                bw = (c.bank(4) + c.bank(5)) if s_ == 0 else []

                def accfn(te, s_=s_):
                    last = None
                    for b in range(2):
                        last = te.matmul(c.pb[4 + b][:], identb[:],
                                         tmul[s_ % 2][:, b * 512:(b + 1) * 512],
                                         start=(s_ == 0), stop=(s_ == SS - 1))
                    return last

                tm = c.step("te", accfn, waits=[ta1, ta2] + bw)
                tmW[s_ % 2] = tm
                gW[s_ % 4] = [ta1, ta2]
            idxW[tj % 2] = gts[-4:]
            attnW[tj % 2] = [ta1, ta2]

            def evfn(se):
                se.activation(acc[:, 0:512], c.pb[4][:], AF.Copy)
                return se.activation(acc[:, 512:1024], c.pb[5][:], AF.Copy)

            ar = [X["accR"]] if X["accR"] else []
            tmv = c.step("se", evfn, waits=[tm] + ar)
            c.bank_done(4, tmv)
            c.bank_done(5, tmv)
            otr = X["otTR"] or []
            mt = et = None
            for oi in range(ND):
                bidx = oi % 2
                mt = c.step("te", lambda te, oi=oi, bidx=bidx: te.transpose(
                    c.pb[bidx][:, 0:128], acc[:, oi * 128:(oi + 1) * 128], c.ident[:]),
                    waits=[tmv] + c.bank(bidx))
                et = c.step("se", lambda se, oi=oi, bidx=bidx: se.activation(
                    otT[:, oi * 128:(oi + 1) * 128], c.pb[bidx][:, 0:128], AF.Copy),
                    waits=[mt] + (otr if oi == 0 else []))
                c.bank_done(bidx, et)
            X["accR"] = mt
            zr = [X["zTR"]] if X["zTR"] else []
            mt1 = None
            for oi2 in range(2):
                bidx = 2 + oi2
                mt1 = c.step("te", lambda te, oi2=oi2, bidx=bidx: [te.matmul(
                    c.pb[bidx][:, 0:128], wse1[:, di * 256 + oi2 * 128:di * 256 + (oi2 + 1) * 128],
                    otT[:, di * 128:(di + 1) * 128], start=(di == 0), stop=(di == ND - 1))
                    for di in range(ND)][-1], waits=[et] + c.bank(bidx))
                ez = c.step("se", lambda se, oi2=oi2, bidx=bidx: se.activation(
                    zT[:, oi2 * 128:(oi2 + 1) * 128], c.pb[bidx][:, 0:128], AF.Silu,
                    bias=w["bs1c"][:, oi2:oi2 + 1]), waits=[mt1] + (zr if oi2 == 0 else []))
                c.bank_done(bidx, ez)
            tgr = [X["tgR"]] if X["tgR"] else []
            mt2 = None
            for oi in range(ND):
                bidx = 4 + oi % 2
                mt2 = c.step("te", lambda te, oi=oi, bidx=bidx: [te.matmul(
                    c.pb[bidx][:, 0:128], wse2[:, li * 1024 + oi * 128:li * 1024 + (oi + 1) * 128],
                    zT[:, li * 128:(li + 1) * 128], start=(li == 0), stop=(li == 1))
                    for li in range(2)][-1], waits=[ez] + c.bank(bidx))
                es = c.step("se", lambda se, oi=oi, bidx=bidx: se.activation(
                    seT[:, oi * 128:(oi + 1) * 128], c.pb[bidx][:, 0:128], AF.Sigmoid,
                    bias=w["bs2c"][:, oi:oi + 1]), waits=[mt2] + (tgr if oi == 0 else []))
                c.bank_done(bidx, es)
            X["zTR"] = mt2
            X["otTR"] = [mt1]
            gr = [X["gTR"]] if X["gTR"] else []
            tg = c.step("ve", lambda ve: ve.tensor_mul(gT[:], otT[:], seT[:]), waits=[es] + gr)
            X["otTR"].append(tg)
            X["tgR"] = tg
            exprev = X["exprev"]
            mtw = None
            for di in range(ND):
                bidx = 6 + di % 2
                mtw = c.step("te", lambda te, di=di, bidx=bidx: [te.matmul(
                    c.pb[bidx][:, 0:128], wout[ii][:, di * 128:(di + 1) * 128],
                    gT[:, ii * 128:(ii + 1) * 128], start=(ii == 0), stop=(ii == ND - 1))
                    for ii in range(ND)][-1], waits=[tg] + c.bank(bidx))
                ef = c.step("se", lambda se, bidx=bidx, di=di: se.activation(
                    sef[:, (di % 2) * 128:(di % 2 + 1) * 128], c.pb[bidx][:, 0:128], AF.Silu),
                    waits=[mtw] + ([exprev] if exprev else []))
                c.bank_done(bidx, ef)
                ex = c.step("ve", lambda ve, di=di, tjs=tjs: ve.tensor_add(
                    c.xr[di][:, tjs], c.xr[di][:, tjs], sef[:, (di % 2) * 128:(di % 2 + 1) * 128]),
                    waits=[ef])
                if di % 2 == 1:
                    exprev = ex
            X["exprev"] = exprev
            X["gTR"] = mtw

        Aq = emitA(0)
        for tj in range(NT):
            nxt = emitA(tj + 1) if tj + 1 < NT else None
            emitB(tj, Aq)
            Aq = nxt
    c.barrier()


def emit_l1_attn_b(c, w):
    nc = c.nc
    with ExitStack() as sctx:
        xlf = [sctx.enter_context(nc.sbuf_tensor(c.nid("xlf"), [128, S], F32)) for _ in range(ND)]
        emit_ln(c, out_tiles=xlf, out_dtype=F32)
        B = [sctx.enter_context(nc.sbuf_tensor(c.nid("sbuf"), [128, 1024], F32)) for _ in range(7)]
        carry = sctx.enter_context(nc.sbuf_tensor(c.nid("carry"), [128, 1], F32))
        msk = sctx.enter_context(nc.sbuf_tensor(c.nid("msk"), [128, 1024], U8))
        CM1 = c.cc[:, CCOL[-1.0]:CCOL[-1.0] + 1]
        C1 = c.cc[0:1, CCOL[1.0]:CCOL[1.0] + 1]

        for di in range(ND):
            prev_scan = None
            for hh in range(2):
                sl = slice(hh * 1024, (hh + 1) * 1024)
                xg = xlf[di]
                t1 = c.step("ve", lambda ve, xg=xg, sl=sl, di=di: (
                    ve.tensor_scalar(B[0][:], xg[:, sl], w["wfc"][:, di:di + 1], None, ALU.mult),
                    ve.tensor_scalar(B[1][:], xg[:, sl], w["wic"][:, di:di + 1], None, ALU.mult),
                    ve.tensor_scalar(B[2][:], xg[:, sl], w["whc"][:, di:di + 1], None, ALU.mult))[-1])
                sp = []
                for gi in (0, 1):
                    ta1 = c.step("se", lambda se, gi=gi: se.activation(B[3][:], B[gi][:], AF.Abs),
                                 waits=[t1])
                    ta2 = c.step("se", lambda se: se.activation(B[3][:], B[3][:], AF.Exp, scale=CM1))
                    ta3 = c.step("se", lambda se: se.activation(B[3][:], B[3][:], AF.Ln,
                                                                bias=c.cc[:, CCOL[1.0]:CCOL[1.0] + 1]))
                    tr = c.step("ve", lambda ve, gi=gi: ve.tensor_scalar(
                        B[4][:], B[gi][:], -1.0, 0.0, ALU.mult, ALU.max), waits=[t1])
                    tsp = c.step("gp", lambda gp, gi=gi: gp.tensor_add(B[5 + gi][:], B[4][:], B[3][:]),
                                 waits=[ta3, tr])
                    sp.append(tsp)
                tdiff = c.step("gp", lambda gp: gp.tensor_sub(B[0][:], B[5][:], B[6][:]), waits=sp)
                tf = c.step("se", lambda se: se.activation(B[5][:], B[0][:], AF.Sigmoid, scale=CM1),
                            waits=[tdiff])
                ti = c.step("se", lambda se: se.activation(B[6][:], B[0][:], AF.Sigmoid),
                            waits=[tdiff])
                tg1 = c.step("ve", lambda ve: ve.tensor_scalar(B[1][:], B[2][:], 0.5, None, ALU.add),
                             waits=[tdiff])
                tg2 = c.step("se", lambda se: se.activation(B[0][:], B[2][:], AF.Sigmoid), waits=[tg1])
                tm = c.step("ve", lambda ve: ve.tensor_scalar(msk[:], B[2][:], 0.0, None, ALU.is_ge),
                            waits=[tg1])
                tsel = c.step("ve", lambda ve: ve.select(B[4][:], msk[:], B[1][:], B[0][:]),
                              waits=[tg2, tm])
                tv = c.step("gp", lambda gp: gp.tensor_mul(B[4][:], B[6][:], B[4][:]),
                            waits=[tsel, ti])
                init = 0.0 if hh == 0 else carry[:]
                scw = [tv, tf] + ([prev_scan] if prev_scan else [])
                tscan = c.step("ve", lambda ve, init=init: ve.tensor_tensor_scan(
                    B[1][:], B[5][:], B[4][:], init, ALU.mult, ALU.add), waits=scw)
                tres = c.step("ve", lambda ve, di=di, sl=sl: ve.tensor_add(
                    c.xr[di][:, sl], c.xr[di][:, sl], B[1][:]), waits=[tscan])
                if hh == 0:
                    prev_scan = c.step("ve", lambda ve: ve.tensor_copy(
                        carry[:], B[1][:, 1023:1024]), waits=[tscan])
                else:
                    prev_scan = None
    c.barrier()


def emit_load_x(c, xin):
    qs = ("sy", "se", "gp")
    c._x_toks = [c.step(qs[di % 3], lambda e, di=di: e.dma_start(
        out=c.xr[di][:], in_=xin[di * 128:(di + 1) * 128, :]),
        sem=f"qx{di % 3}", dma=True) for di in range(ND)]


def emit_store_x(c, xout):
    c.barrier()
    c.sem("qx")

    def stfn(sy):
        for di in range(ND):
            sy.dma_start(out=xout[di * 128:(di + 1) * 128, :], in_=c.xr[di][:]).then_inc(c.sems["qx"], 16)
            c.cnt["qx"] += 16
        sy.wait_ge(c.sems["qx"], c.cnt["qx"])
        return None

    c.step("sy", stfn)


def emit_load_cols(c, pairs):
    c.sem("qx")

    def ldfn(sy):
        for t, d in pairs:
            sy.dma_start(out=t[:], in_=d).then_inc(c.sems["qx"], 16)
            c.cnt["qx"] += 16
        sy.wait_ge(c.sems["qx"], c.cnt["qx"])
        return None

    c.step("sy", ldfn)
    c.barrier()


def _prep(inputs):
    p = {}
    p["w1tf"] = np.ascontiguousarray(inputs["L0_ffn_W1"].T)
    p["w2tf"] = np.ascontiguousarray(inputs["L0_ffn_W2"].T)
    p["l0b1c"] = np.ascontiguousarray(inputs["L0_ffn_b1"].reshape(NF, 128).T)
    p["l0b2c"] = np.ascontiguousarray(inputs["L0_ffn_b2"].reshape(ND, 128).T)
    p["l1w1t"] = np.ascontiguousarray(inputs["L1_ffn_w1_w"].T)
    p["l1w2t"] = np.ascontiguousarray(inputs["L1_ffn_w2_w"].T)
    p["l1h1t"] = np.ascontiguousarray(inputs["L1_ffn_h1_w"].T)
    p["l1h2t"] = np.ascontiguousarray(inputs["L1_ffn_h2_w"].T)
    p["l1w3t"] = np.ascontiguousarray(inputs["L1_ffn_w3_w"].T)
    p["l1s1c"] = np.ascontiguousarray(inputs["L1_ffn_w1_s"].reshape(NF, 128).T)
    p["l1s2c"] = np.ascontiguousarray(inputs["L1_ffn_w2_s"].reshape(NF, 128).T)
    p["l1h1c"] = np.ascontiguousarray(inputs["L1_ffn_h1_s"].reshape(NF, 128).T)
    p["l1h2c"] = np.ascontiguousarray(inputs["L1_ffn_h2_s"].reshape(NF, 128).T)
    p["l1w3c"] = np.ascontiguousarray(inputs["L1_ffn_w3_s"].reshape(ND, 128).T)
    p["boc"] = np.ascontiguousarray(inputs["L0_attn_a_bo"].reshape(ND, 128).T)
    p["bwvc"] = np.ascontiguousarray(inputs["L1_attn_a_b_wave"].reshape(24, 1))
    p["bqc"] = np.ascontiguousarray(inputs["L1_attn_a_b_query"].reshape(128, 1))
    p["bs1c"] = np.ascontiguousarray(inputs["L1_attn_a_b_se1"].reshape(2, 128).T)
    p["bs2c"] = np.ascontiguousarray(inputs["L1_attn_a_b_se2"].reshape(ND, 128).T)
    p["kwr"] = np.ascontiguousarray(inputs["L1_attn_a_key_weight"].reshape(16))
    p["wic"] = np.ascontiguousarray(inputs["L1_attn_b_w_i"].reshape(ND, 128).T)
    p["wfc"] = np.ascontiguousarray(inputs["L1_attn_b_w_f"].reshape(ND, 128).T)
    p["whc"] = np.ascontiguousarray(inputs["L1_attn_b_w_h"].reshape(ND, 128).T)
    return p


def build_l0_ffn(first=True):
    c = C()
    nc = c.nc
    xin = nc.dram_tensor("xin", [DM, S], F32, kind="ExternalInput").ap()
    xout = nc.dram_tensor("xout", [DM, S], F32, kind="ExternalOutput").ap()
    w = {}
    if first:
        w["w1tf"] = nc.dram_tensor("w1tf", [DM, FF], F32, kind="ExternalInput").ap()
        w["w2tf"] = nc.dram_tensor("w2tf", [FF, DM], F32, kind="ExternalInput").ap()
        w["w1t"] = nc.dram_tensor("w1t", [DM, FF], BF16, kind="ExternalOutput").ap()
        w["w2t"] = nc.dram_tensor("w2t", [FF, DM], BF16, kind="ExternalOutput").ap()
    else:
        w["w1t"] = nc.dram_tensor("w1t", [DM, FF], BF16, kind="ExternalInput").ap()
        w["w2t"] = nc.dram_tensor("w2t", [FF, DM], BF16, kind="ExternalInput").ap()
    b1c_d = nc.dram_tensor("b1c", [128, NF], F32, kind="ExternalInput").ap()
    b2c_d = nc.dram_tensor("b2c", [128, ND], F32, kind="ExternalInput").ap()
    c.setup()
    w["b1c"] = c.sb("b1cs", [128, NF], F32)
    w["b2c"] = c.sb("b2cs", [128, ND], F32)
    emit_load_cols(c, [(w["b1c"], b1c_d), (w["b2c"], b2c_d)])
    emit_load_x(c, xin)
    if first:
        xsp = nc.dram_tensor("xqsp", [DM, S], F32, kind="Internal").ap()
        emit_ln(c)
        _emit_spill(c, xsp, store=True)
        c.barrier()
        for _ in emit_l0_ffn_quant(c, w):
            pass
        c.barrier()
        _emit_spill(c, xsp, store=False)
        c.barrier()
    emit_l0_ffn(c, w, False)
    emit_store_x(c, xout)
    c.flush()
    c.ctx.close()
    return nc


def build_l1_ffn():
    c = C()
    nc = c.nc
    xin = nc.dram_tensor("xin", [DM, S], F32, kind="ExternalInput").ap()
    xout = nc.dram_tensor("xout", [DM, S], F32, kind="ExternalOutput").ap()
    w = {}
    for n, shp in (("w1t", [DM, FF]), ("w2t", [DM, FF]), ("h1t", [FF, FF]),
                   ("h2t", [FF, FF]), ("w3t", [FF, DM])):
        w[n] = nc.dram_tensor(n, shp, F32, kind="ExternalInput").ap()
    cols_d = []
    for n, k in (("s1c", NF), ("s2c", NF), ("h1c", NF), ("h2c", NF), ("w3c", ND)):
        d = nc.dram_tensor(n, [128, k], F32, kind="ExternalInput").ap()
        cols_d.append((n, k, d))
    xspill = nc.dram_tensor("xspill", [DM, S], F32, kind="Internal").ap()
    c.setup()
    pairs = []
    for n, k, d in cols_d:
        w[n] = c.sb(n + "s", [128, k], F32)
        pairs.append((w[n], d))
    emit_load_cols(c, pairs)
    emit_load_x(c, xin)
    emit_l1_ffn(c, w, xspill)
    emit_store_x(c, xout)
    c.flush()
    c.ctx.close()
    return nc


def build_l0_attn_a():
    c = C()
    nc = c.nc
    xin = nc.dram_tensor("xin", [DM, S], F32, kind="ExternalInput").ap()
    xout = nc.dram_tensor("xout", [DM, S], F32, kind="ExternalOutput").ap()
    w = {}
    for n, shp in (("wq", [DM, DM]), ("wkv", [DM, 2 * DM]), ("wo", [DM, DM])):
        w[n] = nc.dram_tensor(n, shp, F32, kind="ExternalInput").ap()
    boc_d = nc.dram_tensor("boc", [128, ND], F32, kind="ExternalInput").ap()
    xspill = nc.dram_tensor("xspill", [DM, S], F32, kind="Internal").ap()
    c.setup()
    w["boc"] = c.sb("bocs", [128, ND], F32)
    emit_load_cols(c, [(w["boc"], boc_d)])
    emit_load_x(c, xin)
    emit_l0_attn_a(c, w, xspill, xin)
    emit_store_x(c, xout)
    c.flush()
    c.ctx.close()
    return nc


def build_l1_attn_a():
    c = C()
    nc = c.nc
    xin = nc.dram_tensor("xin", [DM, S], F32, kind="ExternalInput").ap()
    xout = nc.dram_tensor("xout", [DM, S], F32, kind="ExternalOutput").ap()
    w = {}
    for n, shp in (("w_wave", [DM, 24]), ("w_query", [DM, 128]), ("w_se1", [DM, 256]),
                   ("w_se2", [256, DM]), ("w_out", [DM, DM]), ("key_weight", [16]),
                   ("bwvc", [24, 1]), ("bqc", [128, 1]), ("bs1c", [128, 2]), ("bs2c", [128, ND])):
        w[n] = nc.dram_tensor(n, shp, F32, kind="ExternalInput").ap()
    xtok = nc.dram_tensor("xtok", [S, DM], BF16, kind="Internal").ap()
    c.setup()
    cols = []
    for n in ("bwvc", "bqc", "bs1c", "bs2c"):
        t = c.sb(n + "s", list(w[n].shape), F32)
        cols.append((t, w[n]))
        w[n] = t
    emit_load_cols(c, cols)
    emit_load_x(c, xin)
    emit_l1_attn_a(c, w, xtok)
    emit_store_x(c, xout)
    c.flush()
    c.ctx.close()
    return nc


def build_l0_attn_b():
    c = C()
    nc = c.nc
    xin = nc.dram_tensor("xin", [DM, S], F32, kind="ExternalInput").ap()
    xout = nc.dram_tensor("xout", [DM, S], F32, kind="ExternalOutput").ap()
    w = {}
    for n, shp in (("Wq", [DM, DM]), ("Wk", [DM, DM]), ("Wv", [DM, DM]), ("Wo", [DM, DM]),
                   ("Wdn", [DM, 256]), ("Wuk", [256, DM]), ("Wuv", [256, DM])):
        w[n] = nc.dram_tensor(n, shp, F32, kind="ExternalInput").ap()
    xspill = nc.dram_tensor("xspill", [DM, S], F32, kind="Internal").ap()
    c.setup()
    emit_load_x(c, xin)
    emit_l0_attn_b(c, w, xspill)
    emit_store_x(c, xout)
    c.flush()
    c.ctx.close()
    return nc


def build_l1_attn_b():
    c = C()
    nc = c.nc
    xin = nc.dram_tensor("xin", [DM, S], F32, kind="ExternalInput").ap()
    xout = nc.dram_tensor("xout", [DM, S], F32, kind="ExternalOutput").ap()
    cols_d = [(n, nc.dram_tensor(n, [128, ND], F32, kind="ExternalInput").ap())
              for n in ("wic", "wfc", "whc")]
    c.setup()
    w = {}
    pairs = []
    for n, d in cols_d:
        w[n] = c.sb(n + "s", [128, ND], F32)
        pairs.append((w[n], d))
    emit_load_cols(c, pairs)
    emit_load_x(c, xin)
    emit_l1_attn_b(c, w)
    emit_store_x(c, xout)
    c.flush()
    c.ctx.close()
    return nc


def build_fused():
    c = C()
    nc = c.nc
    xin = nc.dram_tensor("xin", [DM, S], F32, kind="ExternalInput").ap()
    xout = nc.dram_tensor("xout", [DM, S], F32, kind="ExternalOutput").ap()
    dt_ = lambda n, shp, dtp=F32, kind="ExternalInput": nc.dram_tensor(n, shp, dtp, kind=kind).ap()

    wa = {"wq": dt_("l0a_wq", [DM, DM]), "wkv": dt_("l0a_wkv", [DM, 2 * DM]),
          "wo": dt_("l0a_wo", [DM, DM])}
    wf0 = {"w1tf": dt_("l0f_w1tf", [DM, FF]), "w2tf": dt_("l0f_w2tf", [FF, DM]),
           "w1t": dt_("l0f_w1t", [DM, FF], BF16, "Internal"),
           "w2t": dt_("l0f_w2t", [FF, DM], BF16, "Internal")}
    wb = {n: dt_("l0b_" + n, shp) for n, shp in (("Wq", [DM, DM]), ("Wk", [DM, DM]),
          ("Wv", [DM, DM]), ("Wo", [DM, DM]), ("Wdn", [DM, 256]), ("Wuk", [256, DM]),
          ("Wuv", [256, DM]))}
    wa1 = {n: dt_("l1a_" + n, shp) for n, shp in (("w_wave", [DM, 24]), ("w_query", [DM, 128]),
           ("w_se1", [DM, 256]), ("w_se2", [256, DM]), ("w_out", [DM, DM]), ("key_weight", [16]))}
    wf1 = {n: dt_("l1f_" + n, shp) for n, shp in (("w1t", [DM, FF]), ("w2t", [DM, FF]),
           ("h1t", [FF, FF]), ("h2t", [FF, FF]), ("w3t", [FF, DM]))}
    coldefs = (("l0a_boc", [128, ND]), ("l0f_b1c", [128, NF]), ("l0f_b2c", [128, ND]),
               ("l1a_bwvc", [24, 1]), ("l1a_bqc", [128, 1]), ("l1a_bs1c", [128, 2]),
               ("l1a_bs2c", [128, ND]), ("l1f_s1c", [128, NF]), ("l1f_s2c", [128, NF]),
               ("l1f_h1c", [128, NF]), ("l1f_h2c", [128, NF]), ("l1f_w3c", [128, ND]),
               ("l1b_wic", [128, ND]), ("l1b_wfc", [128, ND]), ("l1b_whc", [128, ND]))
    cold = {n: dt_(n, shp) for n, shp in coldefs}
    xspill = dt_("xspill", [DM, S], F32, "Internal")
    xtok = dt_("xtok", [S, DM], BF16, "Internal")

    c.setup()
    pairs = []
    cols = {}
    for n, shp in coldefs:
        t = c.sb(n + "s", shp, F32)
        cols[n] = t
        pairs.append((t, cold[n]))
    emit_load_cols(c, pairs)
    wa["boc"] = cols["l0a_boc"]
    wf0["b1c"], wf0["b2c"] = cols["l0f_b1c"], cols["l0f_b2c"]
    wa1.update({"bwvc": cols["l1a_bwvc"], "bqc": cols["l1a_bqc"],
                "bs1c": cols["l1a_bs1c"], "bs2c": cols["l1a_bs2c"]})
    wf1.update({"s1c": cols["l1f_s1c"], "s2c": cols["l1f_s2c"], "h1c": cols["l1f_h1c"],
                "h2c": cols["l1f_h2c"], "w3c": cols["l1f_w3c"]})
    wb1 = {"wic": cols["l1b_wic"], "wfc": cols["l1b_wfc"], "whc": cols["l1b_whc"]}

    emit_load_x(c, xin)
    emit_l0_attn_a(c, wa, xspill, xin, after_ln=lambda: emit_l0_ffn_quant(c, wf0))
    emit_l0_ffn(c, wf0, first=False)
    emit_l0_attn_b(c, wb, xspill)
    emit_l0_ffn(c, wf0, first=False)
    emit_l1_attn_a(c, wa1, xtok)
    emit_l1_ffn(c, wf1, xspill)
    emit_l1_attn_b(c, wb1)
    emit_l1_ffn(c, wf1, xspill)
    emit_store_x(c, xout)
    c.flush()
    c.ctx.close()
    return nc


def kernel(**inputs):
    B = 8
    x = inputs["L0_attn_a_x"]
    xt = [np.ascontiguousarray(np.asarray(x[i]).T).astype(np.float32) for i in range(B)]
    p = _prep(inputs)
    nc = build_fused()
    common = {
        "l0a_wq": inputs["L0_attn_a_wq"], "l0a_wkv": inputs["L0_attn_a_wkv"],
        "l0a_wo": inputs["L0_attn_a_wo"], "l0a_boc": p["boc"],
        "l0f_w1tf": p["w1tf"], "l0f_w2tf": p["w2tf"],
        "l0f_b1c": p["l0b1c"], "l0f_b2c": p["l0b2c"],
        "l0b_Wq": inputs["L0_attn_b_Wq"], "l0b_Wk": inputs["L0_attn_b_Wk"],
        "l0b_Wv": inputs["L0_attn_b_Wv"], "l0b_Wo": inputs["L0_attn_b_Wo"],
        "l0b_Wdn": inputs["L0_attn_b_W_down_kv"], "l0b_Wuk": inputs["L0_attn_b_W_up_k"],
        "l0b_Wuv": inputs["L0_attn_b_W_up_v"],
        "l1a_w_wave": inputs["L1_attn_a_w_wave"], "l1a_w_query": inputs["L1_attn_a_w_query"],
        "l1a_w_se1": inputs["L1_attn_a_w_se1"], "l1a_w_se2": inputs["L1_attn_a_w_se2"],
        "l1a_w_out": inputs["L1_attn_a_w_out"], "l1a_key_weight": p["kwr"],
        "l1a_bwvc": p["bwvc"], "l1a_bqc": p["bqc"], "l1a_bs1c": p["bs1c"], "l1a_bs2c": p["bs2c"],
        "l1f_w1t": p["l1w1t"], "l1f_w2t": p["l1w2t"], "l1f_h1t": p["l1h1t"],
        "l1f_h2t": p["l1h2t"], "l1f_w3t": p["l1w3t"],
        "l1f_s1c": p["l1s1c"], "l1f_s2c": p["l1s2c"], "l1f_h1c": p["l1h1c"],
        "l1f_h2c": p["l1h2c"], "l1f_w3c": p["l1w3c"],
        "l1b_wic": p["wic"], "l1b_wfc": p["wfc"], "l1b_whc": p["whc"],
    }
    maps = [dict(common, xin=xt[i]) for i in range(B)]
    res = run_bass_kernel_spmd(nc, maps, core_ids=list(range(B)))
    return np.stack([np.ascontiguousarray(np.asarray(res.results[i]["xout"], np.float32).T)
                     for i in range(B)])


# version 30
# speedup vs baseline: 1.0171x; 1.0036x over previous
import numpy as np
import concourse.bass as bass
import concourse.mybir as mybir
from concourse.bass import IndirectOffsetOnAxis
from concourse.bass_utils import run_bass_kernel_spmd
from contextlib import ExitStack

F32 = mybir.dt.float32
BF16 = mybir.dt.bfloat16
I32 = mybir.dt.int32
U8 = mybir.dt.uint8
AF = mybir.ActivationFunctionType
ALU = mybir.AluOpType
AX = mybir.AxisListType

S = 2048
DM = 1024
ND = 8
NSC = 4
SC = 512
NT = 16
FF = 4096
NF = 32

CCOL_VALS = (1e-5, 1.0, -1.0, 0.5, 2047.0, 0.0, -1e-4, 1e-4, 2e-4, -2e-4, 0.03125, 0.125)
CCOL = {v: i for i, v in enumerate(CCOL_VALS)}

ENGBODY = {"te": "tensor", "ve": "vector", "se": "scalar", "gp": "gpsimd", "sy": "sync"}
ENGSEM = {"te": "pe", "ve": "ve", "se": "se", "gp": "gp", "sy": "sy"}


class C:
    def __init__(self):
        self.nc = bass.Bass("TRN2", target_bir_lowering=False, debug=False)
        self.ctx = ExitStack()
        self.cnt = {}
        self.sems = {}
        self.pending = {}
        self.watermark = {}
        self.own_count = {}
        self.uid = 0

    def nid(self, p):
        self.uid += 1
        return f"{p}{self.uid}"

    def sem(self, name):
        if name not in self.sems:
            self.sems[name] = self.ctx.enter_context(self.nc.semaphore(name))
            self.cnt[name] = 0
        return self.sems[name]

    def inc(self, instr, name, n=1):
        instr.then_inc(self.sems[name], n)
        self.cnt[name] += n
        return (name, self.cnt[name])

    def sb(self, name, shape, dtype):
        return self.ctx.enter_context(self.nc.sbuf_tensor(self.nid(name), shape, dtype))

    def ps(self, name, shape=(128, 512), dtype=F32):
        return self.ctx.enter_context(self.nc.psum_tensor(self.nid(name), list(shape), dtype))

    def step(self, eng, fn, waits=(), sem=None, dma=False):
        semn = sem or ENGSEM[eng]
        self.sem(semn)
        n = 16 if dma else 1
        self.cnt[semn] += n
        tok = (semn, self.cnt[semn])
        q = self.pending.setdefault(eng, [])
        q.append((fn, tuple(waits), semn, n))
        if len(q) >= 600:
            self.flush(eng)
        return tok

    def flush(self, eng=None):
        if eng is None:
            for e in list(self.pending):
                self.flush(e)
            return
        q = self.pending.get(eng)
        if not q:
            return
        self.pending[eng] = []
        body = getattr(self.blk, ENGBODY[eng])
        wm = self.watermark.setdefault(eng, {})

        def b(e):
            for fn, waits, semn, n in q:
                for (s, v) in waits:
                    if v > 0 and v > wm.get(s, 0):
                        e.wait_ge(self.sems[s], v)
                        wm[s] = v
                last = fn(e)
                if last is None:
                    last = e.nop()
                last.then_inc(self.sems[semn], n)
            return None

        body(b)

    def setup(self):
        nc = self.nc
        self.xr = [self.sb("xr", [128, S], F32) for _ in range(ND)]
        self.xl = [self.sb("xl", [128, S], BF16) for _ in range(ND)]
        self.PB = self.ps("PB", (128, 4096))
        self.pb = [self.PB[:, i * 512:(i + 1) * 512] for i in range(8)]
        self.pbank_tok = [None] * 8
        for s in ("pe", "ve", "se", "gp", "sy"):
            self.sem(s)
        self.ones_bf = self.sb("onesbf", [128, 1], BF16)
        self.ones_colf = self.sb("onescf", [128, 1], F32)
        self.ones_row = self.sb("onesrow", [1, 128], F32)
        self.ident = self.sb("ident", [128, 128], F32)
        self.identi = self.sb("identi", [128, 128], I32)
        self.mean_row = self.sb("meanrow", [1, SC], F32)
        self.rstd_row = self.sb("rstdrow", [1, SC], F32)
        self.srow = [self.sb("srow", [1, SC], F32) for _ in range(2)]
        self.tscr = self.sb("tscr", [128, SC], F32)
        self.cc = self.sb("cc", [128, len(CCOL_VALS)], F32)
        self.blk = self.ctx.enter_context(nc.Block("main"))

        def init_ve(ve):
            ve.memset(self.ones_bf[:], 1.0)
            ve.memset(self.ones_colf[:], 1.0)
            for j, v in enumerate(CCOL_VALS):
                ve.memset(self.cc[:, j:j + 1], v)
            return ve.memset(self.ones_row[:], 1.0)

        tv = self.step("ve", init_ve)

        def init_gp(gp):
            return gp.iota(self.identi[:], [[-1, 128]], base=0, channel_multiplier=1)

        tg = self.step("gp", init_gp)
        self.tok_ident = self.step("ve", lambda ve: ve.tensor_scalar(
            self.ident[:], self.identi[:], 0, None, ALU.is_equal), waits=[tg])
        self.tok_consts = tv

    def bank(self, i):
        t = self.pbank_tok[i]
        return [t] if t is not None else []

    def bank_done(self, i, tok):
        self.pbank_tok[i] = tok

    def barrier(self):
        self.flush()
        vals = [(s, self.cnt[s]) for s in self.sems]
        for e in ("te", "ve", "se", "gp", "sy"):
            own = ENGSEM[e]

            def b(eng, own=own):
                for (s, v) in vals:
                    if s != own and v > 0:
                        eng.wait_ge(self.sems[s], v)
                return None

            self.step(e, b)
        self.flush()


class Ring:

    def __init__(self, c, name, nslot, queues=("sy", "se")):
        self.c = c
        self.name = name
        self.nslot = nslot
        self.queues = queues
        self.i = 0
        for i in range(nslot):
            c.sem(f"{name}_{i}")

    def load(self, fn, waits=()):
        s = f"{self.name}_{self.i % self.nslot}"
        q = self.queues[self.i % len(self.queues)]
        self.i += 1
        return self.c.step(q, fn, waits=waits, sem=s, dma=True)


def emit_ln(c, out_tiles=None, out_dtype=BF16, trange=(0, S), sq_space=None):
    if out_tiles is None:
        out_tiles = c.xl
    xb = c.xl
    t0, t1 = trange
    nsc = (t1 - t0) // SC

    cast_waits = getattr(c, "_x_toks", None)
    cast_tok = [None] * ND
    c._x_toks = None

    with ExitStack() as sctx:
        if sq_space is not None:
            sq = [sq_space[:, 0:SC], sq_space[:, SC:2 * SC]]
        else:
            sq = [sctx.enter_context(c.nc.sbuf_tensor(c.nid("lnsq"), [128, SC], BF16)) for _ in range(2)]
        sq_cons = [None, None]
        prev_trep = None
        for sci in range(nsc):
            sc0 = t0 + sci * SC
            sb_ = sci % 2
            stat_tok = None
            for di in range(ND):
                cw = [cast_waits[di]] if (sci == 0 and cast_waits is not None) else []
                cast_tok[di] = c.step("se", lambda se, di=di, sc0=sc0: se.activation(
                    xb[di][:, sc0:sc0 + SC], c.xr[di][:, sc0:sc0 + SC], AF.Copy), waits=cw)
                w = []
                if sq_cons[di % 2] is not None:
                    w.append(sq_cons[di % 2])
                tsq = c.step("se", lambda se, di=di, sc0=sc0: se.activation(
                    sq[di % 2][:], xb[di][:, sc0:sc0 + SC], AF.Square), waits=w)

                def statfn(te, di=di, sc0=sc0, sb_=sb_):
                    te.matmul(c.pb[sb_][0:1, :], c.ones_bf[:], xb[di][:, sc0:sc0 + SC],
                              start=(di == 0), stop=(di == ND - 1))
                    return te.matmul(c.pb[sb_ + 2][0:1, :], c.ones_bf[:], sq[di % 2][:],
                                     start=(di == 0), stop=(di == ND - 1))

                bw = (c.bank(sb_) + c.bank(sb_ + 2)) if di == 0 else []
                stat_tok = c.step("te", statfn, waits=[tsq] + bw)
                sq_cons[di % 2] = stat_tok

            def rowsfn(ve, sb_=sb_):
                ve.tensor_scalar(c.mean_row[:], c.pb[sb_][0:1, :], 1.0 / DM, None, ALU.mult)
                ve.tensor_scalar(c.srow[0][:], c.pb[sb_ + 2][0:1, :], 1.0 / DM, None, ALU.mult)
                ve.tensor_mul(c.srow[1][:], c.mean_row[:], c.mean_row[:])
                return ve.tensor_sub(c.srow[0][:], c.srow[0][:], c.srow[1][:])

            trows = c.step("ve", rowsfn, waits=[stat_tok])
            c.bank_done(sb_, trows)
            c.bank_done(sb_ + 2, trows)
            tsqrt = c.step("se", lambda se: se.activation(
                c.srow[1][:], c.srow[0][:], AF.Sqrt, bias=c.cc[0:1, 0:1]), waits=[trows])
            trstd = c.step("ve", lambda ve: ve.reciprocal(c.rstd_row[:], c.srow[1][:]),
                           waits=[tsqrt])

            bm, br = 4 + (sci % 2) * 2, 5 + (sci % 2) * 2

            def repfn(te, bm=bm, br=br):
                te.matmul(c.pb[bm][:], c.ones_row[:], c.mean_row[:], start=True, stop=True)
                return te.matmul(c.pb[br][:], c.ones_row[:], c.rstd_row[:], start=True, stop=True)

            trep = c.step("te", repfn, waits=[trstd] + c.bank(bm) + c.bank(br))

            def normfn(ve, sc0=sc0, bm=bm, br=br):
                for di in range(ND):
                    ve.tensor_sub(c.tscr[:], c.xr[di][:, sc0:sc0 + SC], c.pb[bm][:])
                    last = ve.tensor_mul(out_tiles[di][:, sc0:sc0 + SC], c.tscr[:], c.pb[br][:])
                return last

            tnorm = c.step("ve", normfn, waits=[trep])
            c.bank_done(bm, tnorm)
            c.bank_done(br, tnorm)
    c._cast_tok = cast_tok
    c.barrier()


def emit_l0_ffn_quant(c, w):
    nc = c.nc
    ct = c._cast_tok
    wfv = [c.xr[0][:], c.xr[1][:], c.xr[5][:], c.xr[6][:]]
    wfct = [ct[0], ct[1], ct[5], ct[6]]
    x2b = c.xr[2][:].bitcast(BF16)
    wtv = [x2b[:, 0:2048], x2b[:, 2048:4096]]
    dump = c.xr[3][:]
    onesf = c.xr[4][:, 0:128]
    acc = c.xr[4][:, 128:168]
    inv = c.xr[4][:, 168:170]
    ring = Ring(c, "l0q", 4)
    st_ring = Ring(c, "l0qs", 2, queues=("gp",))
    tof = c.step("ve", lambda ve: ve.memset(onesf, 1.0), waits=[ct[4]])
    q_cons = [None, None]
    tq_slot = [None] * 4
    prev_red = None
    for wi, (wsrc, wdst, nrow) in enumerate(((w["w1tf"], w["w1t"], ND),
                                             (w["w2tf"], w["w2t"], NF))):
        ncol = wsrc.shape[1]
        nh = 2 if ncol > 2048 else 1
        cw = min(ncol, 2048)
        nu = nrow * nh
        vts = []
        for u in range(nu):
            r, hc = u // nh, u % nh
            src_ = wsrc[r * 128:(r + 1) * 128, hc * cw:(hc + 1) * cw]
            if u >= 4:
                bw = [vts[u - 4]]
            elif wi == 0:
                bw = [wfct[u]]
            else:
                bw = [t for t in (tq_slot[u % 4],) if t]
            lt = ring.load(lambda e, u=u, src_=src_, cw=cw: e.dma_start(
                out=wfv[u % 4][:, :cw], in_=src_), waits=bw)
            aw = [lt]
            if u == 0 and wi == 1 and prev_red is not None:
                aw.append(prev_red[0])
            vts.append(c.step("se", lambda se, u=u, cw=cw: se.activation(
                dump[:, :cw], wfv[u % 4][:, :cw], AF.Abs,
                accum_out=acc[:, u:u + 1]), waits=aw))
            yield "q" if u == nu - 1 else "a"
        trw = [prev_red[1]] if prev_red is not None else []
        tred = c.step("ve", lambda ve, nu=nu: ve.reduce_sum(
            acc[:, 39:40], acc[:, 0:nu], AX.X), waits=[vts[-1]] + trw)
        tsum = c.step("te", lambda te, wi=wi: te.matmul(
            c.pb[6][:, wi:wi + 1], onesf, acc[:, 39:40], start=True, stop=True),
            waits=[tred, tof] + c.bank(6))
        nel = float(wsrc.shape[0] * ncol)
        tscl0 = c.step("ve", lambda ve, nel=nel, wi=wi: ve.tensor_scalar(
            acc[:, 38:39], c.pb[6][:, wi:wi + 1],
            1.0 / nel, 1e-8, ALU.mult, ALU.max), waits=[tsum])
        c.bank_done(6, tscl0)
        yield "a"
        tinv = c.step("ve", lambda ve, wi=wi: ve.reciprocal(
            inv[:, wi:wi + 1], acc[:, 38:39]), waits=[tscl0])
        prev_red = (tred, tsum)
        yield "a"
        for u in range(nu):
            r, hc = u // nh, u % nh
            src_ = wsrc[r * 128:(r + 1) * 128, hc * cw:(hc + 1) * cw]
            bw = [tq_slot[u % 4]] if tq_slot[u % 4] else []
            if u < 4:
                bw.append(vts[-1])
            lt = ring.load(lambda e, u=u, src_=src_, cw=cw: e.dma_start(
                out=wfv[u % 4][:, :cw], in_=src_), waits=bw)

            def qfn(ve, u=u, wi=wi, cw=cw):
                v = wfv[u % 4][:, :cw]
                t = wtv[u % 2][:, :cw]
                ve.tensor_scalar(v, v, inv[:, wi:wi + 1], None, ALU.mult)
                ve.tensor_scalar(t, v, -0.5, None, ALU.is_le)
                return ve.scalar_tensor_tensor(t, v, 0.5, t, ALU.is_ge, ALU.subtract)

            qw = [q_cons[u % 2]] if q_cons[u % 2] else []
            if u < 2 and wi == 0:
                qw.append(ct[2])
            tq = c.step("ve", qfn, waits=[lt, tinv] + qw)
            tq_slot[u % 4] = tq
            tst = st_ring.load(lambda e, wdst=wdst, r=r, hc=hc, cw=cw, u=u: e.dma_start(
                out=wdst[r * 128:(r + 1) * 128, hc * cw:(hc + 1) * cw],
                in_=wtv[u % 2][:, :cw]), waits=[tq])
            q_cons[u % 2] = tst
            yield "a"
    c._q_prev = (q_cons[0], q_cons[1], tq)


def emit_l0_ffn(c, w, first):
    nc = c.nc
    emit_ln(c)

    with ExitStack() as sctx:
        hT = [sctx.enter_context(nc.sbuf_tensor(c.nid("hT"), [128, 1024], BF16)) for _ in range(NF)]
        wbuf = [sctx.enter_context(nc.sbuf_tensor(c.nid("wst"), [128, 2048], BF16)) for _ in range(2)]
        w2buf = [sctx.enter_context(nc.sbuf_tensor(c.nid("w2st"), [128, FF], BF16)) for _ in range(2)]
        ring = Ring(c, "l0f", 2)

        for half in range(2):
            scs = [2 * half, 2 * half + 1]
            wtoks, cons_of_load = [], [None] * 99
            j = 0
            etoks = []
            for fp in range(NF // 2):
                bw = [etoks[(fp - 2) * 4 + 3]] if fp >= 2 else []
                src = w["w1t"].rearrange("(dp p) f -> p dp f", p=128)[:, :, fp * 256:(fp + 1) * 256]
                lt = ring.load(lambda e, src=src, fp=fp: e.dma_start(
                    out=wbuf[fp % 2][:], in_=src), waits=bw)
                for fo in range(2):
                    fi = fp * 2 + fo
                    ba, bb = j % 6, (j + 1) % 6

                    def mmfn(te, fp=fp, fo=fo, scs=scs, ba=ba, bb=bb):
                        for di in range(ND):
                            wt = wbuf[fp % 2][:, di * 256 + fo * 128: di * 256 + (fo + 1) * 128]
                            te.matmul(c.pb[ba][:], wt,
                                      c.xl[di][:, scs[0] * SC:(scs[0] + 1) * SC],
                                      start=(di == 0), stop=(di == ND - 1), skip_group_check=True)
                            last = te.matmul(c.pb[bb][:], wt,
                                             c.xl[di][:, scs[1] * SC:(scs[1] + 1) * SC],
                                             start=(di == 0), stop=(di == ND - 1), skip_group_check=True)
                        return last

                    mt = c.step("te", mmfn, waits=[lt] + c.bank(ba) + c.bank(bb))
                    for k2, bx in ((0, ba), (1, bb)):
                        et = c.step("se", lambda se, fi=fi, k2=k2, bx=bx: se.activation(
                            hT[fi][:, k2 * SC:(k2 + 1) * SC], c.pb[bx][:],
                            AF.Gelu, bias=w["b1c"][:, fi:fi + 1]), waits=[mt])
                        c.bank_done(bx, et)
                        etoks.append(et)
                    j += 2

            j = 0
            etoks2 = []
            for di in range(ND):
                bw = [etoks2[(di - 2) * 2 + 1]] if di >= 2 else []
                src = w["w2t"].rearrange("(fp p) d -> p fp d", p=128)[:, :, di * 128:(di + 1) * 128]
                lt = ring.load(lambda e, src=src, di=di: e.dma_start(
                    out=w2buf[di % 2][:], in_=src), waits=bw + [etoks[-1]])
                ba, bb = j % 6, (j + 1) % 6

                def mmfn(te, di=di, ba=ba, bb=bb):
                    for fi in range(NF):
                        wt = w2buf[di % 2][:, fi * 128:(fi + 1) * 128]
                        te.matmul(c.pb[ba][:], wt, hT[fi][:, 0:SC],
                                  start=(fi == 0), stop=(fi == NF - 1), skip_group_check=True)
                        last = te.matmul(c.pb[bb][:], wt, hT[fi][:, SC:2 * SC],
                                         start=(fi == 0), stop=(fi == NF - 1), skip_group_check=True)
                    return last

                mt = c.step("te", mmfn, waits=[lt] + c.bank(ba) + c.bank(bb))
                for k2, bx in ((0, ba), (1, bb)):
                    sl = slice(scs[k2] * SC, (scs[k2] + 1) * SC)
                    et = c.step("ve", lambda ve, di=di, sl=sl, bx=bx: ve.scalar_tensor_tensor(
                        c.xr[di][:, sl], c.pb[bx][:], w["b2c"][:, di:di + 1],
                        c.xr[di][:, sl], ALU.add, ALU.add), waits=[mt])
                    c.bank_done(bx, et)
                    etoks2.append(et)
                j += 2
            c.barrier()


def emit_l1_ffn(c, w, xspill):
    nc = c.nc
    tspill = _emit_spill(c, xspill, store=True)
    emit_ln(c)

    with ExitStack() as sctx:
        u = [sctx.enter_context(nc.sbuf_tensor(c.nid("u"), [128, 1024], BF16)) for _ in range(NF)]
        wst = [sctx.enter_context(nc.sbuf_tensor(c.nid("wl1"), [128, FF], BF16)) for _ in range(2)]
        rbuf = [sctx.enter_context(nc.sbuf_tensor(c.nid("rb"), [128, SC], F32)) for _ in range(2)]
        obuf = [sctx.enter_context(nc.sbuf_tensor(c.nid("ob"), [128, SC], F32)) for _ in range(2)]
        ring = Ring(c, "l1f", 3, queues=("gp",))
        rring = Ring(c, "l1r", 2, queues=("sy",))
        oring = Ring(c, "l1o", 2, queues=("sy",))
        bt = []
        for fi in range(NF):
            bt.append(c.xr[fi // 4][:].bitcast(BF16)[:, (fi % 4) * 1024:(fi % 4 + 1) * 1024])

        for half in range(2):
            t1 = _l1_stage_in1024(c, ring, wst, w["w1t"], w["s1c"], u, half,
                                  mode="silu", barrier=False)
            t2 = _l1_stage_in1024(c, ring, wst, w["w2t"], w["s2c"], u, half,
                                  mode="mul", entry=[t1], barrier=False)
            t3 = _l1_stage_in4096(c, ring, wst, w["h1t"], w["h1c"], u, bt,
                                  extra=[tspill], entry=[t2], barrier=False)
            t4 = _l1_stage_in4096(c, ring, wst, w["h2t"], w["h2c"], bt, u,
                                  entry=[t3], barrier=False)
            _l1_stage_spillout(c, ring, rring, oring, wst, rbuf, obuf,
                               w["w3t"], w["w3c"], u, half, xspill, entry=[t4])

    _emit_spill(c, xspill, store=False)
    c.barrier()


def _emit_spill(c, xspill, store):
    qs = ("sy", "se", "gp")
    toks = []
    for di in range(ND):
        def fn(e, di=di, store=store):
            if store:
                return e.dma_start(out=xspill[di * 128:(di + 1) * 128, :], in_=c.xr[di][:])
            return e.dma_start(out=c.xr[di][:], in_=xspill[di * 128:(di + 1) * 128, :])

        toks.append(c.step(qs[di % 3], fn, sem=f"qsp{di % 3}", dma=True))
    return c.step("sy", lambda e: None, waits=toks)


def _l1_stage_in1024(c, ring, wst, wsrc, scol, dst, half, mode, entry=(), barrier=True):
    etoks = []
    j = 0
    entry = list(entry)
    for fp in range(NF // 2):
        bw = [etoks[(fp - 2) * 4 + 3]] if fp >= 2 else list(entry)
        src = wsrc.rearrange("(dp p) f -> p dp f", p=128)[:, :, fp * 256:(fp + 1) * 256]
        lt = ring.load(lambda e, src=src, fp=fp: e.dma_start(
            out=wst[fp % 2][:, :2048], in_=src), waits=bw)
        for fo in range(2):
            fi = fp * 2 + fo
            ba, bb = j % 6, (j + 1) % 6

            def mmfn(te, fp=fp, fo=fo, half=half, ba=ba, bb=bb):
                for di in range(ND):
                    wt = wst[fp % 2][:, di * 256 + fo * 128: di * 256 + (fo + 1) * 128]
                    te.matmul(c.pb[ba][:], wt,
                              c.xl[di][:, (half * 2) * SC:(half * 2 + 1) * SC],
                              start=(di == 0), stop=(di == ND - 1), skip_group_check=True)
                    last = te.matmul(c.pb[bb][:], wt,
                                     c.xl[di][:, (half * 2 + 1) * SC:(half * 2 + 2) * SC],
                                     start=(di == 0), stop=(di == ND - 1), skip_group_check=True)
                return last

            mt = c.step("te", mmfn, waits=[lt] + c.bank(ba) + c.bank(bb) + entry)
            for k2, bx in ((0, ba), (1, bb)):
                if mode == "silu":
                    et = c.step("se", lambda se, fi=fi, k2=k2, bx=bx: se.activation(
                        dst[fi][:, k2 * SC:(k2 + 1) * SC], c.pb[bx][:],
                        AF.Silu, scale=scol[:, fi:fi + 1]), waits=[mt] + entry)
                else:
                    sl = slice(k2 * SC, (k2 + 1) * SC)
                    et = c.step("ve", lambda ve, fi=fi, sl=sl, bx=bx: ve.scalar_tensor_tensor(
                        dst[fi][:, sl], c.pb[bx][:], scol[:, fi:fi + 1],
                        dst[fi][:, sl], ALU.mult, ALU.mult), waits=[mt] + entry)
                c.bank_done(bx, et)
                etoks.append(et)
            j += 2
    if barrier:
        c.barrier()
    return etoks[-1]


def _l1_stage_in4096(c, ring, wst, wsrc, scol, src_t, dst_t, extra=(), entry=(), barrier=True):
    etoks = []
    entry = list(entry)
    for oi in range(NF):
        bw = [etoks[(oi - 2) * 2 + 1]] if oi >= 2 else list(entry)
        src = wsrc.rearrange("(fp p) o -> p fp o", p=128)[:, :, oi * 128:(oi + 1) * 128]
        lt = ring.load(lambda e, src=src, oi=oi: e.dma_start(
            out=wst[oi % 2][:], in_=src), waits=bw)
        ba, bb = (2 * oi) % 6, (2 * oi + 1) % 6

        def mmfn(te, oi=oi, ba=ba, bb=bb):
            for fi in range(NF):
                wt = wst[oi % 2][:, fi * 128:(fi + 1) * 128]
                te.matmul(c.pb[ba][:], wt, src_t[fi][:, 0:SC],
                          start=(fi == 0), stop=(fi == NF - 1), skip_group_check=True)
                last = te.matmul(c.pb[bb][:], wt, src_t[fi][:, SC:2 * SC],
                                 start=(fi == 0), stop=(fi == NF - 1), skip_group_check=True)
            return last

        mt = c.step("te", mmfn, waits=[lt] + c.bank(ba) + c.bank(bb) + entry)
        for k2, bx in ((0, ba), (1, bb)):
            et = c.step("se", lambda se, oi=oi, k2=k2, bx=bx: se.activation(
                dst_t[oi][:, k2 * SC:(k2 + 1) * SC], c.pb[bx][:],
                AF.Gelu, scale=scol[:, oi:oi + 1]), waits=[mt] + list(extra) + entry)
            c.bank_done(bx, et)
            etoks.append(et)
    if barrier:
        c.barrier()
    return etoks[-1]


def _l1_stage_spillout(c, ring, rring, oring, wst, rbuf, obuf, wsrc, scol, src_t, half, xspill,
                       entry=()):
    stoks = []
    j = 0
    entry = list(entry)
    for di in range(ND):
        bw = [stoks[(di - 2) * 2 + 1]] if di >= 2 else list(entry)
        src = wsrc.rearrange("(fp p) d -> p fp d", p=128)[:, :, di * 128:(di + 1) * 128]
        lt = ring.load(lambda e, src=src, di=di: e.dma_start(
            out=wst[di % 2][:], in_=src), waits=bw)
        ba, bb = j % 6, (j + 1) % 6
        rts = []
        for k2 in range(2):
            sc = half * 2 + k2
            sl = slice(sc * SC, (sc + 1) * SC)
            rw = [stoks[j + k2 - 2]] if j + k2 >= 2 else []
            rts.append(rring.load(lambda e, di=di, sl=sl, jj=j + k2: e.dma_start(
                out=rbuf[jj % 2][:], in_=xspill[di * 128:(di + 1) * 128, sl]), waits=rw))

        def mmfn(te, di=di, ba=ba, bb=bb):
            for fi in range(NF):
                wt = wst[di % 2][:, fi * 128:(fi + 1) * 128]
                te.matmul(c.pb[ba][:], wt, src_t[fi][:, 0:SC],
                          start=(fi == 0), stop=(fi == NF - 1), skip_group_check=True)
                last = te.matmul(c.pb[bb][:], wt, src_t[fi][:, SC:2 * SC],
                                 start=(fi == 0), stop=(fi == NF - 1), skip_group_check=True)
            return last

        mt = c.step("te", mmfn, waits=[lt] + c.bank(ba) + c.bank(bb) + entry)
        for k2, bx in ((0, ba), (1, bb)):
            sc = half * 2 + k2
            sl = slice(sc * SC, (sc + 1) * SC)
            jj = j + k2
            ow = [stoks[jj - 2]] if jj >= 2 else []
            et = c.step("ve", lambda ve, di=di, jj=jj, bx=bx: ve.scalar_tensor_tensor(
                obuf[jj % 2][:], c.pb[bx][:], scol[:, di:di + 1],
                rbuf[jj % 2][:], ALU.mult, ALU.add), waits=[mt, rts[k2]] + ow)
            c.bank_done(bx, et)
            st = oring.load(lambda e, di=di, sl=sl, jj=jj: e.dma_start(
                out=xspill[di * 128:(di + 1) * 128, sl], in_=obuf[jj % 2][:]), waits=[et])
            stoks.append(st)
        j += 2
    c.barrier()


def emit_l0_attn_b(c, w, xspill):
    nc = c.nc
    H = 16
    ASC = c.cc[:, CCOL[0.03125]:CCOL[0.03125] + 1]

    with ExitStack() as sctx:
        sb = lambda n, shp, dt: sctx.enter_context(nc.sbuf_tensor(c.nid(n), shp, dt))
        qraw = [sb("qraw", [128, S], BF16) for _ in range(ND)]
        lat = [c.xl[4], c.xl[5]]
        vext = [sb("vext", [128, H * 65], BF16) for _ in range(NT)]
        abf = sb("abf", [128, S], BF16)
        bbf = sb("bbf", [128, S], BF16)
        cdiag = sb("cdiag", [128, 128], BF16)
        qav = lambda oi: c.xl[oi // 4][:, (oi % 4) * SC:(oi % 4 + 1) * SC]
        qbv = lambda oi: c.xl[2 + oi // 4][:, (oi % 4) * SC:(oi % 4 + 1) * SC]
        at = [c.xl[5][:, i * 512:(i + 1) * 512] for i in range(4)] + \
             [sb("at4", [128, SC], BF16)]
        a2bc = c.xl[4][:, 1024:1536]
        dscb = c.xl[4][:, 1536:2048]
        tb2 = sb("tb2", [128, 16], F32)
        i16 = sb("i16", [128, 16], I32)
        onorms = [c.xl[7][:].bitcast(F32)[0:65, 512:1024],
                  c.xl[6][:].bitcast(F32)[0:65, 512:1024]]
        dsc = sb("dsc", [128, 128], F32)
        rbc = c.tscr[0:64, 0:SC]
        ao = sb("ao", [128, ND * SC], BF16)
        rbuf = [c.xl[6][:].bitcast(F32)[:, i * SC:(i + 1) * SC] for i in range(2)]
        obuf = [c.xl[7][:].bitcast(F32)[:, i * SC:(i + 1) * SC] for i in range(2)]
        ones65 = sb("on65b", [65, 64], F32)
        stage = [c.xl[4][0:64, 0:SC], c.xl[4][0:64, SC:2 * SC]]
        wkv = [sb("wkv", [128, 1024], BF16) for _ in range(ND)]
        wsm = ao[:, 0:2048]
        wsm2 = wsm
        ring = Ring(c, "ab", 3, queues=("gp",))
        iscr = [c.xl[0][:].bitcast(I32), c.xl[1][:].bitcast(I32)]
        fscr = [c.xl[2][:].bitcast(F32), c.xl[3][:].bitcast(F32)]

        t65 = None
        tvo = []
        for tj in range(NT):
            tvo.append(c.step("ve", lambda ve, tj=tj: ve.memset(
                vext[tj][:].rearrange("p (h e) -> p h e", e=65)[:, :, 64:65], 1.0)))

        kpv = [c.xr[oi // 2][:].bitcast(BF16)[:, (oi % 2) * S:(oi % 2 + 1) * S] for oi in range(ND)]
        kA = [c.xr[4 + oi // 2][:].bitcast(BF16)[:, (oi % 2) * S:(oi % 2 + 1) * S] for oi in range(ND)]
        kB = kpv

        wq_t = [ring.load(lambda e, di=di: e.dma_start(
            out=wkv[di][:], in_=w["Wq"][di * 128:(di + 1) * 128, :])) for di in range(ND)]
        tspill = _emit_spill(c, xspill, store=True)
        emit_ln(c, sq_space=qraw[0][:, 0:2 * SC])

        j = 0
        for oi in range(ND):
            for sc in range(NSC):
                bidx = j % 6

                def mmfn(te, oi=oi, sc=sc, bidx=bidx):
                    for di in range(ND):
                        last = te.matmul(c.pb[bidx][:], wkv[di][:, oi * 128:(oi + 1) * 128],
                                         c.xl[di][:, sc * SC:(sc + 1) * SC],
                                         start=(di == 0), stop=(di == ND - 1))
                    return last

                mt = c.step("te", mmfn, waits=[wq_t[-1]] + c.bank(bidx))
                et = c.step("se", lambda se, oi=oi, sc=sc, bidx=bidx: se.activation(
                    qraw[oi][:, sc * SC:(sc + 1) * SC], c.pb[bidx][:], AF.Copy), waits=[mt])
                c.bank_done(bidx, et)
                j += 1
        c.barrier()

        for pi, (wn, first_) in enumerate((("Wk", True), ("Wv", False))):
            wt_ = [ring.load(lambda e, di=di, wn=wn: e.dma_start(
                out=wkv[di][:], in_=w[wn][di * 128:(di + 1) * 128, :])) for di in range(ND)]
            j = 0
            for oi in range(ND):
                for sc in range(NSC):
                    bidx = j % 6

                    def mmfn(te, oi=oi, sc=sc, bidx=bidx):
                        for di in range(ND):
                            last = te.matmul(c.pb[bidx][:], wkv[di][:, oi * 128:(oi + 1) * 128],
                                             c.xl[di][:, sc * SC:(sc + 1) * SC],
                                             start=(di == 0), stop=(di == ND - 1))
                        return last

                    mt = c.step("te", mmfn, waits=[wt_[-1], tspill] + c.bank(bidx))
                    if first_:
                        et = c.step("se", lambda se, oi=oi, sc=sc, bidx=bidx: se.activation(
                            kpv[oi][:, sc * SC:(sc + 1) * SC], c.pb[bidx][:], AF.Copy), waits=[mt])
                    else:
                        et = c.step("ve", lambda ve, oi=oi, sc=sc, bidx=bidx: ve.tensor_add(
                            kpv[oi][:, sc * SC:(sc + 1) * SC],
                            kpv[oi][:, sc * SC:(sc + 1) * SC],
                            c.pb[bidx][:]), waits=[mt])
                    c.bank_done(bidx, et)
                    j += 1
            c.barrier()

        tio = c.step("gp", lambda gp: (gp.iota(iscr[0][:], [[1, 1024]], base=0, channel_multiplier=0),
                                       gp.iota(iscr[1][:], [[1, 1024]], base=1024,
                                               channel_multiplier=0))[-1])
        tfs = c.step("ve", lambda ve: (ve.tensor_copy(fscr[0][:], iscr[0][:]),
                                       ve.tensor_copy(fscr[1][:], iscr[1][:]))[-1], waits=[tio])

        def tabfn(se):
            se.activation(abf[:, 0:1024], fscr[0][:], AF.Exp, scale=c.cc[:, CCOL[-1e-4]:CCOL[-1e-4] + 1])
            se.activation(abf[:, 1024:2048], fscr[1][:], AF.Exp, scale=c.cc[:, CCOL[-1e-4]:CCOL[-1e-4] + 1])
            se.activation(bbf[:, 0:1024], fscr[0][:], AF.Exp, scale=c.cc[:, CCOL[1e-4]:CCOL[1e-4] + 1])
            return se.activation(bbf[:, 1024:2048], fscr[1][:], AF.Exp, scale=c.cc[:, CCOL[1e-4]:CCOL[1e-4] + 1])

        tab = c.step("se", tabfn, waits=[tfs])
        tio2 = c.step("gp", lambda gp: gp.iota(iscr[0][:, 0:128], [[-1, 128]], base=0,
                                               channel_multiplier=1), waits=[tab])
        tcp2 = c.step("ve", lambda ve: ve.tensor_copy(fscr[0][:, 0:128], iscr[0][:, 0:128]),
                      waits=[tio2])
        tmn = c.step("ve", lambda ve: ve.tensor_scalar(dsc[:, 0:128], fscr[0][:, 0:128], 0.0, None, ALU.min),
                     waits=[tcp2])
        tcd = c.step("se", lambda se: se.activation(cdiag[:], dsc[:, 0:128], AF.Exp,
                                                    scale=c.cc[:, CCOL[2e-4]:CCOL[2e-4] + 1]),
                     waits=[tmn])

        tdn = ring.load(lambda e: e.dma_start(
            out=wsm[:].rearrange("p (d o) -> p d o", o=256),
            in_=w["Wdn"].rearrange("(d p) o -> p d o", p=128)))
        j = 0
        for oi2 in range(2):
            for sc in range(NSC):
                bidx = j % 6

                def mmfn(te, oi2=oi2, sc=sc, bidx=bidx):
                    for di in range(ND):
                        last = te.matmul(c.pb[bidx][:],
                                         wsm[:, di * 256 + oi2 * 128: di * 256 + (oi2 + 1) * 128],
                                         kpv[di][:, sc * SC:(sc + 1) * SC],
                                         start=(di == 0), stop=(di == ND - 1))
                    return last

                mt = c.step("te", mmfn, waits=[tdn] + c.bank(bidx))
                et = c.step("se", lambda se, oi2=oi2, sc=sc, bidx=bidx: se.activation(
                    lat[oi2][:, sc * SC:(sc + 1) * SC], c.pb[bidx][:], AF.Copy), waits=[mt])
                c.bank_done(bidx, et)
                j += 1
        c.barrier()

        tuk = ring.load(lambda e: e.dma_start(
            out=wsm2[:].rearrange("p (l o) -> p l o", o=1024),
            in_=w["Wuk"].rearrange("(l p) o -> p l o", p=128)))
        j = 0
        for oi in range(ND):
            for sc in range(NSC):
                bidx = j % 6
                sl = slice(sc * SC, (sc + 1) * SC)

                def mmfn(te, oi=oi, sc=sc, bidx=bidx):
                    for li in range(2):
                        last = te.matmul(c.pb[bidx][:],
                                         wsm2[:, li * 1024 + oi * 128: li * 1024 + (oi + 1) * 128],
                                         lat[li][:, sc * SC:(sc + 1) * SC],
                                         start=(li == 0), stop=(li == 1))
                    return last

                mt = c.step("te", mmfn, waits=[tuk, tab] + c.bank(bidx))
                ea = c.step("ve", lambda ve, oi=oi, sl=sl, bidx=bidx: ve.tensor_mul(
                    kA[oi][:, sl], c.pb[bidx][:], abf[:, sl]), waits=[mt])
                eb = c.step("ve", lambda ve, oi=oi, sl=sl, bidx=bidx: ve.tensor_mul(
                    kB[oi][:, sl], c.pb[bidx][:], bbf[:, sl]), waits=[ea])
                c.bank_done(bidx, eb)
                j += 1
        c.barrier()

        tuv = ring.load(lambda e: e.dma_start(
            out=wsm[:].rearrange("p (l o) -> p l o", o=1024),
            in_=w["Wuv"].rearrange("(l p) o -> p l o", p=128)))
        j = 0
        for tj in range(NT):
            for vo in range(2):
                bidx = j % 6

                def mmfn(te, tj=tj, vo=vo, bidx=bidx):
                    for li in range(2):
                        last = te.matmul(c.pb[bidx][:],
                                         lat[li][:, tj * 128:(tj + 1) * 128],
                                         wsm[:, li * 1024 + vo * 512: li * 1024 + (vo + 1) * 512],
                                         start=(li == 0), stop=(li == 1))
                    return last

                mt = c.step("te", mmfn, waits=[tuv, tvo[tj]] + c.bank(bidx))
                dst = vext[tj][:].rearrange("p (h e) -> p h e", e=65)[:, vo * 8:(vo + 1) * 8, 0:64]
                et = c.step("ve", lambda ve, dst=dst, bidx=bidx: ve.tensor_copy(
                    dst, c.pb[bidx][:].rearrange("p (h e) -> p h e", e=64)), waits=[mt])
                c.bank_done(bidx, et)
                j += 1
        c.barrier()

        wo_t = [ring.load(lambda e, di=di: e.dma_start(
            out=wkv[di][:], in_=w["Wo"][di * 128:(di + 1) * 128, :])) for di in range(ND)]
        t65 = c.step("ve", lambda ve: ve.memset(ones65[:], 1.0))
        tb2i = c.step("gp", lambda gp: gp.iota(i16[:], [[128, 16]], base=0, channel_multiplier=1))
        tb2f = c.step("ve", lambda ve: ve.tensor_copy(tb2[:], i16[:]), waits=[tb2i])
        tb2t = c.step("se", lambda se: se.activation(tb2[:], tb2[:], AF.Exp,
                                                     scale=c.cc[:, CCOL[2e-4]:CCOL[2e-4] + 1]),
                      waits=[tb2f])
        c.barrier()

        rW = [None, None]
        oW = [None, None]
        for qc in range(NSC):
            qsl = slice(qc * SC, (qc + 1) * SC)
            tq = None
            for oi in range(ND):
                c.step("ve", lambda ve, oi=oi, qsl=qsl: ve.tensor_mul(
                    qav(oi), qraw[oi][:, qsl], abf[:, qsl]))
                tq = c.step("ve", lambda ve, oi=oi, qsl=qsl: ve.tensor_mul(
                    qbv(oi), qraw[oi][:, qsl], bbf[:, qsl]))
            i512 = c.xl[6][:].bitcast(I32)[:, 0:512]
            f512 = c.xl[7][:].bitcast(F32)[:, 0:512]
            a2w = [t for t in (rW[0], oW[0]) if t]
            ta2i = c.step("gp", lambda gp, qc=qc: gp.iota(i512, [[1, 512]],
                                                          base=qc * SC, channel_multiplier=0),
                          waits=a2w)
            ta2f = c.step("ve", lambda ve: ve.tensor_copy(f512, i512), waits=[ta2i])
            ta2 = c.step("se", lambda se: se.activation(a2bc[:], f512, AF.Exp,
                                                        scale=c.cc[:, CCOL[-2e-4]:CCOL[-2e-4] + 1]),
                         waits=[ta2f])

            av_war = [None] * 5
            str_war = None
            last_tn = None
            LAG = 4
            norm_q = []

            def emit_norm(h, av_last):
                hi, hb = h // 2, (h % 2) * 64
                po = 5 + (h % 2)
                onorm = onorms[h % 2]
                onw = []
                if hasattr(c, "_ab_on") and c._ab_on.get(h % 2):
                    onw.append(c._ab_on[h % 2])
                if h < 2:
                    onw += [t for t in (rW[1], oW[1]) if t]
                tev = c.step("ve", lambda ve, po=po, onorm=onorm: ve.tensor_copy(
                    onorm, c.pb[po][0:65, :]), waits=[av_last] + onw)
                c.bank_done(po, tev)
                trep = c.step("te", lambda te, onorm=onorm: te.matmul(
                    c.pb[7][0:64, :], ones65[64:65, :], onorm[64:65, :],
                    start=True, stop=True), waits=[tev, t65] + c.bank(7))
                trc = c.step("ve", lambda ve: ve.reciprocal(rbc, c.pb[7][0:64, :]), waits=[trep])
                c.bank_done(7, trc)
                if hb == 0:
                    tn = c.step("ve", lambda ve, hi=hi, onorm=onorm: ve.tensor_mul(
                        ao[0:64, hi * SC:(hi + 1) * SC], onorm[0:64, :], rbc), waits=[trc])
                else:
                    sgi = (h // 2) % 2
                    sgw = [c._ab_sg[sgi]] if hasattr(c, "_ab_sg") and c._ab_sg.get(sgi) else []
                    tn0 = c.step("ve", lambda ve, sgi=sgi, onorm=onorm: ve.tensor_mul(
                        stage[sgi][:], onorm[0:64, :], rbc), waits=[trc] + sgw)
                    tn = c.step("sy", lambda e, hi=hi, sgi=sgi: e.dma_start(
                        out=ao[64:128, hi * SC:(hi + 1) * SC], in_=stage[sgi][:]),
                        sem="abs%d" % sgi, dma=True, waits=[tn0])
                    if not hasattr(c, "_ab_sg"):
                        c._ab_sg = {}
                    c._ab_sg[sgi] = tn
                if not hasattr(c, "_ab_on"):
                    c._ab_on = {}
                c._ab_on[h % 2] = tn
                return tn

            for h in range(H):
                hi, hb = h // 2, (h % 2) * 64
                po = 5 + (h % 2)
                av_last = None
                ets = [None] * NT
                for i in range(NT + LAG):
                    if i < NT:
                        kt = i
                        lo = kt * 128
                        slot = kt % 5
                        pw = c.pb[slot]
                        ab = at[kt % 5]
                        if lo >= qc * SC + SC:
                            kind = "below"
                        elif lo + 128 <= qc * SC:
                            kind = "above"
                        else:
                            kind = "straddle"
                        kv, qv = (kB, qav) if kind == "above" else (kA, qbv)
                        mt = c.step("te", lambda te, kv=kv, qv=qv, hi=hi, hb=hb, kt=kt, pw=pw:
                                    te.matmul(pw, kv[hi][hb:hb + 64, kt * 128:(kt + 1) * 128],
                                              qv(hi)[hb:hb + 64, :], start=True, stop=True),
                                    waits=[tq, ta2] + c.bank(slot))
                        eww = [av_war[kt % 5]] if av_war[kt % 5] else []
                        if kind != "straddle":
                            et = c.step("se", lambda se, ab=ab, pw=pw: se.activation(
                                ab[:], pw, AF.Exp, scale=ASC), waits=[mt] + eww)
                            c.bank_done(slot, et)
                        else:
                            dcol = lo - qc * SC
                            sw = [str_war] if str_war else []

                            def fixfn(ve, pw=pw, dcol=dcol, kt=kt):
                                last = ve.tensor_mul(dscb[:, dcol:dcol + 128],
                                                     pw[:, dcol:dcol + 128], cdiag[:])
                                if dcol + 128 < SC:
                                    last = ve.scalar_tensor_tensor(
                                        dscb[:, dcol + 128:SC], pw[:, dcol + 128:SC],
                                        tb2[:, kt:kt + 1], a2bc[:, dcol + 128:SC],
                                        ALU.mult, ALU.mult)
                                return last

                            tfix = c.step("ve", fixfn, waits=[mt, tcd, tb2t, ta2] + sw)

                            def epi(se, ab=ab, pw=pw, dcol=dcol):
                                if dcol > 0:
                                    se.activation(ab[:, 0:dcol], pw[:, 0:dcol], AF.Exp, scale=ASC)
                                return se.activation(ab[:, dcol:SC], dscb[:, dcol:SC],
                                                     AF.Exp, scale=ASC)

                            et = c.step("se", epi, waits=[tfix] + eww)
                            str_war = et
                            c.bank_done(slot, et)
                        ets[kt] = et
                    if i == 1 and h > 0:
                        norm_q.append(emit_norm(h - 1, prev_av))
                    if i >= LAG:
                        kt = i - LAG
                        ab = at[kt % 5]
                        avw = [ets[kt]] + (c.bank(po) if kt == 0 else [])
                        av_last = c.step("te", lambda te, kt=kt, h=h, ab=ab, po=po: te.matmul(
                            c.pb[po][0:65, :], vext[kt][:, h * 65:(h + 1) * 65], ab[:],
                            start=(kt == 0), stop=(kt == NT - 1)), waits=avw)
                        av_war[kt % 5] = av_last
                prev_av = av_last
            last_tn = emit_norm(H - 1, prev_av)

            aow_toks = [last_tn] + [t for t in (c._ab_sg.get(0), c._ab_sg.get(1)) if t]
            for di in range(ND):
                bidx = di % 4
                rw = [rW[di % 2]] if rW[di % 2] else []
                rt = c.step("sy", lambda e, di=di, qsl=qsl: e.dma_start(
                    out=rbuf[di % 2][:], in_=xspill[di * 128:(di + 1) * 128, qsl]),
                    sem="abr%d" % (di % 2), dma=True, waits=rw)

                def wofn(te, di=di, bidx=bidx):
                    for ii in range(ND):
                        last = te.matmul(c.pb[bidx][:], wkv[ii][:, di * 128:(di + 1) * 128],
                                         ao[:, ii * SC:(ii + 1) * SC],
                                         start=(ii == 0), stop=(ii == ND - 1))
                    return last

                mt = c.step("te", wofn, waits=aow_toks + [wo_t[-1]] + c.bank(bidx))
                ow = [oW[di % 2]] if oW[di % 2] else []
                ot = c.step("ve", lambda ve, di=di, bidx=bidx: ve.tensor_add(
                    obuf[di % 2][:], c.pb[bidx][:], rbuf[di % 2][:]), waits=[mt, rt] + ow)
                c.bank_done(bidx, ot)
                rW[di % 2] = ot
                st = c.step("sy", lambda e, di=di, qsl=qsl: e.dma_start(
                    out=xspill[di * 128:(di + 1) * 128, qsl], in_=obuf[di % 2][:]),
                    sem="abo%d" % (di % 2), dma=True, waits=[ot])
                oW[di % 2] = st
                aow_toks = aow_toks[:1]
            c.barrier()
            c._ab_on = {}
            c._ab_sg = {}


    _emit_spill(c, xspill, store=False)
    c.barrier()


def emit_l0_attn_a(c, w, xspill, xin, after_ln=None):
    nc = c.nc
    H = 16
    with ExitStack() as sctx:
        sb = lambda n, shp, dt: sctx.enter_context(nc.sbuf_tensor(c.nid(n), shp, dt))
        wqsb = [sb("wq", [128, 1024], BF16) for _ in range(ND)]
        wkvsb = [sb("wkv", [128, 2048], BF16) for _ in range(ND)]
        wosb = [sb("wo", [128, 1024], BF16) for _ in range(2)]
        qsc = sb("qsc", [128, ND * SC], BF16)
        ksc = sb("ksc", [128, ND * SC], BF16)
        vsc = [sb("vsc", [128, H * 65], BF16) for _ in range(4)]
        kprev = sb("kpv", [128, 1024], BF16)
        vprev = sb("vpv", [128, H * 65], BF16)
        kz = sb("kz", [128, 1024], BF16)
        vz = sb("vz", [128, H * 65], BF16)
        atb = [sb("at", [128, 512], BF16) for _ in range(3)]
        cmask = sb("cm", [128, 128], BF16)
        cmi = sb("cmi", [128, 128], I32)
        aowsc = sb("aow", [128, ND * SC], BF16)
        stg = [sb("stg", [64, 128], BF16) for _ in range(2)]
        aonm = [sb("aonm", [65, 128], F32) for _ in range(2)]
        on65 = sb("on65", [65, 64], F32)
        ring = Ring(c, "aa", 3, queues=("gp",))

        for di in range(ND):
            ring.load(lambda e, di=di: e.dma_start(out=wqsb[di][:], in_=w["wq"][di * 128:(di + 1) * 128, :]))
            ring.load(lambda e, di=di: e.dma_start(out=wkvsb[di][:], in_=w["wkv"][di * 128:(di + 1) * 128, :]))
        emit_ln(c, sq_space=ksc[:, 0:2 * SC])
        cast_tok = c._cast_tok
        c._qgen = after_ln() if after_ln is not None else None

        c._qtag = "a"

        def qdrive(n=1, quiet=False):
            g = c._qgen
            if g is None:
                return
            try:
                for _ in range(n):
                    if c._qtag == "q" and not quiet:
                        return
                    c._qtag = next(g)
            except StopIteration:
                c._qgen = None

        x7 = c.xr[7][:]
        arb = [x7[:, 0:512], x7[:, 512:1024]]
        aob = [x7[:, 1024:1536], x7[:, 1536:2048]]
        rW = {0: cast_tok[7], 1: cast_tok[7]}
        oW = {0: None, 1: None}

        def initfn(ve):
            ve.memset(kz[:], 0.0)
            ve.memset(vz[:], 0.0)
            ve.memset(vz[:].rearrange("p (h e) -> p h e", e=65)[:, :, 64:65], 1.0)
            ve.memset(vprev[:].rearrange("p (h e) -> p h e", e=65)[:, :, 64:65], 1.0)
            for i in range(4):
                ve.memset(vsc[i][:].rearrange("p (h e) -> p h e", e=65)[:, :, 64:65], 1.0)
            return ve.memset(on65[:], 1.0)

        c.step("ve", initfn)
        tcmi = c.step("gp", lambda gp: gp.iota(cmi[:], [[1, 128]], base=0, channel_multiplier=-1))
        c.step("ve", lambda ve: ve.tensor_scalar(cmask[:], cmi[:], 0.0, None, ALU.is_ge),
               waits=[tcmi])
        c.barrier()
        qdrive(4, quiet=True)

        for scn in range(NSC):
            csl = slice(scn * SC, (scn + 1) * SC)
            for kind in range(2):
                dst = qsc if kind == 0 else ksc
                for oi in range(ND):
                    bidx = oi % 6

                    def mmfn(te, oi=oi, csl=csl, kind=kind, bidx=bidx):
                        for di in range(ND):
                            src_ = wqsb[di][:, oi * 128:(oi + 1) * 128] if kind == 0 \
                                else wkvsb[di][:, oi * 128:(oi + 1) * 128]
                            last = te.matmul(c.pb[bidx][:], src_, c.xl[di][:, csl],
                                             start=(di == 0), stop=(di == ND - 1))
                        return last

                    mt = c.step("te", mmfn, waits=c.bank(bidx))
                    et = c.step("se", lambda se, oi=oi, dst=dst, bidx=bidx: se.activation(
                        dst[:, oi * SC:(oi + 1) * SC], c.pb[bidx][:], AF.Copy), waits=[mt])
                    c.bank_done(bidx, et)
                    qdrive()
            for wj in range(4):
                for vo in range(2):
                    bidx = 6 + vo

                    def mmfn(te, wj=wj, vo=vo, scn=scn, bidx=bidx):
                        wsl = slice(scn * SC + wj * 128, scn * SC + (wj + 1) * 128)
                        for di in range(ND):
                            last = te.matmul(c.pb[bidx][:], c.xl[di][:, wsl],
                                             wkvsb[di][:, 1024 + vo * 512:1024 + (vo + 1) * 512],
                                             start=(di == 0), stop=(di == ND - 1))
                        return last

                    mt = c.step("te", mmfn, waits=c.bank(bidx))
                    dst = vsc[wj][:].rearrange("p (h e) -> p h e", e=65)[:, vo * 8:(vo + 1) * 8, 0:64]
                    et = c.step("ve", lambda ve, dst=dst, bidx=bidx: ve.tensor_copy(
                        dst, c.pb[bidx][:].rearrange("p (h e) -> p h e", e=64)), waits=[mt])
                    c.bank_done(bidx, et)
                    qdrive()
            c.barrier()

            sgW = {}
            for wj in range(4):
                qdrive(2)
                w_ = scn * 4 + wj
                kp = kz if w_ == 0 else (kprev if wj == 0 else None)
                vp = vz if w_ == 0 else (vprev if wj == 0 else None)
                LAG2 = 2
                ets = [None] * (H // 2)
                avpairs = [None] * (H // 2)
                for i in range(H // 2 + LAG2 + 1):
                    if i < H // 2:
                        hp = i
                        h0 = hp * 2
                        b0 = (hp * 2) % 6
                        b1 = (hp * 2 + 1) % 6
                        slot = hp % 3

                        def smm(te, wj=wj, h0=h0, b0=b0, b1=b1, kp=kp):
                            last = None
                            for j2 in range(2):
                                h = h0 + j2
                                hi, hb = h // 2, (h % 2) * 64
                                bx = b0 if j2 == 0 else b1
                                kpap = kp[hb:hb + 64, hi * 128:(hi + 1) * 128] if kp is not None \
                                    else ksc[hb:hb + 64, hi * SC + (wj - 1) * 128:hi * SC + wj * 128]
                                te.matmul(c.pb[bx][:, 0:128], kpap,
                                          qsc[hb:hb + 64, hi * SC + wj * 128:hi * SC + (wj + 1) * 128],
                                          start=True, stop=True)
                                last = te.matmul(c.pb[bx][:, 128:256],
                                                 ksc[hb:hb + 64, hi * SC + wj * 128:hi * SC + (wj + 1) * 128],
                                                 qsc[hb:hb + 64, hi * SC + wj * 128:hi * SC + (wj + 1) * 128],
                                                 start=True, stop=True)
                            return last

                        mt = c.step("te", smm, waits=c.bank(b0) + c.bank(b1))
                        eww = sgW.get(("at", slot))

                        def expfn(se, slot=slot, b0=b0, b1=b1):
                            se.activation(atb[slot][:, 0:256], c.pb[b0][:, 0:256], AF.Exp,
                                          scale=c.cc[:, CCOL[0.125]:CCOL[0.125] + 1])
                            return se.activation(atb[slot][:, 256:512], c.pb[b1][:, 0:256], AF.Exp,
                                                 scale=c.cc[:, CCOL[0.125]:CCOL[0.125] + 1])

                        et = c.step("se", expfn, waits=[mt] + ([eww] if eww else []))
                        c.bank_done(b0, et)
                        c.bank_done(b1, et)

                        def maskfn(ve, slot=slot):
                            ve.tensor_mul(atb[slot][:, 128:256], atb[slot][:, 128:256], cmask[:])
                            return ve.tensor_mul(atb[slot][:, 384:512], atb[slot][:, 384:512], cmask[:])

                        ets[hp] = c.step("ve", maskfn, waits=[et])
                    if LAG2 + 1 <= i:
                        hp = i - LAG2 - 1
                        if hp < H // 2:
                            avs = avpairs[hp]
                            h0 = hp * 2
                            trc_prev = sgW.get(("trc",))
                            for j2 in range(2):
                                h = h0 + j2
                                hi, hb = h // 2, (h % 2) * 64
                                po = 6 + j2
                                tev = c.step("ve", lambda ve, po=po, j2=j2: ve.tensor_copy(
                                    aonm[j2][:], c.pb[po][0:65, 0:128]),
                                    waits=[avs[j2]] + ([sgW.get(("nm", j2))] if sgW.get(("nm", j2)) else []))
                                c.bank_done(po, tev)
                                trep = c.step("te", lambda te, j2=j2: te.matmul(
                                    c.pb[5][0:64, j2 * 128:(j2 + 1) * 128], on65[64:65, :],
                                    aonm[j2][64:65, :], start=True, stop=True),
                                    waits=[tev] + (c.bank(5) if trc_prev is None else [trc_prev]))
                                trc = c.step("ve", lambda ve, j2=j2: ve.reciprocal(
                                    c.tscr[0:64, j2 * 128:(j2 + 1) * 128],
                                    c.pb[5][0:64, j2 * 128:(j2 + 1) * 128]), waits=[trep])
                                trc_prev = trc
                                sgW[("trc",)] = trc
                                c.bank_done(5, trc)
                                if hb == 0:
                                    tn = c.step("ve", lambda ve, hi=hi, wj=wj, j2=j2: ve.tensor_mul(
                                        aowsc[0:64, hi * SC + wj * 128:hi * SC + (wj + 1) * 128],
                                        aonm[j2][0:64, :], c.tscr[0:64, j2 * 128:(j2 + 1) * 128]),
                                        waits=[trc])
                                else:
                                    sgi = j2
                                    sgw = sgW.get(("sg", sgi))
                                    tn0 = c.step("ve", lambda ve, sgi=sgi, j2=j2: ve.tensor_mul(
                                        stg[sgi][:], aonm[j2][0:64, :],
                                        c.tscr[0:64, j2 * 128:(j2 + 1) * 128]),
                                        waits=[trc] + ([sgw] if sgw else []))
                                    tn = c.step("sy", lambda e, hi=hi, wj=wj, sgi=sgi: e.dma_start(
                                        out=aowsc[64:128, hi * SC + wj * 128:hi * SC + (wj + 1) * 128],
                                        in_=stg[sgi][:]), sem="aas%d" % sgi, dma=True, waits=[tn0])
                                    sgW[("sg", sgi)] = tn
                                sgW[("nm", j2)] = tn
                                sgW[("tn",)] = tn

                    if LAG2 <= i < H // 2 + LAG2:
                        hp = i - LAG2
                        h0 = hp * 2
                        slot = hp % 3
                        avs = []
                        for j2 in range(2):
                            h = h0 + j2
                            po = 6 + j2

                            def avfn(te, wj=wj, h=h, j2=j2, slot=slot, po=po, vp=vp):
                                vpap = vp[:, h * 65:(h + 1) * 65] if vp is not None \
                                    else vsc[wj - 1][:, h * 65:(h + 1) * 65]
                                te.matmul(c.pb[po][0:65, 0:128], vpap,
                                          atb[slot][:, j2 * 256:j2 * 256 + 128],
                                          start=True, stop=False)
                                return te.matmul(c.pb[po][0:65, 0:128], vsc[wj][:, h * 65:(h + 1) * 65],
                                                 atb[slot][:, j2 * 256 + 128:j2 * 256 + 256],
                                                 start=False, stop=True)

                            avs.append(c.step("te", avfn, waits=[ets[hp]] + c.bank(po)))
                        sgW[("at", slot)] = avs[-1]
                        avpairs[hp] = avs
            if scn < NSC - 1:
                tk = c.step("sy", lambda e, scn=scn: e.dma_start(
                    out=kprev[:], in_=ksc[:].rearrange("p (o w) -> p o w", w=SC)[:, :, 384:512]),
                    sem="aakp", dma=True, waits=[sgW[("tn",)]])
                tv2 = c.step("sy", lambda e: e.dma_start(out=vprev[:], in_=vsc[3][:]),
                             sem="aakp", dma=True)
            aow_toks = [t for kk, t in sgW.items() if kk[0] in ("tn", "sg") and t]
            mt = None
            for ii in range(ND):
                wt_ = ring.load(lambda e, ii=ii: e.dma_start(
                    out=wosb[ii % 2][:], in_=w["wo"][ii * 128:(ii + 1) * 128, :]),
                    waits=[mt] if ii >= 2 else [])

                def wofn(te, ii=ii):
                    for di in range(ND):
                        last = te.matmul(c.pb[di][:], wosb[ii % 2][:, di * 128:(di + 1) * 128],
                                         aowsc[:, ii * SC:(ii + 1) * SC],
                                         start=(ii == 0), stop=(ii == ND - 1))
                    return last

                bw = sum((c.bank(di) for di in range(ND)), []) if ii == 0 else []
                mt = c.step("te", wofn, waits=[wt_] + aow_toks + bw)
                aow_toks = []
                qdrive()
            for di in range(ND):
                j = di % 2
                rt = c.step("sy", lambda e, di=di, csl=csl, j=j: e.dma_start(
                    out=arb[j], in_=xin[di * 128:(di + 1) * 128, csl]),
                    sem="aar%d" % j, dma=True, waits=[t for t in (rW[j],) if t])
                ot = c.step("ve", lambda ve, di=di, j=j: ve.scalar_tensor_tensor(
                    aob[j], c.pb[di][:], w["boc"][:, di:di + 1], arb[j],
                    ALU.add, ALU.add), waits=[mt, rt] + ([oW[j]] if oW[j] else []))
                c.bank_done(di, ot)
                rW[j] = ot
                st = c.step("sy", lambda e, di=di, csl=csl, j=j: e.dma_start(
                    out=xspill[di * 128:(di + 1) * 128, csl], in_=aob[j]),
                    sem="aao%d" % j, dma=True, waits=[ot])
                oW[j] = st
            c.barrier()
            qdrive(4, quiet=True)
        while c._qgen is not None:
            qdrive(50, quiet=True)
    c.barrier()
    _emit_spill(c, xspill, store=False)
    c.barrier()


def emit_l1_attn_a(c, w, xtok):
    nc = c.nc
    H, POS, SS = 8, 16, 33
    with ExitStack() as sctx:
        sb = lambda n, shp, dt: sctx.enter_context(nc.sbuf_tensor(c.nid(n), shp, dt))
        wwv = sb("wwv", [128, ND * 24], BF16)
        wqr = sb("wqr", [128, ND * 128], BF16)
        wse1 = sb("ws1", [128, ND * 256], BF16)
        wse2 = sb("ws2", [128, 2 * 1024], BF16)
        wout = [sb("wo", [128, 1024], BF16) for _ in range(ND)]
        kwcol = sb("kwc", [128, 8], BF16)
        wp = sb("wp", [24, S], F32)
        tB = sb("tb", [24, S], F32)
        qT = sb("qt", [128, S], BF16)
        meta = sb("meta", [32, S], F32)
        mtok = [sb("mtok", [128, 32], F32) for _ in range(2)]
        qk8 = sb("qk8", [8, SC], F32)
        sel = sb("sel", [24, 4], F32)
        seli = sb("seli", [24, 4], I32)
        stok = [sb("stok", [128, 128], BF16) for _ in range(2)]
        stride_bc = sb("stb", [128, SS], F32)
        strabs = sb("sta", [128, SS], F32)
        colsc = sb("colsc", [128, 8], F32)
        coli = sb("coli", [128, 1], I32)
        sc_all = sb("sca", [128, H * SS], F32)
        env = sb("env", [128, SS], F32)
        mval = sb("mva", [128, SS], F32)
        mbc = sb("mbc", [128, H * SS], F32)
        mu8 = sb("mu8", [128, H * SS], U8)
        negt = sb("neg", [128, H * SS], F32)
        mx8 = sb("mx8", [128, H], F32)
        attnb = [sb("attn", [128, H * SS], F32) for _ in range(2)]
        spf = sb("spf", [128, SS], F32)
        spc = sb("spc", [128, SS], F32)
        idxb = [sb("idxi", [128, SS], I32) for _ in range(2)]
        G = [sb("G", [128, 1024], BF16) for _ in range(4)]
        gtmp = sb("gtmp", [128, 256], F32)
        acc = sb("acc", [128, 1024], F32)
        tmul = [sb("tmul", [128, 1024], BF16) for _ in range(2)]
        identb = sb("idb", [128, 128], BF16)
        otT = sb("otT", [128, 1024], BF16)
        zT = sb("zT", [128, 2 * 128], BF16)
        seT = sb("seT", [128, ND * 128], BF16)
        gT = sb("gT", [128, ND * 128], BF16)
        sef = sb("sef", [128, 256], F32)
        ring = Ring(c, "wa", 3, queues=("gp",))
        gring = Ring(c, "ga", 4, queues=("gp",))
        tring = Ring(c, "tr", 2, queues=("sy",))
        sring = Ring(c, "st", 2, queues=("se",))

        lt = []
        lt.append(ring.load(lambda e: e.dma_start(
            out=wwv[:].rearrange("p (d o) -> p d o", o=24),
            in_=w["w_wave"].rearrange("(d p) o -> p d o", p=128))))
        lt.append(ring.load(lambda e: e.dma_start(
            out=wqr[:].rearrange("p (d o) -> p d o", o=128),
            in_=w["w_query"].rearrange("(d p) o -> p d o", p=128))))
        lt.append(ring.load(lambda e: e.dma_start(
            out=wse1[:].rearrange("p (d o) -> p d o", o=256),
            in_=w["w_se1"].rearrange("(d p) o -> p d o", p=128))))
        lt.append(ring.load(lambda e: e.dma_start(
            out=wse2[:].rearrange("p (l o) -> p l o", o=1024),
            in_=w["w_se2"].rearrange("(l p) o -> p l o", p=128))))
        for di in range(ND):
            lt.append(ring.load(lambda e, di=di: e.dma_start(
                out=wout[di][:], in_=w["w_out"][di * 128:(di + 1) * 128, :])))
        emit_ln(c, sq_space=G[0][:])
        tkz = c.step("ve", lambda ve: ve.memset(kwcol[:], 0.0))
        tkw = []
        for h in range(H):
            tkw.append(c.step("gp", lambda e, h=h: e.dma_start(
                out=kwcol[h * 16:(h + 1) * 16, h:h + 1], in_=w["key_weight"][:, None],
                allow_slow_non_contiguous=True), sem="kwl", dma=True, waits=[tkz]))
        tng = c.step("ve", lambda ve: ve.memset(negt[:], -1e30))
        c.step("ve", lambda ve: ve.tensor_scalar(identb[:], c.identi[:], 0, None,
                                                 ALU.is_equal), waits=[c.tok_ident])

        tsel = []
        for jcol, off in ((0, 0), (1, 16), (2, 8)):
            ti = c.step("gp", lambda gp, jcol=jcol, off=off: gp.iota(
                seli[:, jcol:jcol + 1], [[0, 1]], base=-off, channel_multiplier=1))
            t1 = c.step("ve", lambda ve, jcol=jcol: ve.tensor_copy(
                sel[:, jcol:jcol + 1], seli[:, jcol:jcol + 1]), waits=[ti])
            t2 = c.step("ve", lambda ve, jcol=jcol: ve.tensor_scalar(
                sel[:, jcol:jcol + 1], sel[:, jcol:jcol + 1], 2.0, -7.0, ALU.mult, ALU.add),
                waits=[t1])
            t3 = c.step("ve", lambda ve, jcol=jcol: ve.tensor_mul(
                sel[:, 3:4], sel[:, jcol:jcol + 1], sel[:, jcol:jcol + 1]), waits=[t2])
            t4 = c.step("ve", lambda ve, jcol=jcol: ve.tensor_scalar(
                sel[:, jcol:jcol + 1], sel[:, 3:4], 49.5, None, ALU.is_le), waits=[t3])
            tsel.append(t4)

        ti = c.step("gp", lambda gp: gp.iota(idxb[0][:, 0:SS], [[1, SS]], base=-16,
                                             channel_multiplier=0))
        t1 = c.step("ve", lambda ve: ve.tensor_copy(stride_bc[:], idxb[0][:, 0:SS]), waits=[ti])
        t2 = c.step("ve", lambda ve: ve.tensor_scalar(strabs[:], stride_bc[:], -1.0, None, ALU.mult),
                    waits=[t1])
        tstr = c.step("ve", lambda ve: ve.tensor_max(strabs[:], strabs[:], stride_bc[:]), waits=[t2])
        c.barrier()

        ttok = []
        for tj in range(NT):
            for di in range(ND):
                k = tj * ND + di
                bw = [ttok[k - 2]] if k >= 2 else []
                t1 = tring.load(lambda e, tj=tj, di=di, k=k: e.dma_start(
                    out=stok[k % 2][:], in_=c.xl[di][:, tj * 128:(tj + 1) * 128],
                    transpose=True), waits=bw)
                t2 = sring.load(lambda e, tj=tj, di=di, k=k: e.dma_start(
                    out=xtok[tj * 128:(tj + 1) * 128, di * 128:(di + 1) * 128],
                    in_=stok[k % 2][:]), waits=[t1])
                ttok.append(t2)
        c.barrier()

        for sc in range(NSC):
            ssl = slice(sc * SC, (sc + 1) * SC)
            mt = c.step("te", lambda te, ssl=ssl: [te.matmul(
                c.pb[0][0:24, :], wwv[:, di * 24:(di + 1) * 24], c.xl[di][:, ssl],
                start=(di == 0), stop=(di == ND - 1)) for di in range(ND)][-1],
                waits=c.bank(0))
            et = c.step("se", lambda se, ssl=ssl: se.activation(
                wp[:, ssl], c.pb[0][0:24, :], AF.Silu, bias=w["bwvc"][:]), waits=[mt])
            c.bank_done(0, et)
            mt2 = c.step("te", lambda te, ssl=ssl: [te.matmul(
                c.pb[1][:], wqr[:, di * 128:(di + 1) * 128], c.xl[di][:, ssl],
                start=(di == 0), stop=(di == ND - 1)) for di in range(ND)][-1],
                waits=c.bank(1))
            et2 = c.step("se", lambda se, ssl=ssl: se.activation(
                qT[:, ssl], c.pb[1][:], AF.Silu, bias=w["bqc"][:]), waits=[mt2])
            c.bank_done(1, et2)
        tA = c.step("se", lambda se: se.activation(tB[:], wp[:], AF.Tanh))
        tS = c.step("se", lambda se: se.activation(wp[:], wp[:], AF.Sigmoid))

        for sc in range(NSC):
            ssl = slice(sc * SC, (sc + 1) * SC)
            mt = c.step("te", lambda te, ssl=ssl: (
                te.matmul(c.pb[2][0:1, :], sel[:, 0:1], wp[0:24, ssl], start=True, stop=True),
                te.matmul(c.pb[3][0:1, :], sel[:, 1:2], wp[0:24, ssl], start=True, stop=True),
                te.matmul(c.pb[4][0:1, :], sel[:, 2:3], tB[0:24, ssl], start=True, stop=True),
                te.matmul(c.pb[5][0:8, :], kwcol[:], qT[:, ssl], start=True, stop=True))[-1],
                waits=[tS, tA] + tsel + c.bank(2) + c.bank(3) + c.bank(4) + c.bank(5))
            tv = c.step("ve", lambda ve: (
                ve.tensor_scalar(c.srow[0][:], c.pb[2][0:1, :], 15.0 / 8, 1.0, ALU.mult, ALU.add),
                ve.tensor_scalar(c.srow[1][:], c.pb[3][0:1, :], 9.5 / 8, 0.5, ALU.mult, ALU.add),
                ve.tensor_scalar(c.mean_row[:], c.pb[4][0:1, :], 16.0 / 8, None, ALU.mult),
                ve.tensor_copy(qk8[:], c.pb[5][0:8, :]))[-1], waits=[mt])
            for b_ in (2, 3, 4, 5):
                c.bank_done(b_, tv)
            td0 = c.step("sy", lambda e, ssl=ssl: e.dma_start(out=meta[0:1, ssl], in_=c.srow[0][:]),
                         sem="mrow", dma=True, waits=[tv])
            td1 = c.step("sy", lambda e, ssl=ssl: e.dma_start(out=meta[2:3, ssl], in_=c.srow[1][:]),
                         sem="mrow", dma=True, waits=[tv])
            td2 = c.step("sy", lambda e, ssl=ssl: e.dma_start(out=meta[1:2, ssl], in_=c.mean_row[:]),
                         sem="mrow", dma=True, waits=[tv])
            td3 = c.step("sy", lambda e, ssl=ssl: e.dma_start(out=meta[3:11, ssl], in_=qk8[:]),
                         sem="mrow", dma=True, waits=[tv])
        c.barrier()

        mtokW = [None, None]
        gW = [None] * 4
        tmW = [None, None]
        idxW = [None, None]
        attnW = [None, None]
        X = {"te4": None, "tx3": None, "accR": None, "otTR": None, "zTR": None,
             "tgR": None, "gTR": None, "exprev": None, "tc1": None}

        def emitA(tj):
            ix = idxb[tj % 2]
            at = attnb[tj % 2]
            mw = [mtokW[tj % 2]] if mtokW[tj % 2] else []
            mt = c.step("te", lambda te, tj=tj: te.transpose(
                c.pb[0][:, 0:32], meta[:, tj * 128:(tj + 1) * 128], c.ident[0:32, 0:32]),
                waits=c.bank(0) + mw)
            tmt = c.step("ve", lambda ve, tj=tj: ve.tensor_copy(mtok[tj % 2][:], c.pb[0][:, 0:32]),
                         waits=[mt])
            c.bank_done(0, tmt)
            MT = mtok[tj % 2][:]
            pv = [X["te4"]] if X["te4"] else []
            tc0 = c.step("gp", lambda gp, tj=tj: gp.iota(coli[:], [[0, 1]], base=tj * 128,
                                                         channel_multiplier=1),
                         waits=[X["tc1"]] if X["tc1"] else [])
            tc1 = c.step("ve", lambda ve: ve.tensor_copy(colsc[:, 0:1], coli[:]),
                         waits=[tc0] + pv)
            X["tc1"] = tc1
            tc2 = c.step("ve", lambda ve, MT=MT: ve.tensor_add(colsc[:, 1:2], colsc[:, 0:1],
                                                               MT[:, 1:2]), waits=[tc1, tmt])
            ts1 = c.step("ve", lambda ve, MT=MT: ve.tensor_scalar(spf[:], stride_bc[:], MT[:, 0:1],
                                                                  None, ALU.mult),
                         waits=[tmt, tstr] + pv)
            ts2 = c.step("ve", lambda ve: ve.tensor_scalar(spf[:], spf[:], colsc[:, 1:2], None,
                                                           ALU.add), waits=[ts1, tc2])
            tm1 = c.step("ve", lambda ve: ve.tensor_scalar(mval[:], spf[:], 0.0, None, ALU.is_ge),
                         waits=[ts2])
            tm2 = c.step("ve", lambda ve: ve.tensor_scalar(spc[:], spf[:], 2048.0, None, ALU.is_lt),
                         waits=[ts2] + pv)
            tm3 = c.step("ve", lambda ve: ve.tensor_mul(mval[:], mval[:], spc[:]), waits=[tm1, tm2])
            tq1 = c.step("ve", lambda ve: ve.tensor_scalar(spc[:], spf[:], 0.0, 2047.0, ALU.max,
                                                           ALU.min), waits=[tm3])
            iw = idxW[tj % 2] or []
            tq2 = c.step("ve", lambda ve, ix=ix: ve.tensor_copy(ix[:], spc[:]), waits=[tq1] + iw)
            tq3 = c.step("ve", lambda ve, ix=ix: ve.tensor_copy(spf[:], ix[:]), waits=[tq2])
            tq4 = c.step("ve", lambda ve: ve.tensor_tensor(mbc[:, 0:SS], spf[:], spc[:], ALU.is_gt),
                         waits=[tq3])
            tq5 = c.step("ve", lambda ve: ve.tensor_sub(spf[:], spf[:], mbc[:, 0:SS]), waits=[tq4])
            tidx = c.step("ve", lambda ve, ix=ix: ve.tensor_copy(ix[:], spf[:]), waits=[tq5])
            tr1 = c.step("ve", lambda ve, MT=MT: ve.tensor_scalar(spc[:], strabs[:], MT[:, 0:1],
                                                                  None, ALU.mult),
                         waits=[tidx, tmt])
            tsc = None
            psc = [X["tx3"]] if X["tx3"] else []
            for h in range(H):
                tsc = c.step("ve", lambda ve, h=h, MT=MT: ve.tensor_scalar(
                    sc_all[:, h * SS:(h + 1) * SS], spc[:], MT[:, 3 + h:4 + h], 0.25,
                    ALU.mult, ALU.mult), waits=[tr1] + (psc if h == 0 else []))
            te1 = c.step("ve", lambda ve, MT=MT: ve.tensor_scalar(colsc[:, 3:4], MT[:, 2:3], 0.1,
                                                                  None, ALU.max), waits=[tmt])
            te2 = c.step("ve", lambda ve: ve.reciprocal(colsc[:, 4:5], colsc[:, 3:4]), waits=[te1])
            te3 = c.step("ve", lambda ve: ve.tensor_scalar(colsc[:, 5:6], colsc[:, 4:5], -1.0, None,
                                                           ALU.mult), waits=[te2])
            te4 = c.step("se", lambda se: se.activation(env[:], spc[:], AF.Exp,
                                                        scale=colsc[:, 5:6]), waits=[te3, tr1])
            mtokW[tj % 2] = te1
            X["te4"] = te4
            tb1 = c.step("ve", lambda ve: ve.tensor_copy(
                mbc[:], mval[:].unsqueeze(1).broadcast_to([128, H, SS])), waits=[tm3, tq4])
            tb2 = c.step("ve", lambda ve: ve.tensor_copy(mu8[:], mbc[:]), waits=[tb1])
            aw = attnW[tj % 2] or []
            tb3 = c.step("ve", lambda ve, at=at: ve.select(at[:], mu8[:], sc_all[:], negt[:]),
                         waits=[tb2, tsc, tng] + aw)
            tx1 = c.step("ve", lambda ve, at=at: ve.reduce_max(
                mx8[:].unsqueeze(2), at[:].rearrange("p (h s) -> p h s", s=SS), AX.X,
                negate=True), waits=[tb3])
            tx2 = c.step("ve", lambda ve, at=at: ve.tensor_add(
                sc_all[:].rearrange("p (h s) -> p h s", s=SS),
                at[:].rearrange("p (h s) -> p h s", s=SS),
                mx8[:].unsqueeze(2).broadcast_to([128, H, SS])), waits=[tx1])
            tx3 = c.step("se", lambda se, at=at: se.activation(at[:], sc_all[:], AF.Exp),
                         waits=[tx2])
            X["tx3"] = tx3
            tx4 = c.step("ve", lambda ve, at=at: ve.reduce_sum(
                mx8[:].unsqueeze(2), at[:].rearrange("p (h s) -> p h s", s=SS), AX.X),
                waits=[tx3])
            tx5 = c.step("ve", lambda ve: ve.reciprocal(mx8[:], mx8[:]), waits=[tx4])
            tx6 = c.step("ve", lambda ve, at=at: ve.tensor_mul(
                at[:].rearrange("p (h s) -> p h s", s=SS),
                at[:].rearrange("p (h s) -> p h s", s=SS),
                mx8[:].unsqueeze(2).broadcast_to([128, H, SS])), waits=[tx5])
            tx7 = c.step("ve", lambda ve, at=at: ve.tensor_mul(
                at[:].rearrange("p (h s) -> p h s", s=SS),
                at[:].rearrange("p (h s) -> p h s", s=SS),
                env[:].unsqueeze(1).broadcast_to([128, H, SS])), waits=[tx6, te4])
            tx8 = c.step("ve", lambda ve, at=at: ve.tensor_mul(at[:], at[:], mbc[:]), waits=[tx7])
            tx9 = c.step("ve", lambda ve, at=at: ve.reduce_sum(
                mx8[:].unsqueeze(2), at[:].rearrange("p (h s) -> p h s", s=SS), AX.X),
                waits=[tx8])
            tx10 = c.step("ve", lambda ve: ve.tensor_scalar(mx8[:], mx8[:], 1e-8, None, ALU.add),
                          waits=[tx9])
            tx11 = c.step("ve", lambda ve: ve.reciprocal(mx8[:], mx8[:]), waits=[tx10])
            tx12 = c.step("ve", lambda ve, at=at: ve.tensor_mul(
                at[:].rearrange("p (h s) -> p h s", s=SS),
                at[:].rearrange("p (h s) -> p h s", s=SS),
                mx8[:].unsqueeze(2).broadcast_to([128, H, SS])), waits=[tx11])
            return {"tidx": tidx, "tx12": tx12, "ix": ix, "at": at}

        def emitB(tj, a):
            tjs = slice(tj * 128, (tj + 1) * 128)
            ix, at = a["ix"], a["at"]
            tm = None
            gts = []
            ta1 = ta2 = None
            for s_ in range(SS):
                gw = list(gW[s_ % 4]) if gW[s_ % 4] else []
                gt = gring.load(lambda e, s_=s_, ix=ix: e.indirect_dma_start(
                    out=G[s_ % 4][:], out_offset=None, in_=xtok,
                    in_offset=IndirectOffsetOnAxis(ap=ix[:, s_:s_ + 1], axis=0)),
                    waits=gw + [a["tidx"]])
                gts.append(gt)

                def scfn_ve(ve, s_=s_, at=at):
                    last = None
                    for h in range(4):
                        last = ve.tensor_scalar(
                            tmul[s_ % 2][:, h * 128:(h + 1) * 128],
                            G[s_ % 4][:, h * 128:(h + 1) * 128],
                            at[:, h * SS + s_:h * SS + s_ + 1], None, ALU.mult)
                    return last

                def scfn_se(se, s_=s_, at=at):
                    last = None
                    for h in range(4, H):
                        last = se.activation(
                            tmul[s_ % 2][:, h * 128:(h + 1) * 128],
                            G[s_ % 4][:, h * 128:(h + 1) * 128], AF.Copy,
                            scale=at[:, h * SS + s_:h * SS + s_ + 1])
                    return last

                tw = [tmW[s_ % 2]] if tmW[s_ % 2] else []
                ta1 = c.step("ve", scfn_ve, waits=[gt, a["tx12"]] + tw)
                ta2 = c.step("se", scfn_se, waits=[gt, a["tx12"]] + tw)
                bw = (c.bank(4) + c.bank(5)) if s_ == 0 else []

                def accfn(te, s_=s_):
                    last = None
                    for b in range(2):
                        last = te.matmul(c.pb[4 + b][:], identb[:],
                                         tmul[s_ % 2][:, b * 512:(b + 1) * 512],
                                         start=(s_ == 0), stop=(s_ == SS - 1))
                    return last

                tm = c.step("te", accfn, waits=[ta1, ta2] + bw)
                tmW[s_ % 2] = tm
                gW[s_ % 4] = [ta1, ta2]
            idxW[tj % 2] = gts[-4:]
            attnW[tj % 2] = [ta1, ta2]

            def evfn(se):
                se.activation(acc[:, 0:512], c.pb[4][:], AF.Copy)
                return se.activation(acc[:, 512:1024], c.pb[5][:], AF.Copy)

            ar = [X["accR"]] if X["accR"] else []
            tmv = c.step("se", evfn, waits=[tm] + ar)
            c.bank_done(4, tmv)
            c.bank_done(5, tmv)
            otr = X["otTR"] or []
            mt = et = None
            for oi in range(ND):
                bidx = oi % 2
                mt = c.step("te", lambda te, oi=oi, bidx=bidx: te.transpose(
                    c.pb[bidx][:, 0:128], acc[:, oi * 128:(oi + 1) * 128], c.ident[:]),
                    waits=[tmv] + c.bank(bidx))
                et = c.step("se", lambda se, oi=oi, bidx=bidx: se.activation(
                    otT[:, oi * 128:(oi + 1) * 128], c.pb[bidx][:, 0:128], AF.Copy),
                    waits=[mt] + (otr if oi == 0 else []))
                c.bank_done(bidx, et)
            X["accR"] = mt
            zr = [X["zTR"]] if X["zTR"] else []
            mt1 = None
            for oi2 in range(2):
                bidx = 2 + oi2
                mt1 = c.step("te", lambda te, oi2=oi2, bidx=bidx: [te.matmul(
                    c.pb[bidx][:, 0:128], wse1[:, di * 256 + oi2 * 128:di * 256 + (oi2 + 1) * 128],
                    otT[:, di * 128:(di + 1) * 128], start=(di == 0), stop=(di == ND - 1))
                    for di in range(ND)][-1], waits=[et] + c.bank(bidx))
                ez = c.step("se", lambda se, oi2=oi2, bidx=bidx: se.activation(
                    zT[:, oi2 * 128:(oi2 + 1) * 128], c.pb[bidx][:, 0:128], AF.Silu,
                    bias=w["bs1c"][:, oi2:oi2 + 1]), waits=[mt1] + (zr if oi2 == 0 else []))
                c.bank_done(bidx, ez)
            tgr = [X["tgR"]] if X["tgR"] else []
            mt2 = None
            for oi in range(ND):
                bidx = 4 + oi % 2
                mt2 = c.step("te", lambda te, oi=oi, bidx=bidx: [te.matmul(
                    c.pb[bidx][:, 0:128], wse2[:, li * 1024 + oi * 128:li * 1024 + (oi + 1) * 128],
                    zT[:, li * 128:(li + 1) * 128], start=(li == 0), stop=(li == 1))
                    for li in range(2)][-1], waits=[ez] + c.bank(bidx))
                es = c.step("se", lambda se, oi=oi, bidx=bidx: se.activation(
                    seT[:, oi * 128:(oi + 1) * 128], c.pb[bidx][:, 0:128], AF.Sigmoid,
                    bias=w["bs2c"][:, oi:oi + 1]), waits=[mt2] + (tgr if oi == 0 else []))
                c.bank_done(bidx, es)
            X["zTR"] = mt2
            X["otTR"] = [mt1]
            gr = [X["gTR"]] if X["gTR"] else []
            tg = c.step("ve", lambda ve: ve.tensor_mul(gT[:], otT[:], seT[:]), waits=[es] + gr)
            X["otTR"].append(tg)
            X["tgR"] = tg
            exprev = X["exprev"]
            mtw = None
            for di in range(ND):
                bidx = 6 + di % 2
                mtw = c.step("te", lambda te, di=di, bidx=bidx: [te.matmul(
                    c.pb[bidx][:, 0:128], wout[ii][:, di * 128:(di + 1) * 128],
                    gT[:, ii * 128:(ii + 1) * 128], start=(ii == 0), stop=(ii == ND - 1))
                    for ii in range(ND)][-1], waits=[tg] + c.bank(bidx))
                ef = c.step("se", lambda se, bidx=bidx, di=di: se.activation(
                    sef[:, (di % 2) * 128:(di % 2 + 1) * 128], c.pb[bidx][:, 0:128], AF.Silu),
                    waits=[mtw] + ([exprev] if exprev else []))
                c.bank_done(bidx, ef)
                ex = c.step("ve", lambda ve, di=di, tjs=tjs: ve.tensor_add(
                    c.xr[di][:, tjs], c.xr[di][:, tjs], sef[:, (di % 2) * 128:(di % 2 + 1) * 128]),
                    waits=[ef])
                if di % 2 == 1:
                    exprev = ex
            X["exprev"] = exprev
            X["gTR"] = mtw

        Aq = emitA(0)
        for tj in range(NT):
            nxt = emitA(tj + 1) if tj + 1 < NT else None
            emitB(tj, Aq)
            Aq = nxt
    c.barrier()


def emit_l1_attn_b(c, w):
    nc = c.nc
    with ExitStack() as sctx:
        xlf = [sctx.enter_context(nc.sbuf_tensor(c.nid("xlf"), [128, S], F32)) for _ in range(ND)]
        emit_ln(c, out_tiles=xlf, out_dtype=F32)
        B = [sctx.enter_context(nc.sbuf_tensor(c.nid("sbuf"), [128, 1024], F32)) for _ in range(7)]
        carry = sctx.enter_context(nc.sbuf_tensor(c.nid("carry"), [128, 1], F32))
        msk = sctx.enter_context(nc.sbuf_tensor(c.nid("msk"), [128, 1024], U8))
        CM1 = c.cc[:, CCOL[-1.0]:CCOL[-1.0] + 1]
        C1 = c.cc[0:1, CCOL[1.0]:CCOL[1.0] + 1]

        for di in range(ND):
            prev_scan = None
            for hh in range(2):
                sl = slice(hh * 1024, (hh + 1) * 1024)
                xg = xlf[di]
                t1 = c.step("ve", lambda ve, xg=xg, sl=sl, di=di: (
                    ve.tensor_scalar(B[0][:], xg[:, sl], w["wfc"][:, di:di + 1], None, ALU.mult),
                    ve.tensor_scalar(B[1][:], xg[:, sl], w["wic"][:, di:di + 1], None, ALU.mult),
                    ve.tensor_scalar(B[2][:], xg[:, sl], w["whc"][:, di:di + 1], None, ALU.mult))[-1])
                sp = []
                for gi in (0, 1):
                    ta1 = c.step("se", lambda se, gi=gi: se.activation(B[3][:], B[gi][:], AF.Abs),
                                 waits=[t1])
                    ta2 = c.step("se", lambda se: se.activation(B[3][:], B[3][:], AF.Exp, scale=CM1))
                    ta3 = c.step("se", lambda se: se.activation(B[3][:], B[3][:], AF.Ln,
                                                                bias=c.cc[:, CCOL[1.0]:CCOL[1.0] + 1]))
                    tr = c.step("ve", lambda ve, gi=gi: ve.tensor_scalar(
                        B[4][:], B[gi][:], -1.0, 0.0, ALU.mult, ALU.max), waits=[t1])
                    tsp = c.step("gp", lambda gp, gi=gi: gp.tensor_add(B[5 + gi][:], B[4][:], B[3][:]),
                                 waits=[ta3, tr])
                    sp.append(tsp)
                tdiff = c.step("gp", lambda gp: gp.tensor_sub(B[0][:], B[5][:], B[6][:]), waits=sp)
                tf = c.step("se", lambda se: se.activation(B[5][:], B[0][:], AF.Sigmoid, scale=CM1),
                            waits=[tdiff])
                ti = c.step("se", lambda se: se.activation(B[6][:], B[0][:], AF.Sigmoid),
                            waits=[tdiff])
                tg1 = c.step("ve", lambda ve: ve.tensor_scalar(B[1][:], B[2][:], 0.5, None, ALU.add),
                             waits=[tdiff])
                tg2 = c.step("se", lambda se: se.activation(B[0][:], B[2][:], AF.Sigmoid), waits=[tg1])
                tm = c.step("ve", lambda ve: ve.tensor_scalar(msk[:], B[2][:], 0.0, None, ALU.is_ge),
                            waits=[tg1])
                tsel = c.step("ve", lambda ve: ve.select(B[4][:], msk[:], B[1][:], B[0][:]),
                              waits=[tg2, tm])
                tv = c.step("gp", lambda gp: gp.tensor_mul(B[4][:], B[6][:], B[4][:]),
                            waits=[tsel, ti])
                init = 0.0 if hh == 0 else carry[:]
                scw = [tv, tf] + ([prev_scan] if prev_scan else [])
                tscan = c.step("ve", lambda ve, init=init: ve.tensor_tensor_scan(
                    B[1][:], B[5][:], B[4][:], init, ALU.mult, ALU.add), waits=scw)
                tres = c.step("ve", lambda ve, di=di, sl=sl: ve.tensor_add(
                    c.xr[di][:, sl], c.xr[di][:, sl], B[1][:]), waits=[tscan])
                if hh == 0:
                    prev_scan = c.step("ve", lambda ve: ve.tensor_copy(
                        carry[:], B[1][:, 1023:1024]), waits=[tscan])
                else:
                    prev_scan = None
    c.barrier()


def emit_load_x(c, xin):
    qs = ("sy", "se", "gp")
    c._x_toks = [c.step(qs[di % 3], lambda e, di=di: e.dma_start(
        out=c.xr[di][:], in_=xin[di * 128:(di + 1) * 128, :]),
        sem=f"qx{di % 3}", dma=True) for di in range(ND)]


def emit_store_x(c, xout):
    c.barrier()
    c.sem("qx")

    def stfn(sy):
        for di in range(ND):
            sy.dma_start(out=xout[di * 128:(di + 1) * 128, :], in_=c.xr[di][:]).then_inc(c.sems["qx"], 16)
            c.cnt["qx"] += 16
        sy.wait_ge(c.sems["qx"], c.cnt["qx"])
        return None

    c.step("sy", stfn)


def emit_load_cols(c, pairs):
    c.sem("qx")

    def ldfn(sy):
        for t, d in pairs:
            sy.dma_start(out=t[:], in_=d).then_inc(c.sems["qx"], 16)
            c.cnt["qx"] += 16
        sy.wait_ge(c.sems["qx"], c.cnt["qx"])
        return None

    c.step("sy", ldfn)
    c.barrier()


def _prep(inputs):
    p = {}
    p["w1tf"] = np.ascontiguousarray(inputs["L0_ffn_W1"].T)
    p["w2tf"] = np.ascontiguousarray(inputs["L0_ffn_W2"].T)
    p["l0b1c"] = np.ascontiguousarray(inputs["L0_ffn_b1"].reshape(NF, 128).T)
    p["l0b2c"] = np.ascontiguousarray(inputs["L0_ffn_b2"].reshape(ND, 128).T)
    p["l1w1t"] = np.ascontiguousarray(inputs["L1_ffn_w1_w"].T)
    p["l1w2t"] = np.ascontiguousarray(inputs["L1_ffn_w2_w"].T)
    p["l1h1t"] = np.ascontiguousarray(inputs["L1_ffn_h1_w"].T)
    p["l1h2t"] = np.ascontiguousarray(inputs["L1_ffn_h2_w"].T)
    p["l1w3t"] = np.ascontiguousarray(inputs["L1_ffn_w3_w"].T)
    p["l1s1c"] = np.ascontiguousarray(inputs["L1_ffn_w1_s"].reshape(NF, 128).T)
    p["l1s2c"] = np.ascontiguousarray(inputs["L1_ffn_w2_s"].reshape(NF, 128).T)
    p["l1h1c"] = np.ascontiguousarray(inputs["L1_ffn_h1_s"].reshape(NF, 128).T)
    p["l1h2c"] = np.ascontiguousarray(inputs["L1_ffn_h2_s"].reshape(NF, 128).T)
    p["l1w3c"] = np.ascontiguousarray(inputs["L1_ffn_w3_s"].reshape(ND, 128).T)
    p["boc"] = np.ascontiguousarray(inputs["L0_attn_a_bo"].reshape(ND, 128).T)
    p["bwvc"] = np.ascontiguousarray(inputs["L1_attn_a_b_wave"].reshape(24, 1))
    p["bqc"] = np.ascontiguousarray(inputs["L1_attn_a_b_query"].reshape(128, 1))
    p["bs1c"] = np.ascontiguousarray(inputs["L1_attn_a_b_se1"].reshape(2, 128).T)
    p["bs2c"] = np.ascontiguousarray(inputs["L1_attn_a_b_se2"].reshape(ND, 128).T)
    p["kwr"] = np.ascontiguousarray(inputs["L1_attn_a_key_weight"].reshape(16))
    p["wic"] = np.ascontiguousarray(inputs["L1_attn_b_w_i"].reshape(ND, 128).T)
    p["wfc"] = np.ascontiguousarray(inputs["L1_attn_b_w_f"].reshape(ND, 128).T)
    p["whc"] = np.ascontiguousarray(inputs["L1_attn_b_w_h"].reshape(ND, 128).T)
    return p


def build_l0_ffn(first=True):
    c = C()
    nc = c.nc
    xin = nc.dram_tensor("xin", [DM, S], F32, kind="ExternalInput").ap()
    xout = nc.dram_tensor("xout", [DM, S], F32, kind="ExternalOutput").ap()
    w = {}
    if first:
        w["w1tf"] = nc.dram_tensor("w1tf", [DM, FF], F32, kind="ExternalInput").ap()
        w["w2tf"] = nc.dram_tensor("w2tf", [FF, DM], F32, kind="ExternalInput").ap()
        w["w1t"] = nc.dram_tensor("w1t", [DM, FF], BF16, kind="ExternalOutput").ap()
        w["w2t"] = nc.dram_tensor("w2t", [FF, DM], BF16, kind="ExternalOutput").ap()
    else:
        w["w1t"] = nc.dram_tensor("w1t", [DM, FF], BF16, kind="ExternalInput").ap()
        w["w2t"] = nc.dram_tensor("w2t", [FF, DM], BF16, kind="ExternalInput").ap()
    b1c_d = nc.dram_tensor("b1c", [128, NF], F32, kind="ExternalInput").ap()
    b2c_d = nc.dram_tensor("b2c", [128, ND], F32, kind="ExternalInput").ap()
    c.setup()
    w["b1c"] = c.sb("b1cs", [128, NF], F32)
    w["b2c"] = c.sb("b2cs", [128, ND], F32)
    emit_load_cols(c, [(w["b1c"], b1c_d), (w["b2c"], b2c_d)])
    emit_load_x(c, xin)
    if first:
        xsp = nc.dram_tensor("xqsp", [DM, S], F32, kind="Internal").ap()
        emit_ln(c)
        _emit_spill(c, xsp, store=True)
        c.barrier()
        for _ in emit_l0_ffn_quant(c, w):
            pass
        c.barrier()
        _emit_spill(c, xsp, store=False)
        c.barrier()
    emit_l0_ffn(c, w, False)
    emit_store_x(c, xout)
    c.flush()
    c.ctx.close()
    return nc


def build_l1_ffn():
    c = C()
    nc = c.nc
    xin = nc.dram_tensor("xin", [DM, S], F32, kind="ExternalInput").ap()
    xout = nc.dram_tensor("xout", [DM, S], F32, kind="ExternalOutput").ap()
    w = {}
    for n, shp in (("w1t", [DM, FF]), ("w2t", [DM, FF]), ("h1t", [FF, FF]),
                   ("h2t", [FF, FF]), ("w3t", [FF, DM])):
        w[n] = nc.dram_tensor(n, shp, F32, kind="ExternalInput").ap()
    cols_d = []
    for n, k in (("s1c", NF), ("s2c", NF), ("h1c", NF), ("h2c", NF), ("w3c", ND)):
        d = nc.dram_tensor(n, [128, k], F32, kind="ExternalInput").ap()
        cols_d.append((n, k, d))
    xspill = nc.dram_tensor("xspill", [DM, S], F32, kind="Internal").ap()
    c.setup()
    pairs = []
    for n, k, d in cols_d:
        w[n] = c.sb(n + "s", [128, k], F32)
        pairs.append((w[n], d))
    emit_load_cols(c, pairs)
    emit_load_x(c, xin)
    emit_l1_ffn(c, w, xspill)
    emit_store_x(c, xout)
    c.flush()
    c.ctx.close()
    return nc


def build_l0_attn_a():
    c = C()
    nc = c.nc
    xin = nc.dram_tensor("xin", [DM, S], F32, kind="ExternalInput").ap()
    xout = nc.dram_tensor("xout", [DM, S], F32, kind="ExternalOutput").ap()
    w = {}
    for n, shp in (("wq", [DM, DM]), ("wkv", [DM, 2 * DM]), ("wo", [DM, DM])):
        w[n] = nc.dram_tensor(n, shp, F32, kind="ExternalInput").ap()
    boc_d = nc.dram_tensor("boc", [128, ND], F32, kind="ExternalInput").ap()
    xspill = nc.dram_tensor("xspill", [DM, S], F32, kind="Internal").ap()
    c.setup()
    w["boc"] = c.sb("bocs", [128, ND], F32)
    emit_load_cols(c, [(w["boc"], boc_d)])
    emit_load_x(c, xin)
    emit_l0_attn_a(c, w, xspill, xin)
    emit_store_x(c, xout)
    c.flush()
    c.ctx.close()
    return nc


def build_l1_attn_a():
    c = C()
    nc = c.nc
    xin = nc.dram_tensor("xin", [DM, S], F32, kind="ExternalInput").ap()
    xout = nc.dram_tensor("xout", [DM, S], F32, kind="ExternalOutput").ap()
    w = {}
    for n, shp in (("w_wave", [DM, 24]), ("w_query", [DM, 128]), ("w_se1", [DM, 256]),
                   ("w_se2", [256, DM]), ("w_out", [DM, DM]), ("key_weight", [16]),
                   ("bwvc", [24, 1]), ("bqc", [128, 1]), ("bs1c", [128, 2]), ("bs2c", [128, ND])):
        w[n] = nc.dram_tensor(n, shp, F32, kind="ExternalInput").ap()
    xtok = nc.dram_tensor("xtok", [S, DM], BF16, kind="Internal").ap()
    c.setup()
    cols = []
    for n in ("bwvc", "bqc", "bs1c", "bs2c"):
        t = c.sb(n + "s", list(w[n].shape), F32)
        cols.append((t, w[n]))
        w[n] = t
    emit_load_cols(c, cols)
    emit_load_x(c, xin)
    emit_l1_attn_a(c, w, xtok)
    emit_store_x(c, xout)
    c.flush()
    c.ctx.close()
    return nc


def build_l0_attn_b():
    c = C()
    nc = c.nc
    xin = nc.dram_tensor("xin", [DM, S], F32, kind="ExternalInput").ap()
    xout = nc.dram_tensor("xout", [DM, S], F32, kind="ExternalOutput").ap()
    w = {}
    for n, shp in (("Wq", [DM, DM]), ("Wk", [DM, DM]), ("Wv", [DM, DM]), ("Wo", [DM, DM]),
                   ("Wdn", [DM, 256]), ("Wuk", [256, DM]), ("Wuv", [256, DM])):
        w[n] = nc.dram_tensor(n, shp, F32, kind="ExternalInput").ap()
    xspill = nc.dram_tensor("xspill", [DM, S], F32, kind="Internal").ap()
    c.setup()
    emit_load_x(c, xin)
    emit_l0_attn_b(c, w, xspill)
    emit_store_x(c, xout)
    c.flush()
    c.ctx.close()
    return nc


def build_l1_attn_b():
    c = C()
    nc = c.nc
    xin = nc.dram_tensor("xin", [DM, S], F32, kind="ExternalInput").ap()
    xout = nc.dram_tensor("xout", [DM, S], F32, kind="ExternalOutput").ap()
    cols_d = [(n, nc.dram_tensor(n, [128, ND], F32, kind="ExternalInput").ap())
              for n in ("wic", "wfc", "whc")]
    c.setup()
    w = {}
    pairs = []
    for n, d in cols_d:
        w[n] = c.sb(n + "s", [128, ND], F32)
        pairs.append((w[n], d))
    emit_load_cols(c, pairs)
    emit_load_x(c, xin)
    emit_l1_attn_b(c, w)
    emit_store_x(c, xout)
    c.flush()
    c.ctx.close()
    return nc


def build_fused():
    c = C()
    nc = c.nc
    xin = nc.dram_tensor("xin", [DM, S], F32, kind="ExternalInput").ap()
    xout = nc.dram_tensor("xout", [DM, S], F32, kind="ExternalOutput").ap()
    dt_ = lambda n, shp, dtp=F32, kind="ExternalInput": nc.dram_tensor(n, shp, dtp, kind=kind).ap()

    wa = {"wq": dt_("l0a_wq", [DM, DM]), "wkv": dt_("l0a_wkv", [DM, 2 * DM]),
          "wo": dt_("l0a_wo", [DM, DM])}
    wf0 = {"w1tf": dt_("l0f_w1tf", [DM, FF]), "w2tf": dt_("l0f_w2tf", [FF, DM]),
           "w1t": dt_("l0f_w1t", [DM, FF], BF16, "Internal"),
           "w2t": dt_("l0f_w2t", [FF, DM], BF16, "Internal")}
    wb = {n: dt_("l0b_" + n, shp) for n, shp in (("Wq", [DM, DM]), ("Wk", [DM, DM]),
          ("Wv", [DM, DM]), ("Wo", [DM, DM]), ("Wdn", [DM, 256]), ("Wuk", [256, DM]),
          ("Wuv", [256, DM]))}
    wa1 = {n: dt_("l1a_" + n, shp) for n, shp in (("w_wave", [DM, 24]), ("w_query", [DM, 128]),
           ("w_se1", [DM, 256]), ("w_se2", [256, DM]), ("w_out", [DM, DM]), ("key_weight", [16]))}
    wf1 = {n: dt_("l1f_" + n, shp) for n, shp in (("w1t", [DM, FF]), ("w2t", [DM, FF]),
           ("h1t", [FF, FF]), ("h2t", [FF, FF]), ("w3t", [FF, DM]))}
    coldefs = (("l0a_boc", [128, ND]), ("l0f_b1c", [128, NF]), ("l0f_b2c", [128, ND]),
               ("l1a_bwvc", [24, 1]), ("l1a_bqc", [128, 1]), ("l1a_bs1c", [128, 2]),
               ("l1a_bs2c", [128, ND]), ("l1f_s1c", [128, NF]), ("l1f_s2c", [128, NF]),
               ("l1f_h1c", [128, NF]), ("l1f_h2c", [128, NF]), ("l1f_w3c", [128, ND]),
               ("l1b_wic", [128, ND]), ("l1b_wfc", [128, ND]), ("l1b_whc", [128, ND]))
    cold = {n: dt_(n, shp) for n, shp in coldefs}
    xspill = dt_("xspill", [DM, S], F32, "Internal")
    xtok = dt_("xtok", [S, DM], BF16, "Internal")

    c.setup()
    pairs = []
    cols = {}
    for n, shp in coldefs:
        t = c.sb(n + "s", shp, F32)
        cols[n] = t
        pairs.append((t, cold[n]))
    emit_load_cols(c, pairs)
    wa["boc"] = cols["l0a_boc"]
    wf0["b1c"], wf0["b2c"] = cols["l0f_b1c"], cols["l0f_b2c"]
    wa1.update({"bwvc": cols["l1a_bwvc"], "bqc": cols["l1a_bqc"],
                "bs1c": cols["l1a_bs1c"], "bs2c": cols["l1a_bs2c"]})
    wf1.update({"s1c": cols["l1f_s1c"], "s2c": cols["l1f_s2c"], "h1c": cols["l1f_h1c"],
                "h2c": cols["l1f_h2c"], "w3c": cols["l1f_w3c"]})
    wb1 = {"wic": cols["l1b_wic"], "wfc": cols["l1b_wfc"], "whc": cols["l1b_whc"]}

    emit_load_x(c, xin)
    emit_l0_attn_a(c, wa, xspill, xin, after_ln=lambda: emit_l0_ffn_quant(c, wf0))
    emit_l0_ffn(c, wf0, first=False)
    emit_l0_attn_b(c, wb, xspill)
    emit_l0_ffn(c, wf0, first=False)
    emit_l1_attn_a(c, wa1, xtok)
    emit_l1_ffn(c, wf1, xspill)
    emit_l1_attn_b(c, wb1)
    emit_l1_ffn(c, wf1, xspill)
    emit_store_x(c, xout)
    c.flush()
    c.ctx.close()
    return nc


def kernel(**inputs):
    B = 8
    x = inputs["L0_attn_a_x"]
    xt = [np.ascontiguousarray(np.asarray(x[i]).T).astype(np.float32) for i in range(B)]
    p = _prep(inputs)
    nc = build_fused()
    common = {
        "l0a_wq": inputs["L0_attn_a_wq"], "l0a_wkv": inputs["L0_attn_a_wkv"],
        "l0a_wo": inputs["L0_attn_a_wo"], "l0a_boc": p["boc"],
        "l0f_w1tf": p["w1tf"], "l0f_w2tf": p["w2tf"],
        "l0f_b1c": p["l0b1c"], "l0f_b2c": p["l0b2c"],
        "l0b_Wq": inputs["L0_attn_b_Wq"], "l0b_Wk": inputs["L0_attn_b_Wk"],
        "l0b_Wv": inputs["L0_attn_b_Wv"], "l0b_Wo": inputs["L0_attn_b_Wo"],
        "l0b_Wdn": inputs["L0_attn_b_W_down_kv"], "l0b_Wuk": inputs["L0_attn_b_W_up_k"],
        "l0b_Wuv": inputs["L0_attn_b_W_up_v"],
        "l1a_w_wave": inputs["L1_attn_a_w_wave"], "l1a_w_query": inputs["L1_attn_a_w_query"],
        "l1a_w_se1": inputs["L1_attn_a_w_se1"], "l1a_w_se2": inputs["L1_attn_a_w_se2"],
        "l1a_w_out": inputs["L1_attn_a_w_out"], "l1a_key_weight": p["kwr"],
        "l1a_bwvc": p["bwvc"], "l1a_bqc": p["bqc"], "l1a_bs1c": p["bs1c"], "l1a_bs2c": p["bs2c"],
        "l1f_w1t": p["l1w1t"], "l1f_w2t": p["l1w2t"], "l1f_h1t": p["l1h1t"],
        "l1f_h2t": p["l1h2t"], "l1f_w3t": p["l1w3t"],
        "l1f_s1c": p["l1s1c"], "l1f_s2c": p["l1s2c"], "l1f_h1c": p["l1h1c"],
        "l1f_h2c": p["l1h2c"], "l1f_w3c": p["l1w3c"],
        "l1b_wic": p["wic"], "l1b_wfc": p["wfc"], "l1b_whc": p["whc"],
    }
    maps = [dict(common, xin=xt[i]) for i in range(B)]
    res = run_bass_kernel_spmd(nc, maps, core_ids=list(range(B)))
    return np.stack([np.ascontiguousarray(np.asarray(res.results[i]["xout"], np.float32).T)
                     for i in range(B)])
